# version 12
# baseline (speedup 1.0000x reference)
import math
from contextlib import ExitStack
import numpy as np
import concourse.bass as bass
import concourse.mybir as mybir
from concourse.bass_utils import run_bass_kernel_spmd

F32 = mybir.dt.float32
BF16 = mybir.dt.bfloat16
AF = mybir.ActivationFunctionType
ALU = mybir.AluOpType
AX = mybir.AxisListType

D = 4096
T = 1024
S = 2048
NB = 8
HP = 11264
NKV = 8
NQ = 32
HD = 128
KC_D = 32
KC_F = 8
MC_H = 88
KC_H = 22
ISQ = 1.0 / math.sqrt(HD)


def _alibi_slopes(n):
    start = 2 ** (-2 ** (-(math.log2(n) - 3)))
    return np.array([start * start ** i for i in range(n)], dtype=np.float32)


class Sched:
    ENGINES = ("gpsimd", "sync", "tensor", "vector", "scalar")

    def __init__(self):
        self.L = {e: [] for e in self.ENGINES}

    TAG = [""]

    def add(self, eng, meth, *a, w=(), inc=(), lp=False, **k):
        self.L[eng].append([meth, a, k, list(w), list(inc), lp,
                            f"{self.TAG[0]}#{len(self.L[eng])}"])

    def emit(self, nc):
        from contextlib import nullcontext
        with nc.Block() as blk:
            for eng in self.ENGINES:
                if not self.L[eng]:
                    continue

                def runner(e, L=self.L[eng]):
                    for (meth, a, k, w, inc, lp, tag) in L:
                        for (sem, v) in w:
                            e.wait_ge(sem, v)
                        ctx = (nc.allow_low_precision("bf16 matmul operand")
                               if lp else nullcontext())
                        with ctx:
                            inst = getattr(e, meth)(*a, **k)
                        try:
                            inst.annotate(tag)
                        except Exception:
                            pass
                        for (sem, n) in inc:
                            inst.then_inc(sem, n)
                getattr(blk, eng)(runner)


class S16:

    def __init__(self, nc, name):
        self.sem = nc.alloc_semaphore(name)
        self.v = 0

    def inc(self):
        self.v += 16
        return (self.sem, 16)

    def at(self):
        return (self.sem, self.v)


class Ctr:
    def __init__(self, sem):
        self.sem = sem
        self.v = 0

    def inc(self, n=1):
        self.v += n
        return (self.sem, n)

    def at(self):
        return (self.sem, self.v)


def build_nc(stop_stage=3):
    nc = bass.Bass("TRN2", target_bir_lowering=False,
                   detect_race_conditions=False)

    def din(name, shape, dt=F32):
        return nc.dram_tensor(name, shape, dt, kind="ExternalInput").ap()

    xT = din("xT", [D, T])
    wqr = din("wqr", [NQ, 128, KC_D, 128])
    wkr = din("wkr", [NKV, 128, KC_D, 128])
    wvr = din("wvr", [NKV, 128, KC_D, 128])
    wor = din("wor", [KC_D, 128, KC_D, 128])
    w1r = din("w1r", [MC_H, 128, KC_F, 128])
    w3r = din("w3r", [MC_H, 128, KC_F, 128])
    w2r = din("w2r", [KC_D, 128, KC_H, 128])
    b1T = din("b1T", [2, HP])
    b3T = din("b3T", [2, HP])
    b2T = din("b2T", [2, D])
    a2S = din("a2S", [128, 2 * MC_H])
    SaT = din("SaT", [D, 66])
    qwD = din("qw", [128, 1])
    kwD = din("kw", [128, 1])
    masksD = din("masks", [128, 16 * 128])
    qposD = din("qpos", [2, T])
    sel40D = din("sel40", [40, 40 * 128])
    idenD = din("iden", [128, 128])
    iotacD = din("iotac", [128, 16])

    outT = nc.dram_tensor("outT", [D, T], F32, kind="ExternalOutput").ap()
    r1d = nc.dram_tensor("r1d", [D, T], F32, kind="Internal").ap()
    r2d = nc.dram_tensor("r2d", [D, T], F32, kind="Internal").ap()
    qd = nc.dram_tensor("qd", [40 * 128, T], BF16, kind="Internal").ap()
    kv_in = nc.dram_tensor("kv_in", [16384, 128], BF16, kind="Internal").ap()
    kv_out = nc.dram_tensor("kv_out", [32768, 128], BF16, kind="Internal").ap()

    sT = Ctr(nc.alloc_semaphore("sT"))
    sV = Ctr(nc.alloc_semaphore("sV"))
    sA = Ctr(nc.alloc_semaphore("sA"))
    sW = Ctr(nc.alloc_semaphore("sW"))
    sR = Ctr(nc.alloc_semaphore("sR"))
    sO = Ctr(nc.alloc_semaphore("sO"))
    sCC = Ctr(nc.alloc_semaphore("sCC"))
    sB = Ctr(nc.alloc_semaphore("sB"))
    sSet = S16(nc, "sSet")
    LA = [S16(nc, f"LA{i}") for i in range(2)]
    QA = [S16(nc, f"QA{i}") for i in range(2)]
    W1 = [S16(nc, f"W1{i}") for i in range(2)]
    W3 = [S16(nc, f"W3{i}") for i in range(2)]
    KF = [S16(nc, f"KF{i}") for i in range(2)]
    SA_ = [S16(nc, f"SA{i}") for i in range(2)]
    QS = [S16(nc, f"QS{i}") for i in range(2)]
    KV = [S16(nc, f"KV{i}") for i in range(2)]
    KVV = S16(nc, "KVV")
    ALL_STORE = SA_ + QS + KV + [KVV]

    AL = nc.alloc_sbuf_tensor
    ones2 = AL("ones2", [2, 128], BF16, side="left")
    ones2n = AL("ones2n", [2, 128], BF16, side="left")
    onesc = AL("onesc", [128, 2], BF16, side="left")
    iden_bf = AL("iden_bf", [128, 128], BF16, side="left")
    qw = AL("qw_s", [128, 1], F32, side="left")
    kw = AL("kw_s", [128, 1], F32, side="left")
    iotac = AL("iotac_s", [128, 16], F32, side="left")
    u_bf = AL("u_bf", [66, 1024], BF16, side="left")
    u2_bf = AL("u2_bf", [2, 1024], BF16, side="left")
    bcA = AL("bcA", [128, 1024], F32, side="left")
    bcC = AL("bcC", [128, 1024], F32, side="left")
    mrow = AL("mrow", [2, 1024], F32, side="left")
    mrow_bf = AL("mrow_bf", [2, 1024], BF16, side="left")
    vrow = AL("vrow", [2, 1024], F32, side="left")
    rsr = AL("rsr", [2, 1024], F32, side="left")
    rsr_bf = AL("rsr_bf", [2, 1024], BF16, side="left")
    crow = AL("crow", [2, 1024], F32, side="left")
    crow_bf = AL("crow_bf", [2, 1024], BF16, side="left")
    negsA = AL("negsA", [66, 1], F32, side="left")
    xfb = AL("xfb", [128, KC_F * 1024], BF16, side="left")
    satb = AL("satb", [128, KC_D * 66], BF16, side="left")
    W40 = AL("W40", [128, 79], BF16, side="left")
    epsc = AL("epsc", [128, 2], F32, side="left")
    qposs = AL("qposs", [2, 1024], F32, side="left")

    PS = [nc.alloc_psum_tensor(f"ps{i}", [128, 512], F32) for i in range(8)]

    BARP = [[]]
    OWN = {"tensor": sT, "vector": sV, "scalar": sA}

    def seal(sch):
        pairs = [(sCC.sem, sCC.v), (sSet.sem, sSet.v)]
        pairs += [s.at() for s in ALL_STORE if s.v > 0]
        for eng in ("tensor", "vector", "scalar"):
            if sch.L[eng]:
                last = sch.L[eng][-1]
                if not last[4]:
                    last[4].append(OWN[eng].inc())
                pairs.append(OWN[eng].at())
        BARP[0] = [p for p in pairs if p[1] > 0]

    def phase_wait(sch):
        for eng in Sched.ENGINES:
            for (sem, v) in BARP[0]:
                sch.add(eng, "wait_ge", sem, v)

    def ln_post(sch, sa_ps, sq_ps, eps, cmul, pb0, pb1, stats_done,
                first_w=(), fold=None, tfa=None):
        n = float(D)
        for i in range(2):
            sch.add("scalar", "mul", mrow[:, i * 512:(i + 1) * 512],
                    sa_ps[i][0:2, :], 1.0 / n,
                    w=([stats_done] if i == 0 else []),
                    inc=([sA.inc()] if i == 1 else []))
        m_done = sA.at()
        sch.add("vector", "tensor_copy", mrow_bf[:, :], mrow[:, :], w=[m_done])
        sch.add("vector", "tensor_mul", vrow[:, :], mrow[:, :], mrow[:, :])
        for i in range(2):
            sch.add("vector", "scalar_tensor_tensor",
                    rsr[:, i * 512:(i + 1) * 512], sq_ps[i][0:2, :], 1.0 / n,
                    vrow[:, i * 512:(i + 1) * 512], ALU.mult, ALU.subtract,
                    inc=([sV.inc()] if i == 1 else []))
        v_done = sV.at()
        sch.add("scalar", "activation", rsr[:, :], rsr[:, :], AF.Ln,
                bias=epsc[0:2, 0:1], w=[v_done])
        sch.add("scalar", "activation", rsr[:, :], rsr[:, :], AF.Exp,
                scale=-0.5, inc=[sA.inc()])
        r_done = sA.at()
        sch.add("vector", "tensor_copy", rsr_bf[:, :], rsr[:, :], w=[r_done])
        sch.add("vector", "tensor_mul", crow[:, :], mrow[:, :], rsr[:, :])
        sch.add("vector", "tensor_scalar_mul", crow[:, :], crow[:, :],
                float(cmul))
        sch.add("vector", "tensor_copy", crow_bf[:, :], crow[:, :],
                inc=[sV.inc()])
        cb_done = sV.at()
        for i in range(2):
            wl0 = ([cb_done] + list(first_w)) if i == 0 else [sA.at()]
            sch.add("tensor", "matmul", pb0[:, :], ones2[:, :],
                    rsr_bf[:, i * 512:(i + 1) * 512], start=True, stop=True,
                    w=wl0)
            sch.add("tensor", "matmul", pb1[:, :], ones2[:, :],
                    crow_bf[:, i * 512:(i + 1) * 512], start=True, stop=True,
                    inc=[sT.inc()])
            t_done = sT.at()
            sch.add("scalar", "copy", bcA[:, i * 512:(i + 1) * 512],
                    pb0[:, :], w=[t_done])
            sch.add("scalar", "copy", bcC[:, i * 512:(i + 1) * 512],
                    pb1[:, :], inc=[sA.inc()])
        ab_done = sA.at()
        if fold is not None:
            for c in range(KC_F):
                sch.add("vector", "tensor_mul", tfa[:, :],
                        fold[:, c * 1024:(c + 1) * 1024], bcA[:, :],
                        w=([ab_done] if c == 0 else []))
                sch.add("vector", "tensor_add",
                        xfb[:, c * 1024:(c + 1) * 1024], tfa[:, :], bcC[:, :],
                        inc=([sV.inc()] if c == KC_F - 1 else []))
        return ab_done

    def ln_post_lora(sch, sa_ps, pb0, pb1, scale, bc_done, asb, tmp66):
        for i in range(2):
            sch.add("scalar", "copy", asb[:, i * 512:(i + 1) * 512],
                    sa_ps[i][:, :], inc=([sA.inc()] if i == 1 else []))
        a_done = sA.at()
        mb = []
        for i in range(2):
            pb = pb0 if i == 0 else pb1
            sch.add("tensor", "matmul", pb[0:66, :], ones2[:, 0:66],
                    mrow_bf[:, i * 512:(i + 1) * 512], start=True, stop=True,
                    w=([bc_done] if i == 0 else []), inc=[sT.inc()])
            mb.append(sT.at())
        for i in range(2):
            pb = pb0 if i == 0 else pb1
            sch.add("vector", "scalar_tensor_tensor",
                    tmp66[:, i * 512:(i + 1) * 512], pb[0:66, :], negsA[:, :],
                    asb[:, i * 512:(i + 1) * 512], ALU.mult, ALU.add,
                    w=[a_done, mb[i]], inc=[sV.inc()])
        tdone = sV.at()
        rb = []
        for i in range(2):
            pb = pb0 if i == 0 else pb1
            sch.add("tensor", "matmul", pb[0:66, :], ones2[:, 0:66],
                    rsr_bf[:, i * 512:(i + 1) * 512], start=True, stop=True,
                    w=([tdone] if i == 0 else []), inc=[sT.inc()])
            rb.append(sT.at())
        for i in range(2):
            pb = pb0 if i == 0 else pb1
            sch.add("vector", "scalar_tensor_tensor",
                    u_bf[:, i * 512:(i + 1) * 512],
                    tmp66[:, i * 512:(i + 1) * 512], float(scale),
                    pb[0:66, :], ALU.mult, ALU.mult, w=[rb[i]],
                    inc=([sV.inc()] if i == 1 else []))

    with ExitStack() as es:
        stg = es.enter_context(nc.sbuf_tensor("stg_set", [128, 2240], F32,
                                              side="right"))
        sch = Sched()
        Sched.TAG[0] = "P1"
        sch.add("vector", "memset", ones2[:, :], 0.0)
        sch.add("vector", "memset", ones2[0:1, :], 1.0)
        sch.add("vector", "memset", ones2n[:, :], 0.0)
        sch.add("vector", "memset", ones2n[0:1, :], -1.0)
        sch.add("vector", "memset", onesc[:, :], 1.0)
        sch.add("vector", "memset", W40[:, :], 0.0)
        sch.add("vector", "memset", W40[:, 39:40], 1.0)
        sch.add("vector", "memset", epsc[:, 0:1], 1e-5)
        sch.add("vector", "memset", epsc[:, 1:2], 1e-6)
        sch.add("sync", "dma_start", out=stg[:, 2112:2240], in_=idenD,
                inc=[sSet.inc()])
        sch.add("sync", "dma_start", out=qw[:, :], in_=qwD, inc=[sSet.inc()])
        sch.add("sync", "dma_start", out=kw[:, :], in_=kwD, inc=[sSet.inc()])
        sch.add("sync", "dma_start", out=iotac[:, :], in_=iotacD,
                inc=[sSet.inc()])
        sch.add("sync", "dma_start", out=qposs[:, :], in_=qposD,
                inc=[sSet.inc()])
        sch.add("sync", "dma_start", out=stg[:, 0:2112],
                in_=SaT.rearrange("(c p) j -> p c j", p=128),
                inc=[sSet.inc()])
        ld = sSet.at()
        sch.add("vector", "tensor_copy", iden_bf[:, :], stg[:, 2112:2240],
                w=[ld])
        sch.add("vector", "tensor_scalar_mul", qw[:, :], qw[:, :], ISQ)
        sch.add("vector", "tensor_copy", satb[:, :], stg[:, 0:2112])
        seal(sch)
        sch.emit(nc)

    with ExitStack() as es:
        sb = lambda n, s, d: es.enter_context(
            nc.sbuf_tensor(n, s, d, side="right"))
        fold = sb("fold0", [128, KC_F * 1024], F32)
        asb0 = sb("asb0", [66, 1024], F32)
        tmp0 = sb("tmp0", [66, 1024], F32)
        tfold = sb("tfold0", [128, 1024], F32)
        rstg = [sb(f"rstg0{i}", [128, 1024], F32) for i in range(2)]
        xbf = [sb(f"xbf0{i}", [128, 1024], BF16) for i in range(2)]
        sqb = [sb(f"sqb0{i}", [128, 1024], BF16) for i in range(2)]
        sch = Sched()
        Sched.TAG[0] = "P2"
        phase_wait(sch)
        for kc in range(KC_D):
            sch.add("tensor", "matmul", PS[6][0:66, 0:2],
                    satb[:, kc * 66:(kc + 1) * 66], onesc[:, :],
                    start=(kc == 0), stop=(kc == KC_D - 1),
                    inc=([sT.inc()] if kc == KC_D - 1 else []))
        sch.add("scalar", "mul", negsA[:, :], PS[6][0:66, 0:1], -1.0,
                w=[sT.at()], inc=[sA.inc()])
        sa_ps = [PS[0][0:66, :], PS[1][0:66, :]]
        sq_ps = [PS[2][0:2, :], PS[3][0:2, :]]
        cvd, std, sqd, sqa, fdd = {}, {}, {}, {}, {}
        for c in range(KC_D):
            p = c % 2
            wl = [] if c < 2 else [(sV.sem, fdd[c - 2]), (sV.sem, cvd[c - 2])]
            sch.add("sync", "dma_start", out=rstg[p][:, :],
                    in_=xT[c * 128:(c + 1) * 128, :], w=wl,
                    inc=[LA[p].inc()])
            ld = LA[p].at()
            wl = [ld]
            if c >= 2:
                wl += [(sT.sem, std[c - 2]), (sA.sem, sqa[c - 2])]
            sch.add("vector", "tensor_copy", xbf[p][:, :], rstg[p][:, :],
                    w=wl, inc=[sV.inc()])
            cvd[c] = sV.v
            wl = [(sV.sem, cvd[c])]
            if c >= 2:
                wl.append((sT.sem, sqd[c - 2]))
            sch.add("scalar", "activation", sqb[p][:, :], xbf[p][:, :],
                    AF.Square, w=wl, inc=[sA.inc()])
            sqa[c] = sA.v
            for i in range(2):
                sch.add("tensor", "matmul", sa_ps[i],
                        satb[:, c * 66:(c + 1) * 66],
                        xbf[p][:, i * 512:(i + 1) * 512],
                        start=(c == 0), stop=(c == KC_D - 1),
                        w=([(sV.sem, cvd[c])] if i == 0 else []),
                        inc=([sT.inc()] if i == 1 else []))
            std[c] = sT.v
            for i in range(2):
                sch.add("tensor", "matmul", sq_ps[i], onesc[:, :],
                        sqb[p][:, i * 512:(i + 1) * 512],
                        start=(c == 0), stop=(c == KC_D - 1),
                        w=([(sA.sem, sqa[c])] if i == 0 else []),
                        inc=([sT.inc()] if i == 1 else []))
            sqd[c] = sT.v
            fc = c % KC_F
            if c < KC_F:
                sch.add("vector", "tensor_copy",
                        fold[:, fc * 1024:(fc + 1) * 1024], rstg[p][:, :],
                        w=[ld], inc=[sV.inc()])
            else:
                sch.add("vector", "tensor_add",
                        fold[:, fc * 1024:(fc + 1) * 1024],
                        fold[:, fc * 1024:(fc + 1) * 1024], rstg[p][:, :],
                        w=[ld], inc=[sV.inc()])
            fdd[c] = sV.v
        bc = ln_post(sch, sa_ps, sq_ps, eps=1e-5, cmul=-4.0,
                     pb0=PS[4], pb1=PS[5], stats_done=(sT.sem, sT.v),
                     fold=fold, tfa=tfold)
        ln_post_lora(sch, sa_ps, PS[4], PS[5], scale=0.5, bc_done=bc,
                     asb=asb0, tmp66=tmp0)
        seal(sch)
        sch.emit(nc)

    def ffn_phase(r0_src, rdst, is_final, tag, dump_after_p2=False):
        with ExitStack() as es:
            sb = lambda n, s, d: es.enter_context(
                nc.sbuf_tensor(n + tag, s, d, side="right"))
            gf = sb("gf", [128, KC_H * 1024], BF16)
            bv13 = sb("bv13", [66, HP], BF16)
            b2b = sb("b2b", [2, D], BF16)
            a2b = sb("a2b", [128, 2 * MC_H], BF16)
            with ExitStack() as es2:
                bstg = es2.enter_context(nc.sbuf_tensor(
                    "bstg" + tag, [128, 2 * MC_H], F32, side="right"))
                bstg2 = es2.enter_context(nc.sbuf_tensor(
                    "bstg2" + tag, [66, HP], F32, side="right"))
                sch = Sched()
                Sched.TAG[0] = "P3"
                phase_wait(sch)
                sch.add("sync", "dma_start", out=bstg2[32:34, :], in_=b1T,
                        inc=[sSet.inc()])
                sch.add("sync", "dma_start", out=bstg2[64:66, :], in_=b3T,
                        inc=[sSet.inc()])
                sch.add("sync", "dma_start", out=bstg2[0:2, 0:D], in_=b2T,
                        inc=[sSet.inc()])
                sch.add("sync", "dma_start", out=bstg[:, :], in_=a2S,
                        inc=[sSet.inc()])
                bl = sSet.at()
                sch.add("vector", "memset", bv13[:, :], 0.0, w=[bl])
                sch.add("vector", "tensor_copy", bv13[32:34, :],
                        bstg2[32:34, :])
                sch.add("vector", "tensor_copy", bv13[64:66, :],
                        bstg2[64:66, :])
                sch.add("vector", "tensor_copy", b2b[:, :], bstg2[0:2, 0:D])
                sch.add("vector", "tensor_copy", a2b[:, :], bstg[:, :],
                        inc=[sV.inc()])
                seal(sch)
                sch.emit(nc)

            esp2 = ExitStack()
            sbp = lambda n, s, d: esp2.enter_context(
                nc.sbuf_tensor(n + tag, s, d, side="right"))
            w1f = [sbp(f"w1f{i}", [128, 1024], F32) for i in range(2)]
            w3f = [sbp(f"w3f{i}", [128, 1024], F32) for i in range(2)]
            w1b = [sbp(f"w1b{i}", [128, 1024], BF16) for i in range(2)]
            w3b = [sbp(f"w3b{i}", [128, 1024], BF16) for i in range(2)]
            s1b = sbp("s1b", [128, 1024], F32)
            gbf = [sbp(f"gbf{i}", [128, 1024], BF16) for i in range(2)]
            tf2 = sbp("tf2", [128, 1024], F32)

            sch = Sched()
            Sched.TAG[0] = "P4"
            phase_wait(sch)
            y1p = [[PS[0], PS[1]], [PS[2], PS[3]]]
            y3p = [PS[4], PS[5]]
            psU = [PS[6][0:2, :], PS[7][0:2, :]]
            wcv, sld, y1d, y3d, gpd, a2d, pmc = ({}, {}, {}, {}, {}, {}, {})
            for c in range(KC_H):
                for qt in range(4):
                    mc = qt * KC_H + c
                    it = c * 4 + qt
                    p = it % 2
                    wl = [] if it < 2 else [(sA.sem, wcv[it - 2])]
                    sch.add("gpsimd", "dma_start", out=w1f[p][:, :],
                            in_=w1r[mc], w=wl, inc=[W1[p].inc()])
                    sch.add("gpsimd", "dma_start", out=w3f[p][:, :],
                            in_=w3r[mc], inc=[W3[p].inc()])
                    wl = [W1[p].at(), W3[p].at()]
                    if it >= 2:
                        wl.append((sT.sem, y3d[it - 2]))
                    sch.add("scalar", "copy", w1b[p][:, :], w1f[p][:, :],
                            w=wl)
                    sch.add("scalar", "copy", w3b[p][:, :], w3f[p][:, :],
                            inc=[sA.inc()])
                    wcv[it] = sA.v
                    pa = y1p[p]
                    wl = [(sA.sem, wcv[it])]
                    if it >= 2:
                        wl.append((sA.sem, sld[it - 2]))
                    for h in range(2):
                        for kc in range(KC_F):
                            sch.add("tensor", "matmul", pa[h][:, :],
                                    w1b[p][:, kc * 128:(kc + 1) * 128],
                                    xfb[:, kc * 1024 + h * 512:
                                        kc * 1024 + (h + 1) * 512],
                                    start=(kc == 0), stop=False,
                                    w=(wl if (h == 0 and kc == 0) else []))
                        sch.add("tensor", "matmul", pa[h][:, :],
                                bv13[32:34, mc * 128:(mc + 1) * 128],
                                u_bf[32:34, h * 512:(h + 1) * 512],
                                start=False, stop=True,
                                inc=([sT.inc()] if h == 1 else []))
                    y1d[it] = sT.v
                    wl = [] if it < 1 else [(sV.sem, gpd[it - 1])]
                    for h in range(2):
                        for kc in range(KC_F):
                            sch.add("tensor", "matmul", y3p[h][:, :],
                                    w3b[p][:, kc * 128:(kc + 1) * 128],
                                    xfb[:, kc * 1024 + h * 512:
                                        kc * 1024 + (h + 1) * 512],
                                    start=(kc == 0), stop=False,
                                    w=(wl if (h == 0 and kc == 0) else []))
                        sch.add("tensor", "matmul", y3p[h][:, :],
                                bv13[64:66, mc * 128:(mc + 1) * 128],
                                u_bf[64:66, h * 512:(h + 1) * 512],
                                start=False, stop=True,
                                inc=([sT.inc()] if h == 1 else []))
                    y3d[it] = sT.v
                    for h in range(2):
                        wl5 = []
                        if h == 0:
                            wl5 = [(sT.sem, y1d[it])]
                            if it >= 1:
                                wl5.append((sV.sem, gpd[it - 1]))
                        sch.add("scalar", "activation",
                                s1b[:, h * 512:(h + 1) * 512], pa[h][:, :],
                                AF.Silu, w=wl5,
                                inc=([sA.inc()] if h == 1 else []))
                    sld[it] = sA.v
                    gp = it % 2
                    wl6 = [(sT.sem, y3d[it]), (sA.sem, sld[it])]
                    if it >= 2:
                        wl6.append((sT.sem, a2d[it - 2]))
                    for h in range(2):
                        sch.add("vector", "tensor_mul",
                                gbf[gp][:, h * 512:(h + 1) * 512],
                                s1b[:, h * 512:(h + 1) * 512], y3p[h][:, :],
                                w=(wl6 if h == 0 else []),
                                inc=([sV.inc()] if h == 1 else []))
                    gpd[it] = sV.v
                    if it >= 1:
                        for h in range(2):
                            sch.add("tensor", "matmul", psU[h],
                                    a2b[:, 2 * pmc[it - 1]:
                                        2 * pmc[it - 1] + 2],
                                    gbf[(it - 1) % 2]
                                    [:, h * 512:(h + 1) * 512],
                                    start=(it == 1), stop=False,
                                    w=([(sV.sem, gpd[it - 1])]
                                       if h == 0 else []),
                                    inc=([sT.inc()] if h == 1 else []))
                        a2d[it - 1] = sT.v
                    pmc[it] = mc
                    if qt == 0:
                        sch.add("vector", "tensor_copy", tf2[:, :],
                                gbf[gp][:, :])
                    elif qt < 3:
                        sch.add("vector", "tensor_add", tf2[:, :], tf2[:, :],
                                gbf[gp][:, :])
                    else:
                        sch.add("vector", "tensor_add",
                                gf[:, c * 1024:(c + 1) * 1024], tf2[:, :],
                                gbf[gp][:, :], inc=[sV.inc()])
            last_it = KC_H * 4 - 1
            for h in range(2):
                sch.add("tensor", "matmul", psU[h],
                        a2b[:, 2 * pmc[last_it]:2 * pmc[last_it] + 2],
                        gbf[last_it % 2][:, h * 512:(h + 1) * 512],
                        start=False, stop=True,
                        w=([(sV.sem, gpd[last_it])] if h == 0 else []),
                        inc=([sT.inc()] if h == 1 else []))
            sch.add("scalar", "mul", u2_bf[:, 0:512], psU[0], 0.5,
                    w=[(sT.sem, sT.v)])
            sch.add("scalar", "mul", u2_bf[:, 512:1024], psU[1], 0.5,
                    inc=[sA.inc()])
            seal(sch)
            sch.emit(nc)
            esp2.close()

            if dump_after_p2:
                sch = Sched()
                Sched.TAG[0] = "DBGG"
                phase_wait(sch)
                rfd = tf2
                row = 0
                for c in range(KC_F):
                    sch.add("vector", "tensor_copy", rfd[:, :],
                            xfb[:, c * 1024:(c + 1) * 1024],
                            w=([] if row == 0 else [SA_[0].at()]),
                            inc=[sV.inc()])
                    sch.add("sync", "dma_start",
                            out=outT[row * 128:(row + 1) * 128, :],
                            in_=rfd[:, :], w=[sV.at()], inc=[SA_[0].inc()])
                    row += 1
                for c in range(KC_H):
                    sch.add("vector", "tensor_copy", rfd[:, :],
                            gf[:, c * 1024:(c + 1) * 1024],
                            w=[SA_[0].at()], inc=[sV.inc()])
                    sch.add("sync", "dma_start",
                            out=outT[row * 128:(row + 1) * 128, :],
                            in_=rfd[:, :], w=[sV.at()], inc=[SA_[0].inc()])
                    row += 1
                sch.add("vector", "memset", rfd[:, :], 0.0,
                        w=[SA_[0].at()], inc=[sV.inc()])
                for c in range(row, KC_D):
                    sch.add("sync", "dma_start",
                            out=outT[c * 128:(c + 1) * 128, :],
                            in_=rfd[:, :],
                            w=[sV.at(), SA_[0].at()], inc=[SA_[0].inc()])
                sch.add("sync", "wait_ge", *SA_[0].at())
                seal(sch)
                sch.emit(nc)
                return

            with ExitStack() as es3:
                sb3 = lambda n, s, d: es3.enter_context(
                    nc.sbuf_tensor(n + tag, s, d, side="right"))
                w2f = [sb3(f"w2f{i}", [128, KC_H * 128], F32)
                       for i in range(2)]
                w2b = [sb3(f"w2b{i}", [128, KC_H * 128], BF16)
                       for i in range(2)]
                rst3 = [sb3(f"rst3{i}", [128, 1024], F32) for i in range(2)]
                r1c = [sb3(f"r1c{i}", [128, 1024], F32) for i in range(2)]
                xb3 = [sb3(f"xb3{i}", [128, 1024], BF16) for i in range(2)]
                sq3 = [sb3(f"sq3{i}", [128, 1024], BF16) for i in range(2)]
                sch = Sched()
                Sched.TAG[0] = "P5"
                phase_wait(sch)
                wps = [[PS[0], PS[1]], [PS[2], PS[3]]]
                stx = [PS[4][0:2, :], PS[5][0:2, :]]
                stq = [PS[6][0:2, :], PS[7][0:2, :]]
                wcv, mmd, evd, std2, sqd2, cvd2, sqa2, stod = (
                    {} for _ in range(8))
                for mc in range(KC_D):
                    p = mc % 2
                    wl = [] if mc < 2 else [(sA.sem, wcv[mc - 2])]
                    sch.add("gpsimd", "dma_start", out=w2f[p][:, :],
                            in_=w2r[mc], w=wl, inc=[W1[p].inc()])
                    wl = [W1[p].at()]
                    if mc >= 2:
                        wl.append((sT.sem, mmd[mc - 2]))
                    sch.add("scalar", "copy", w2b[p][:, :], w2f[p][:, :],
                            w=wl, inc=[sA.inc()])
                    wcv[mc] = sA.v
                    wl = [] if mc < 2 else [(sV.sem, evd[mc - 2])]
                    sch.add("sync", "dma_start", out=rst3[p][:, :],
                            in_=r0_src[mc * 128:(mc + 1) * 128, :], w=wl,
                            inc=[LA[p].inc()])
                    r0ld = LA[p].at()
                    pa = wps[p]
                    wl = [(sA.sem, wcv[mc])]
                    if mc >= 2:
                        wl.append((sV.sem, evd[mc - 2]))
                    for h in range(2):
                        for kc in range(KC_H):
                            sch.add("tensor", "matmul", pa[h][:, :],
                                    w2b[p][:, kc * 128:(kc + 1) * 128],
                                    gf[:, kc * 1024 + h * 512:
                                        kc * 1024 + (h + 1) * 512],
                                    start=(kc == 0), stop=False,
                                    w=(wl if (h == 0 and kc == 0) else []))
                        sch.add("tensor", "matmul", pa[h][:, :],
                                b2b[:, mc * 128:(mc + 1) * 128],
                                u2_bf[:, h * 512:(h + 1) * 512],
                                start=False, stop=True,
                                inc=([sT.inc()] if h == 1 else []))
                    mmd[mc] = sT.v
                    for h in range(2):
                        wl4 = []
                        if h == 0:
                            wl4 = [(sT.sem, mmd[mc]), r0ld]
                            if mc >= 2:
                                wl4.append(stod[mc - 2])
                                if not is_final:
                                    wl4.append((sV.sem, cvd2[mc - 2]))
                        sch.add("vector", "scalar_tensor_tensor",
                                r1c[p][:, h * 512:(h + 1) * 512],
                                pa[h][:, :], 0.5,
                                rst3[p][:, h * 512:(h + 1) * 512],
                                ALU.mult, ALU.add, w=wl4,
                                inc=([sV.inc()] if h == 1 else []))
                    evd[mc] = sV.v
                    sch.add("sync", "dma_start",
                            out=rdst[mc * 128:(mc + 1) * 128, :],
                            in_=r1c[p][:, :], w=[(sV.sem, evd[mc])],
                            inc=[SA_[p].inc()])
                    stod[mc] = SA_[p].at()
                    if not is_final:
                        wl = [(sV.sem, evd[mc])]
                        if mc >= 2:
                            wl += [(sT.sem, std2.get(mc - 2, 0)),
                                   (sT.sem, sqd2.get(mc - 2, 0))]
                        sch.add("vector", "tensor_copy", xb3[p][:, :],
                                r1c[p][:, :], w=wl, inc=[sV.inc()])
                        cvd2[mc] = sV.v
                        sch.add("scalar", "activation", sq3[p][:, :],
                                xb3[p][:, :], AF.Square,
                                w=[(sV.sem, cvd2[mc])], inc=[sA.inc()])
                        sqa2[mc] = sA.v
                        for mcx in ([mc - 1] if mc >= 1 else []) + (
                                [mc] if mc == KC_D - 1 else []):
                            px = mcx % 2
                            for i in range(2):
                                sch.add("tensor", "matmul", stx[i],
                                        onesc[:, :],
                                        xb3[px][:, i * 512:(i + 1) * 512],
                                        start=(mcx == 0),
                                        stop=(mcx == KC_D - 1),
                                        w=([(sV.sem, cvd2[mcx])] if i == 0
                                           else []),
                                        inc=([sT.inc()] if i == 1 else []))
                            std2[mcx] = sT.v
                            for i in range(2):
                                sch.add("tensor", "matmul", stq[i],
                                        onesc[:, :],
                                        sq3[px][:, i * 512:(i + 1) * 512],
                                        start=(mcx == 0),
                                        stop=(mcx == KC_D - 1),
                                        w=([(sA.sem, sqa2[mcx])] if i == 0
                                           else []),
                                        inc=([sT.inc()] if i == 1 else []))
                            sqd2[mcx] = sT.v
                if not is_final:
                    ln_post(sch, stx, stq, eps=1e-5, cmul=-1.0,
                            pb0=PS[0], pb1=PS[1], stats_done=(sT.sem, sT.v),
                            first_w=[(sV.sem, sV.v)])
                seal(sch)
                sch.emit(nc)

    ffn_phase(xT, r1d, is_final=False, tag="f1")

    def debug_copy_out(srcd):
        with ExitStack() as es:
            rr = es.enter_context(nc.sbuf_tensor("rrcopy", [128, 1024], F32,
                                                 side="right"))
            sch = Sched()
            Sched.TAG[0] = "DBG"
            phase_wait(sch)
            for c in range(KC_D):
                sch.add("sync", "dma_start", out=rr[:, :],
                        in_=srcd[c * 128:(c + 1) * 128, :],
                        w=([] if c == 0 else [SA_[0].at()]),
                        inc=[sSet.inc()])
                sch.add("sync", "dma_start",
                        out=outT[c * 128:(c + 1) * 128, :], in_=rr[:, :],
                        w=[sSet.at()], inc=[SA_[0].inc()])
            sch.add("sync", "wait_ge", *SA_[0].at())
            seal(sch)
            sch.emit(nc)

    def debug_dump_bf16(srcd_view):
        with ExitStack() as es:
            rb = es.enter_context(nc.sbuf_tensor("rdbg_b", [128, 1024], BF16,
                                                 side="right"))
            rf = es.enter_context(nc.sbuf_tensor("rdbg_f", [128, 1024], F32,
                                                 side="right"))
            sch = Sched()
            Sched.TAG[0] = "DBGB"
            phase_wait(sch)
            for c in range(KC_D):
                sch.add("sync", "dma_start", out=rb[:, :],
                        in_=srcd_view[c * 128:(c + 1) * 128, :],
                        w=([] if c == 0 else [SA_[0].at()]),
                        inc=[sSet.inc()])
                sch.add("vector", "tensor_copy", rf[:, :], rb[:, :],
                        w=[sSet.at()], inc=[sV.inc()])
                sch.add("sync", "dma_start",
                        out=outT[c * 128:(c + 1) * 128, :], in_=rf[:, :],
                        w=[sV.at()], inc=[SA_[0].inc()])
            sch.add("sync", "wait_ge", *SA_[0].at())
            seal(sch)
            sch.emit(nc)

    if stop_stage == 1:
        debug_copy_out(r1d)
        return nc

    with ExitStack() as es4:
        sb4 = lambda n, s, d: es4.enter_context(
            nc.sbuf_tensor(n, s, d, side="right"))

        with ExitStack() as es:
            sb = lambda n, s, d: es.enter_context(
                nc.sbuf_tensor(n, s, d, side="right"))
            xln = sb("xln", [128, KC_D * 1024], BF16)
            wfA = [sb(f"wfA{i}", [128, 2048], F32) for i in range(2)]
            wbA = [sb(f"wbA{i}", [128, 2048], BF16) for i in range(2)]
            qet = [sb(f"qet{i}", [128, 1024], BF16) for i in range(2)]
            q2t = [sb(f"q2t{i}", [128, 1024], BF16) for i in range(2)]
            vtb = sb("vtb", [128, 1024], BF16)
            vtm = sb("vtm", [128, 1024], BF16)
            xtp = sb("xtp", [128, 1024], F32)
            rst4 = [sb(f"rst4{i}", [128, 1024], F32) for i in range(2)]
            sch = Sched()
            Sched.TAG[0] = "P7"
            phase_wait(sch)
            xld = {}
            for c in range(KC_D):
                p = c % 2
                wl = [] if c < 2 else [(sV.sem, xld[c - 2])]
                sch.add("sync", "dma_start", out=rst4[p][:, :],
                        in_=r1d[c * 128:(c + 1) * 128, :], w=wl,
                        inc=[LA[p].inc()])
                ld = LA[p].at()
                sch.add("vector", "tensor_mul", xtp[:, :], rst4[p][:, :],
                        bcA[:, :], w=[ld])
                sch.add("vector", "tensor_add",
                        xln[:, c * 1024:(c + 1) * 1024], xtp[:, :],
                        bcC[:, :], inc=[sV.inc()])
                xld[c] = sV.v
            xall = sV.at()
            psQ = [[PS[0], PS[1]], [PS[2], PS[3]]]
            psRM = [PS[4][0:40, :], PS[5][0:40, :]]
            psV = PS[6][:, :].bitcast(BF16)
            wcv = {}
            mmh = {}
            evd = {}
            rmd = {}
            vst = {}
            qsd = {}
            for m in range(48):
                p = m % 2
                src_w = wqr if m < 32 else (wkr if m < 40 else wvr)
                hh = m if m < 32 else (m - 32 if m < 40 else m - 40)
                pa = psQ[p]
                for i in range(2):
                    wl = []
                    if i == 1:
                        wl = [(sV.sem, wcv[(m, 0)])]
                    elif m >= 2:
                        wl = [(sV.sem, wcv[(m - 2, 1)])]
                    sch.add("gpsimd", "dma_start", out=wfA[p][:, :],
                            in_=src_w[hh][:, i * 16:(i + 1) * 16, :],
                            w=wl, inc=[W1[p].inc()])
                    w2l = [W1[p].at()]
                    if i == 1:
                        w2l.append((sT.sem, mmh[(m, 0)]))
                    elif m >= 2:
                        w2l.append((sT.sem, mmh[(m - 2, 1)]))
                    sch.add("vector", "tensor_copy", wbA[p][:, :],
                            wfA[p][:, :], w=w2l, inc=[sV.inc()])
                    wcv[(m, i)] = sV.v
                    wl3 = [(sV.sem, wcv[(m, i)])]
                    if i == 0:
                        if m == 0:
                            wl3.append(xall)
                        if m >= 2:
                            wl3.append((sA.sem, evd[m - 2]))
                    for h in range(2):
                        for kk in range(16):
                            kc = i * 16 + kk
                            sch.add("tensor", "matmul", pa[h][:, :],
                                    wbA[p][:, kk * 128:(kk + 1) * 128],
                                    xln[:, kc * 1024 + h * 512:
                                        kc * 1024 + (h + 1) * 512],
                                    start=(kc == 0), stop=(kc == KC_D - 1),
                                    w=(wl3 if (h == 0 and kk == 0) else []),
                                    inc=([sT.inc()]
                                         if (h == 1 and kk == 15) else []))
                    mmh[(m, i)] = sT.v
                if m < 40:
                    for h in range(2):
                        wl = []
                        if h == 0:
                            wl = [(sT.sem, mmh[(m, 1)])]
                            if m >= 2:
                                wl += [qsd[m - 2],
                                       (sT.sem, rmd.get(m - 2, 0))]
                        sch.add("scalar", "copy",
                                qet[p][:, h * 512:(h + 1) * 512],
                                pa[h][:, :], w=wl)
                        sch.add("scalar", "activation",
                                q2t[p][:, h * 512:(h + 1) * 512],
                                pa[h][:, :], AF.Square,
                                inc=([sA.inc()] if h == 1 else []))
                    evd[m] = sA.v
                    sch.add("sync", "dma_start",
                            out=qd[m * 128:(m + 1) * 128, :],
                            in_=qet[p][:, :], w=[(sA.sem, evd[m])],
                            inc=[QS[p].inc()])
                    qsd[m] = QS[p].at()
                    for mx in ([m - 1] if 1 <= m else []) + (
                            [m] if m == 39 else []):
                        if mx >= 40 or mx < 0:
                            continue
                        px = mx % 2
                        for h in range(2):
                            sch.add("tensor", "matmul", psRM[h],
                                    W40[:, 39 - mx:79 - mx],
                                    q2t[px][:, h * 512:(h + 1) * 512],
                                    start=(mx == 0), stop=(mx == 39),
                                    w=([(sA.sem, evd[mx])] if h == 0
                                       else []),
                                    inc=([sT.inc()] if h == 1 else []))
                        rmd[mx] = sT.v
                else:
                    hv = m - 40
                    for h in range(2):
                        wl = []
                        if h == 0:
                            wl = [(sT.sem, mmh[(m, 1)])]
                            if m > 40:
                                wl.append((sT.sem, vst[m - 1]))
                        sch.add("scalar", "copy",
                                vtb[:, h * 512:(h + 1) * 512], pa[h][:, :],
                                w=wl, inc=([sA.inc()] if h == 1 else []))
                    evd[m] = sA.v
                    for l in range(8):
                        wl = []
                        if l == 0:
                            wl = [(sA.sem, evd[m])]
                            if m > 40:
                                wl.append((sA.sem, vcp[m - 1]))
                        sch.add("tensor", "transpose",
                                psV[:, l * 128:(l + 1) * 128],
                                vtb[:, l * 128:(l + 1) * 128],
                                iden_bf[:, :], w=wl,
                                inc=([sT.inc()] if l == 7 else []))
                    vst[m] = sT.v
                    wl = [(sT.sem, vst[m])]
                    if m > 40:
                        wl.append(vkd[m - 1])
                    sch.add("scalar", "copy", vtm[:, :], psV[:, :], w=wl,
                            inc=[sA.inc()])
                    vcp = vcp if m > 40 else {}
                    vcp[m] = sA.v
                    sch.add("gpsimd", "dma_start",
                            out=kv_in.rearrange(
                                "(q kvh l t) c -> q kvh t l c",
                                q=2, kvh=8, l=8)[1, hv],
                            in_=vtm[:, :], w=[(sA.sem, vcp[m])],
                            inc=[KVV.inc()])
                    vkd = vkd if m > 40 else {}
                    vkd[m] = KVV.at()
            seal(sch)
            sch.emit(nc)

        if stop_stage == 21:
            debug_dump_bf16(qd[0:4096, :])
            return nc

        rstdr_bf = sb4("rstdr_bf", [40, 1024], BF16)
        osb = sb4("osb", [128, NQ * 1024], BF16)
        sel4 = sb4("sel4", [40, 5120], BF16)
        mskb = sb4("mskb", [128, 2048], BF16)
        with ExitStack() as es:
            sb = lambda n, s, d: es.enter_context(
                nc.sbuf_tensor(n, s, d, side="right"))
            rmsm = sb("rmsm", [40, 1024], F32)
            ket = [sb(f"ket{i}", [128, 1024], BF16) for i in range(2)]
            knt = [sb(f"knt{i}", [128, 1024], BF16) for i in range(2)]
            bcs = [sb(f"bcs{i}", [128, 1024], F32) for i in range(2)]
            self4 = sb("self4", [40, 5120], F32)
            sch = Sched()
            Sched.TAG[0] = "P8"
            phase_wait(sch)
            sch.add("sync", "dma_start", out=self4[:, :], in_=sel40D,
                    inc=[sSet.inc()])
            for a in range(2):
                sch.add("sync", "dma_start", out=bcs[a][:, :],
                        in_=masksD[:, a * 1024:(a + 1) * 1024],
                        inc=[sSet.inc()])
            mld1 = sSet.at()
            sch.add("vector", "tensor_copy", sel4[:, :], self4[:, :],
                    w=[sSet.at()], inc=[sV.inc()])
            for a in range(2):
                sch.add("vector", "tensor_copy",
                        mskb[:, a * 1024:(a + 1) * 1024], bcs[a][:, :],
                        w=[mld1], inc=[sV.inc()])
            mcv1 = sV.at()
            for i in range(2):
                sch.add("scalar", "mul", rmsm[:, i * 512:(i + 1) * 512],
                        psRM[i], 1.0 / 128.0,
                        inc=([sA.inc()] if i == 1 else []))
            sch.add("scalar", "activation", rmsm[:, :], rmsm[:, :], AF.Ln,
                    bias=epsc[0:40, 1:2])
            sch.add("scalar", "activation", rmsm[:, :], rmsm[:, :], AF.Exp,
                    scale=-0.5, inc=[sA.inc()])
            rk = sA.at()
            sch.add("vector", "tensor_copy", rstdr_bf[:, :], rmsm[:, :],
                    w=[rk], inc=[sV.inc()])
            rbf = sV.at()
            bcd = {}
            knd = {}
            ksd = {}
            for h in list(range(32, 40)) + list(range(32)):
                p = h % 2
                hseq = (h - 32) if h >= 32 else (h + 8)
                wl = [rbf] if hseq < 2 else [(sV.sem, knd[hseq - 2])]
                sch.add("sync", "dma_start", out=ket[p][:, :],
                        in_=qd[h * 128:(h + 1) * 128, :], w=wl,
                        inc=[QA[p].inc()])
                kld = QA[p].at()
                for i in range(2):
                    wl = []
                    if i == 0:
                        wl = [rbf] if hseq == 0 else [
                            (sA.sem, bcd[hseq - 1])]
                        if hseq >= 2:
                            wl.append((sV.sem, knd[hseq - 2]))
                    sch.add("tensor", "matmul", PS[0 + i][:, :],
                            sel4[:, h * 128:(h + 1) * 128],
                            rstdr_bf[:, i * 512:(i + 1) * 512],
                            start=True, stop=True, w=wl,
                            inc=([sT.inc()] if i == 1 else []))
                bmm = sT.v
                for i in range(2):
                    sch.add("scalar", "copy",
                            bcs[p][:, i * 512:(i + 1) * 512],
                            PS[0 + i][:, :],
                            w=([(sT.sem, bmm), mcv1] if i == 0 else []),
                            inc=([sA.inc()] if i == 1 else []))
                bcd[hseq] = sA.v
                wv = qw if h < 32 else kw
                sch.add("vector", "scalar_tensor_tensor", knt[p][:, :],
                        ket[p][:, :], wv[:, :], bcs[p][:, :],
                        ALU.mult, ALU.mult, w=[kld, (sA.sem, bcd[hseq])],
                        inc=[sV.inc()])
                knd[hseq] = sV.v
                if h < 32:
                    sch.add("sync", "dma_start",
                            out=qd[h * 128:(h + 1) * 128, :],
                            in_=knt[p][:, :], w=[(sV.sem, knd[hseq])],
                            inc=[QS[p].inc()])
                else:
                    sch.add("gpsimd", "dma_start",
                            out=kv_in.rearrange("(q hh l) c -> q hh l c",
                                                q=2, l=8)
                            [0, (h - 32) * 128:(h - 31) * 128],
                            in_=knt[p][:, :], w=[(sV.sem, knd[hseq])],
                            inc=[KV[p].inc()])
                    ksd[h] = KV[p].at()
                if h == 39:
                    sch.add("gpsimd", "collective_compute", "AllGather",
                            ALU.bypass,
                            replica_groups=[[0, 1], [2, 3], [4, 5], [6, 7]],
                            ins=[kv_in.opt()], outs=[kv_out.opt()],
                            w=[KV[0].at(), KV[1].at(), KVV.at()],
                            inc=[sCC.inc()])
            seal(sch)
            sch.emit(nc)

        if stop_stage == 22:
            debug_dump_bf16(
                kv_out.rearrange("(a b) c -> a (b c)", b=8)[0:4096, :])
            return nc

        with ExitStack() as es:
            sb = lambda n, s, d: es.enter_context(
                nc.sbuf_tensor(n, s, d, side="right"))
            kfl = [sb(f"kfl{i}", [128, 2048], BF16) for i in range(2)]
            vfl = [sb(f"vfl{i}", [128, 2048], BF16) for i in range(2)]
            pTA = sb("pTA", [128, 8 * 512], BF16)
            pTB = sb("pTB", [128, 16 * 512], BF16)
            qnb = [sb(f"qnb{i}", [128, 1024], BF16) for i in range(2)]
            acol = [sb(f"acol{i}", [128, 16], F32) for i in range(2)]
            ngr = sb("ngr", [2, 2048], BF16)
            rcb = sb("rcb", [2, 1024], BF16)
            rtmp = sb("rtmp", [2, 1024], F32)
            bco = [sb(f"bco{i}", [128, 512], F32) for i in range(2)]
            q2b = [sb(f"q2b{i}", [128, 1024], BF16) for i in range(2)]
            msqA = sb("msqA", [2, 1024], BF16)
            msqB = sb("msqB", [2, 1024], BF16)
            aqr = sb("aqr", [2, 1024], BF16)
            kwb = sb("kwb", [128, 2], BF16)
            c2m = sb("c2m", [2, 2], BF16)
            e10 = sb("e10", [2, 2], BF16)
            onesC = sb("onesC", [128, 2], BF16)
            ktmp = msqA[:, 0:128]
            c2rs = rcb[:, 0:128]
            sch = Sched()
            Sched.TAG[0] = "P9"
            phase_wait(sch)
            sch.add("vector", "memset", pTA[:, :], 0.0)
            sch.add("vector", "memset", pTB[:, :], 0.0)
            sch.add("vector", "memset", kwb[:, :], 0.0)
            sch.add("vector", "memset", c2m[:, :], 0.0)
            sch.add("vector", "memset", e10[:, :], 0.0)
            sch.add("vector", "memset", e10[0:1, 0:2], 1.0,
                    inc=[sV.inc()])
            sch.add("scalar", "activation", kwb[:, 0:1], kw[:, 0:1],
                    AF.Square, w=[(sV.sem, sV.v)], inc=[sA.inc()], lp=True)
            sch.add("tensor", "transpose",
                    PS[2][:, :].bitcast(BF16)[0:2, 0:128], kwb[:, :],
                    iden_bf[:, :], w=[(sA.sem, sA.v)], inc=[sT.inc()])
            sch.add("scalar", "copy", ktmp,
                    PS[2][:, :].bitcast(BF16)[0:2, 0:128],
                    w=[(sT.sem, sT.v)], inc=[sA.inc()], lp=True)
            ktmp_v = sA.v
            sch.add("vector", "reduce_max", c2m[:, 0:1], ktmp,
                    axis=AX.X, w=[(sA.sem, sA.v)], inc=[sV.inc()], lp=True)
            sch.add("tensor", "matmul", PS[0][0:2, 0:128], c2m[:, :],
                    ones2[:, 0:128], start=True, stop=True,
                    w=[(sV.sem, sV.v)], inc=[sT.inc()])
            sch.add("scalar", "copy", c2rs, PS[0][0:2, 0:128],
                    w=[(sT.sem, sT.v)], inc=[sA.inc()], lp=True)
            sch.add("tensor", "matmul", PS[1][:, 0:2], c2rs,
                    e10[:, :], start=True, stop=True,
                    w=[(sA.sem, sA.v)], inc=[sT.inc()])
            cb0 = sA.v
            sch.add("scalar", "copy", onesC[:, :], PS[1][:, 0:2],
                    w=[(sT.sem, sT.v)], inc=[sA.inc()], lp=True)
            cb1 = sA.v
            kvd = {}
            kvuse = {}
            qld = {}
            acd = {}
            p2use = [0, 0]
            ngruse = {}
            mbank_use = [cb0, cb1]
            q2use = [0, 0]
            sumuse = [0]
            ouse = {6: (sV.sem, 0), 2: (sA.sem, ktmp_v)}
            bc7use = [0]
            bcouse = [0, 0]
            pTuse = {0: 0, 1: 0}
            slopes = _alibi_slopes(NQ)
            NKV_RUN = 1 if stop_stage == 24 else NKV
            NG_RUN = 1 if stop_stage == 24 else 4
            GRPS = 1 if stop_stage == 24 else 2

            AVB = {0: 6, 1: 2}

            def s2_sr(hq, ph, pk, grp, ngr_v):
                pT = pTA if grp == 0 else pTB
                kcm = 8 * grp + 8
                LEAD = 2
                evs = []

                def red(kc):
                    lmin = max(4 * grp, kc // 2)
                    qs = (lmin - 4 * grp) * 128
                    wl = [(sA.sem, evs[kc])]
                    if kc == 0:
                        wl += [(sA.sem, sumuse[0]),
                               ouse[AVB[grp]], kvd[hq // 4]]
                    sch.add("tensor", "matmul", PS[5][0:2, qs:512],
                            onesc[:, :],
                            pT[:, kc * 512 + qs:(kc + 1) * 512],
                            start=(kc == 0), stop=(kc == kcm - 1), w=wl)
                    sch.add("tensor", "matmul", PS[AVB[grp]][:, :],
                            vfl[pk][:, kc * 128:(kc + 1) * 128],
                            pT[:, kc * 512:(kc + 1) * 512],
                            start=(kc == 0), stop=(kc == kcm - 1),
                            inc=([sT.inc()] if kc == kcm - 1 else []))

                for kc in range(kcm):
                    lmin = max(4 * grp, kc // 2)
                    qs = (lmin - 4 * grp) * 128
                    bk = kc % 2
                    p2b = PS[3 + bk]
                    wl = [(sA.sem, p2use[bk])]
                    if kc == 0:
                        wl += [(sV.sem, ngr_v), qld[hq], kvd[hq // 4]]
                    sl = (ph * 2 + grp) * 512
                    sch.add("tensor", "matmul", p2b[:, qs:512],
                            ones2n[:, :], ngr[:, sl + qs:sl + 512],
                            start=True, stop=False, w=wl)
                    sch.add("tensor", "matmul", p2b[:, qs:512],
                            kfl[pk][:, kc * 128:(kc + 1) * 128],
                            qnb[ph][:, grp * 512 + qs:grp * 512 + 512],
                            start=False, stop=True, inc=[sT.inc()])
                    smm = sT.v
                    need_mask = (kc >= 8 * grp)
                    exp_w = [(sT.sem, smm)]
                    if need_mask:
                        sch.add("vector", "tensor_add",
                                p2b[:, qs:qs + 128], p2b[:, qs:qs + 128],
                                mskb[:, kc * 128:(kc + 1) * 128],
                                w=[(sT.sem, smm)], inc=[sV.inc()])
                        exp_w = [(sV.sem, sV.v)]
                    wl = exp_w + [(sV.sem, acd[hq])]
                    if pTuse[grp]:
                        wl.append((sT.sem, pTuse[grp]))
                    sch.add("scalar", "activation",
                            pT[:, kc * 512 + qs:(kc + 1) * 512],
                            p2b[:, qs:512], AF.Exp,
                            bias=acol[ph][:, kc:kc + 1], w=wl,
                            inc=[sA.inc()])
                    p2use[bk] = sA.v
                    evs.append(sA.v)
                    if kc >= LEAD:
                        red(kc - LEAD)
                ngruse[(ph, grp)] = sT.v
                for kc in range(kcm - LEAD, kcm):
                    red(kc)
                avd = sT.v
                pTuse[grp] = avd
                kvuse[hq // 4] = avd
                sch.add("scalar", "activation",
                        rtmp[:, grp * 512:(grp + 1) * 512], PS[5][0:2, :],
                        AF.Ln, w=[(sT.sem, avd)], inc=[sA.inc()])
                sumuse[0] = sA.v
                sch.add("scalar", "activation",
                        rcb[:, grp * 512:(grp + 1) * 512],
                        rtmp[:, grp * 512:(grp + 1) * 512], AF.Exp,
                        scale=-1.0, inc=[sA.inc()], lp=True)
                return avd, sA.v

            def s2_evac(hq, grp, rc_v):
                sch.add("tensor", "matmul", PS[7][:, :], ones2[:, :],
                        rcb[:, grp * 512:(grp + 1) * 512], start=True,
                        stop=True,
                        w=[(sA.sem, rc_v), (sV.sem, bcouse[grp]),
                           (sA.sem, bc7use[0])],
                        inc=[sT.inc()])
                bc_mm = sT.v
                sch.add("scalar", "copy", bco[grp][:, :], PS[7][:, :],
                        w=[(sT.sem, bc_mm)], inc=[sA.inc()])
                bcod = sA.v
                bc7use[0] = sA.v
                sch.add("vector", "tensor_mul",
                        osb[:, hq * 1024 + grp * 512:
                            hq * 1024 + (grp + 1) * 512],
                        PS[AVB[grp]][:, :], bco[grp][:, :],
                        w=[(sA.sem, bcod)], inc=[sV.inc()])
                ouse[AVB[grp]] = (sV.sem, sV.v)
                return sV.v

            NH = NKV_RUN * NG_RUN

            def kv_load(kv):
                pk = kv % 2
                wl = [(sCC.sem, 1)]
                if kv >= 2:
                    wl.append((sT.sem, kvuse[kv - 2]))
                for par in range(2):
                    sch.add("gpsimd", "dma_start",
                        out=kfl[pk][:, :].rearrange(
                            "p (l q c) -> p l q c", l=8, q=2)[:, :, par, :],
                        in_=kv_out.rearrange(
                            "(q hh l) c -> q hh l c", q=2, l=8)
                        [par, kv * 128:(kv + 1) * 128],
                        w=(wl if par == 0 else []), inc=[KF[pk].inc()])
                    sch.add("gpsimd", "dma_start",
                        out=vfl[pk][:, :].rearrange(
                            "p (l q c) -> p l q c", l=8, q=2)[:, :, par, :],
                        in_=kv_out.rearrange(
                            "(q x kvh l t) c -> q x kvh t l c",
                            q=2, x=2, kvh=8, l=8)[par, 1, kv],
                        inc=[KF[pk].inc()])
                kvd[kv] = KF[pk].at()

            def prep_head(hq):
                ph = hq % 2
                pk = (hq // 4) % 2
                if hq % 4 == 0:
                    kv_load(hq // 4)
                wl = []
                if hq >= 2:
                    wl.append((sT.sem, pTuse[0]))
                    wl.append((sT.sem, pTuse[1]))
                sch.add("sync", "dma_start", out=qnb[ph][:, :],
                        in_=qd[hq * 128:(hq + 1) * 128, :], w=wl,
                        inc=[QA[ph].inc()])
                qld[hq] = QA[ph].at()
                wl = []
                if hq >= 2:
                    wl += [(sA.sem, p2use[0]), (sA.sem, p2use[1])]
                sch.add("vector", "tensor_scalar_mul", acol[ph][:, :],
                        iotac[:, :], float(slopes[hq]), w=wl,
                        inc=[sV.inc()])
                acd[hq] = sV.v
                sch.add("vector", "tensor_scalar_mul", aqr[:, :],
                        qposs[:, :], float(slopes[hq]), lp=True)
                sch.add("scalar", "activation", q2b[ph][:, :],
                        qnb[ph][:, :], AF.Square,
                        w=[qld[hq], (sT.sem, q2use[ph])],
                        inc=[sA.inc()], lp=True)
                sq_v = sA.v
                ngv = []
                for grp in range(GRPS):
                    sl = (ph * 2 + grp) * 512
                    sch.add("tensor", "matmul", PS[grp][0:2, 0:512],
                            onesC[:, :],
                            q2b[ph][:, grp * 512:(grp + 1) * 512],
                            start=True, stop=True,
                            w=[(sA.sem, sq_v), (sA.sem, mbank_use[grp])],
                            inc=[sT.inc()])
                    q2use[ph] = sT.v
                    sch.add("scalar", "activation",
                            msqA[:, grp * 512:(grp + 1) * 512],
                            PS[grp][0:2, 0:512], AF.Ln, scale=133.12,
                            w=[(sT.sem, sT.v)], inc=[sA.inc()], lp=True)
                    mbank_use[grp] = sA.v
                    sch.add("scalar", "activation",
                            msqB[:, grp * 512:(grp + 1) * 512],
                            msqA[:, grp * 512:(grp + 1) * 512], AF.Exp,
                            scale=0.5, inc=[sA.inc()], lp=True)
                    sch.add("vector", "tensor_add", ngr[:, sl:sl + 512],
                            msqB[:, grp * 512:(grp + 1) * 512],
                            aqr[:, grp * 512:(grp + 1) * 512],
                            w=[(sA.sem, sA.v),
                               (sT.sem, ngruse.get((ph, grp), 0))],
                            inc=[sV.inc()], lp=True)
                    ngv.append(sV.v)
                return ngv

            ngs_q = {0: prep_head(0)}
            e1_pend = None
            for hq in range(NH):
                ph = hq % 2
                pk = (hq // 4) % 2
                if e1_pend is not None:
                    bcouse[1] = s2_evac(e1_pend[0], 1, e1_pend[1])
                    e1_pend = None
                if hq + 1 < NH:
                    ngs_q[hq + 1] = prep_head(hq + 1)
                ngs = ngs_q.pop(hq)
                if GRPS == 1:
                    avda, rca = s2_sr(hq, ph, pk, 0, ngs[0])
                    bcouse[0] = s2_evac(hq, 0, rca)
                else:
                    avda, rca = s2_sr(hq, ph, pk, 0, ngs[0])
                    avdb, rcb_v = s2_sr(hq, ph, pk, 1, ngs[1])
                    bcouse[0] = s2_evac(hq, 0, rca)
                    e1_pend = (hq, rcb_v)
            if e1_pend is not None:
                bcouse[1] = s2_evac(e1_pend[0], 1, e1_pend[1])
            seal(sch)
            sch.emit(nc)

        if stop_stage == 23:
            with ExitStack() as es:
                rf = es.enter_context(nc.sbuf_tensor(
                    "rdbg_f2", [128, 1024], F32, side="right"))
                sch = Sched()
                Sched.TAG[0] = "DBGO"
                phase_wait(sch)
                for c in range(KC_D):
                    sch.add("vector", "tensor_copy", rf[:, :],
                            osb[:, c * 1024:(c + 1) * 1024],
                            w=([] if c == 0 else [SA_[0].at()]),
                            inc=[sV.inc()])
                    sch.add("sync", "dma_start",
                            out=outT[c * 128:(c + 1) * 128, :], in_=rf[:, :],
                            w=[sV.at()], inc=[SA_[0].inc()])
                sch.add("sync", "wait_ge", *SA_[0].at())
                seal(sch)
                sch.emit(nc)
            return nc

        with ExitStack() as es:
            sb = lambda n, s, d: es.enter_context(
                nc.sbuf_tensor(n, s, d, side="right"))
            asb4 = sb("asb4", [66, 1024], F32)
            tmp4 = sb("tmp4", [66, 1024], F32)
            wfO = [sb(f"wfO{i}", [128, 2048], F32) for i in range(2)]
            wbO = [sb(f"wbO{i}", [128, 2048], BF16) for i in range(2)]
            rst5 = [sb(f"rst5{i}", [128, 1024], F32) for i in range(2)]
            r2c = [sb(f"r2c{i}", [128, 1024], F32) for i in range(2)]
            xb5 = [sb(f"xb5{i}", [128, 1024], BF16) for i in range(2)]
            sq5 = [sb(f"sq5{i}", [128, 1024], BF16) for i in range(2)]
            sch = Sched()
            Sched.TAG[0] = "P10"
            phase_wait(sch)
            wps = [[PS[0], PS[1]], [PS[2], PS[3]]]
            sa_ps = [PS[4][0:66, :], PS[5][0:66, :]]
            sq_ps = [PS[6][0:2, :], PS[7][0:2, :]]
            wcv = {}
            mmh = {}
            mmd = {}
            evd = {}
            std2 = {}
            sqd2 = {}
            cvd2 = {}
            sqa2 = {}
            fdd = {}
            stod = {}
            for mc in range(KC_D):
                p = mc % 2
                pa = wps[p]
                for i in range(2):
                    wl = []
                    if i == 1:
                        wl = [(sV.sem, wcv[(mc, 0)])]
                    elif mc >= 2:
                        wl = [(sV.sem, wcv[(mc - 2, 1)])]
                    sch.add("gpsimd", "dma_start", out=wfO[p][:, :],
                            in_=wor[mc][:, i * 16:(i + 1) * 16, :], w=wl,
                            inc=[W1[p].inc()])
                    w2l = [W1[p].at()]
                    if i == 1:
                        w2l.append((sT.sem, mmh[(mc, 0)]))
                    elif mc >= 2:
                        w2l.append((sT.sem, mmh[(mc - 2, 1)]))
                    sch.add("vector", "tensor_copy", wbO[p][:, :],
                            wfO[p][:, :], w=w2l, inc=[sV.inc()])
                    wcv[(mc, i)] = sV.v
                    wl3 = [(sV.sem, wcv[(mc, i)])]
                    if i == 0 and mc >= 2:
                        wl3.append((sV.sem, evd[mc - 2]))
                    for h in range(2):
                        for kk in range(16):
                            kc = i * 16 + kk
                            sch.add("tensor", "matmul", pa[h][:, :],
                                    wbO[p][:, kk * 128:(kk + 1) * 128],
                                    osb[:, kc * 1024 + h * 512:
                                        kc * 1024 + (h + 1) * 512],
                                    start=(kc == 0), stop=(kc == KC_D - 1),
                                    w=(wl3 if (h == 0 and kk == 0) else []),
                                    inc=([sT.inc()]
                                         if (h == 1 and kk == 15) else []))
                    mmh[(mc, i)] = sT.v
                mmd[mc] = sT.v
                wl = [] if mc < 2 else [(sV.sem, evd[mc - 2])]
                sch.add("sync", "dma_start", out=rst5[p][:, :],
                        in_=r1d[mc * 128:(mc + 1) * 128, :], w=wl,
                        inc=[LA[p].inc()])
                r1ld = LA[p].at()
                for h in range(2):
                    wl = []
                    if h == 0:
                        wl = [(sT.sem, mmd[mc]), r1ld]
                        if mc >= 2:
                            wl += [stod[mc - 2],
                                   (sV.sem, cvd2[mc - 2])]
                    sch.add("vector", "tensor_add",
                            r2c[p][:, h * 512:(h + 1) * 512], pa[h][:, :],
                            rst5[p][:, h * 512:(h + 1) * 512], w=wl,
                            inc=([sV.inc()] if h == 1 else []))
                evd[mc] = sV.v
                sch.add("sync", "dma_start",
                        out=r2d[mc * 128:(mc + 1) * 128, :],
                        in_=r2c[p][:, :], w=[(sV.sem, evd[mc])],
                        inc=[SA_[p].inc()])
                stod[mc] = SA_[p].at()
                wl = [(sV.sem, evd[mc])]
                if mc >= 2:
                    wl += [(sT.sem, std2.get(mc - 2, 0)),
                           (sA.sem, sqa2[mc - 2])]
                sch.add("vector", "tensor_copy", xb5[p][:, :], r2c[p][:, :],
                        w=wl, inc=[sV.inc()])
                cvd2[mc] = sV.v
                wl = [(sV.sem, cvd2[mc])]
                if mc >= 2:
                    wl.append((sT.sem, sqd2.get(mc - 2, 0)))
                sch.add("scalar", "activation", sq5[p][:, :], xb5[p][:, :],
                        AF.Square, w=wl, inc=[sA.inc()])
                sqa2[mc] = sA.v
                for mcx in ([mc - 1] if mc >= 1 else []) + (
                        [mc] if mc == KC_D - 1 else []):
                    px = mcx % 2
                    for i in range(2):
                        sch.add("tensor", "matmul", sa_ps[i],
                                satb[:, mcx * 66:(mcx + 1) * 66],
                                xb5[px][:, i * 512:(i + 1) * 512],
                                start=(mcx == 0), stop=(mcx == KC_D - 1),
                                w=([(sV.sem, cvd2[mcx])] if i == 0 else []),
                                inc=([sT.inc()] if i == 1 else []))
                    std2[mcx] = sT.v
                    for i in range(2):
                        sch.add("tensor", "matmul", sq_ps[i], onesc[:, :],
                                sq5[px][:, i * 512:(i + 1) * 512],
                                start=(mcx == 0), stop=(mcx == KC_D - 1),
                                w=([(sA.sem, sqa2[mcx])] if i == 0 else []),
                                inc=([sT.inc()] if i == 1 else []))
                    sqd2[mcx] = sT.v
                fdd[mc] = sV.v
            bc = ln_post(sch, sa_ps, sq_ps, eps=1e-5, cmul=-4.0,
                         pb0=PS[0], pb1=PS[1], stats_done=(sT.sem, sT.v),
                         first_w=[(sV.sem, sV.v)])
            ln_post_lora(sch, sa_ps, PS[0], PS[1], scale=0.5, bc_done=bc,
                         asb=asb4, tmp66=tmp4)
            seal(sch)
            sch.emit(nc)

    if stop_stage == 2:
        debug_copy_out(r2d)
        return nc

    with ExitStack() as es:
        sb = lambda n, s, d: es.enter_context(
            nc.sbuf_tensor(n, s, d, side="right"))
        fold5 = sb("fold5", [128, KC_F * 1024], F32)
        tfo5 = sb("tfo5", [128, 1024], F32)
        rst6 = [sb(f"rst6{i}", [128, 1024], F32) for i in range(2)]
        sch = Sched()
        Sched.TAG[0] = "P11"
        phase_wait(sch)
        fdd = {}
        for c in range(KC_D):
            p = c % 2
            wl = [] if c < 2 else [(sV.sem, fdd[c - 2])]
            sch.add("sync", "dma_start", out=rst6[p][:, :],
                    in_=r2d[c * 128:(c + 1) * 128, :], w=wl,
                    inc=[LA[p].inc()])
            ld = LA[p].at()
            fc = c % KC_F
            if c < KC_F:
                sch.add("vector", "tensor_copy",
                        fold5[:, fc * 1024:(fc + 1) * 1024], rst6[p][:, :],
                        w=[ld], inc=[sV.inc()])
            else:
                sch.add("vector", "tensor_add",
                        fold5[:, fc * 1024:(fc + 1) * 1024],
                        fold5[:, fc * 1024:(fc + 1) * 1024], rst6[p][:, :],
                        w=[ld], inc=[sV.inc()])
            fdd[c] = sV.v
        for c in range(KC_F):
            sch.add("vector", "tensor_mul", tfo5[:, :],
                    fold5[:, c * 1024:(c + 1) * 1024], bcA[:, :])
            sch.add("vector", "tensor_add", xfb[:, c * 1024:(c + 1) * 1024],
                    tfo5[:, :], bcC[:, :],
                    inc=([sV.inc()] if c == KC_F - 1 else []))
        seal(sch)
        sch.emit(nc)

    ffn_phase(r2d, outT, is_final=True, tag="f2",
              dump_after_p2=(stop_stage == 28))
    if stop_stage == 28:
        return nc

    sch = Sched()
    Sched.TAG[0] = "P12"
    phase_wait(sch)
    for s in ALL_STORE:
        if s.v > 0:
            sch.add("sync", "wait_ge", *s.at())
    sch.emit(nc)
    return nc


def _prep_weights(inputs):
    f = np.float32
    Wq = np.asarray(inputs["L0_attn_Wq"], f)
    Wk = np.asarray(inputs["L0_attn_Wk"], f)
    Wv = np.asarray(inputs["L0_attn_Wv"], f)
    Wo = np.asarray(inputs["L0_attn_Wo"], f)
    w1q = np.asarray(inputs["L0_ffn_w1q"], f)
    a1 = np.asarray(inputs["L0_ffn_a1"], f)
    b1 = np.asarray(inputs["L0_ffn_b1"], f)
    w3q = np.asarray(inputs["L0_ffn_w3q"], f)
    a3 = np.asarray(inputs["L0_ffn_a3"], f)
    b3 = np.asarray(inputs["L0_ffn_b3"], f)
    w2q = np.asarray(inputs["L0_ffn_w2q"], f)
    a2 = np.asarray(inputs["L0_ffn_a2"], f)
    b2 = np.asarray(inputs["L0_ffn_b2"], f)

    def tile4(wT, nm, nk):
        K, M = wT.shape
        assert K == nk * 128 and M == nm * 128
        return np.ascontiguousarray(
            wT.reshape(nk, 128, nm, 128).transpose(2, 1, 0, 3))

    wqr = tile4(np.ascontiguousarray(Wq.T), NQ, KC_D)
    wkr = tile4(np.ascontiguousarray(Wk.T), NKV, KC_D)
    wvr = tile4(np.ascontiguousarray(Wv.T), NKV, KC_D)
    wor = tile4(np.ascontiguousarray(Wo.T), KC_D, KC_D)

    def padh_cols(m):
        R = m.shape[0]
        z = np.zeros((R, 4, 2816), f)
        z[:, :, :2752] = m.reshape(R, 4, 2752)
        return z.reshape(R, HP)

    w1p = padh_cols(np.ascontiguousarray(w1q.T))
    w3p = padh_cols(np.ascontiguousarray(w3q.T))
    b1p = padh_cols(np.ascontiguousarray(b1.T))
    b3p = padh_cols(np.ascontiguousarray(b3.T))
    a2p = padh_cols(a2)
    w2p = np.zeros((2816, D), f)
    w2p[:2752] = w2q.T
    w1r = tile4(w1p, MC_H, KC_F)
    w3r = tile4(w3p, MC_H, KC_F)
    w2r = tile4(w2p, KC_D, KC_H)
    a2S = np.ascontiguousarray(
        a2p.reshape(2, MC_H, 128).transpose(2, 1, 0).reshape(128, 2 * MC_H))
    b2T = np.ascontiguousarray(b2.T)

    SaTa = np.zeros((D, 66), f)
    SaTa[:, 0] = 1.0
    SaTa[:, 1] = 1.0
    SaTa[:, 32] = a1[0]
    SaTa[:, 33] = a1[1]
    SaTa[:, 64] = a3[0]
    SaTa[:, 65] = a3[1]

    sel40 = np.zeros((40, 40 * 128), f)
    for h in range(40):
        sel40[h, h * 128:(h + 1) * 128] = 1.0
    iotacv = (np.arange(128, dtype=f)[:, None]
              + 128.0 * np.arange(16, dtype=f)[None, :])
    return dict(
        wqr=wqr, wkr=wkr, wvr=wvr, wor=wor, w1r=w1r, w3r=w3r, w2r=w2r,
        b1T=np.ascontiguousarray(b1p), b3T=np.ascontiguousarray(b3p),
        b2T=b2T, a2S=a2S, SaT=SaTa,
        qw=np.ascontiguousarray(
            np.asarray(inputs["L0_attn_q_norm_w"], f).reshape(128, 1)),
        kw=np.ascontiguousarray(
            np.asarray(inputs["L0_attn_k_norm_w"], f).reshape(128, 1)),
        sel40=sel40,
        iden=np.eye(128, dtype=f), iotac=np.ascontiguousarray(iotacv))


def _core_masks(parity):
    f = np.float32
    tri = np.where(np.arange(128)[:, None] <= np.arange(128)[None, :],
                   0.0, -1e30).astype(f)
    m = np.empty((128, 16 * 128), f)
    for kc in range(16):
        cc = kc % 2
        if cc == parity:
            t = tri
        elif parity == 0:
            t = np.full((128, 128), -1e30, f)
        else:
            t = np.zeros((128, 128), f)
        m[:, kc * 128:(kc + 1) * 128] = t
    return m


_BLOCKS = {p: [p + 2 * i for i in range(8)] for p in (0, 1)}


def kernel(_stop=3, **inputs):
    x = np.asarray(inputs["L0_attn_x"], np.float32)
    shared = _prep_weights(inputs)
    masks = {p: _core_masks(p) for p in (0, 1)}
    qpos = {}
    for p in (0, 1):
        qp = np.zeros((2, T), np.float32)
        qp[0] = (128 * np.repeat(np.array(_BLOCKS[p], np.float32), 128)
                 + np.tile(np.arange(128, dtype=np.float32), 8))
        qpos[p] = qp
    in_maps = []
    for core in range(8):
        b, p = core // 2, core % 2
        blocks = _BLOCKS[p]
        xs = x[b].reshape(16, 128, D)[blocks].reshape(T, D)
        m = dict(shared)
        m["xT"] = np.ascontiguousarray(xs.T)
        m["masks"] = masks[p]
        m["qpos"] = qpos[p]
        in_maps.append(m)
    nc = build_nc(stop_stage=_stop)
    res = run_bass_kernel_spmd(nc, in_maps, core_ids=list(range(8)))
    out = np.empty((4, 2048, D), np.float32)
    for core in range(8):
        b, p = core // 2, core % 2
        blocks = _BLOCKS[p]
        oT = res.results[core]["outT"]
        out[b].reshape(16, 128, D)[blocks] = (
            oT.T.reshape(8, 128, D))
    return out


# version 13
# speedup vs baseline: 1.2462x; 1.2462x over previous
import math
from contextlib import ExitStack
import numpy as np
import concourse.bass as bass
import concourse.mybir as mybir
from concourse.bass_utils import run_bass_kernel_spmd

F32 = mybir.dt.float32
BF16 = mybir.dt.bfloat16
AF = mybir.ActivationFunctionType
ALU = mybir.AluOpType
AX = mybir.AxisListType

D = 4096
T = 1024
S = 2048
NB = 8
HP = 11264
NKV = 8
NQ = 32
HD = 128
KC_D = 32
KC_F = 8
MC_H = 88
KC_H = 22
ISQ = 1.0 / math.sqrt(HD)


def _alibi_slopes(n):
    start = 2 ** (-2 ** (-(math.log2(n) - 3)))
    return np.array([start * start ** i for i in range(n)], dtype=np.float32)


class Sched:
    ENGINES = ("gpsimd", "sync", "tensor", "vector", "scalar")

    def __init__(self):
        self.L = {e: [] for e in self.ENGINES}

    TAG = [""]

    def add(self, eng, meth, *a, w=(), inc=(), lp=False, **k):
        self.L[eng].append([meth, a, k, list(w), list(inc), lp,
                            f"{self.TAG[0]}#{len(self.L[eng])}"])

    def emit(self, nc):
        from contextlib import nullcontext
        with nc.Block() as blk:
            for eng in self.ENGINES:
                if not self.L[eng]:
                    continue

                def runner(e, L=self.L[eng]):
                    for (meth, a, k, w, inc, lp, tag) in L:
                        for (sem, v) in w:
                            e.wait_ge(sem, v)
                        ctx = (nc.allow_low_precision("bf16 matmul operand")
                               if lp else nullcontext())
                        with ctx:
                            inst = getattr(e, meth)(*a, **k)
                        try:
                            inst.annotate(tag)
                        except Exception:
                            pass
                        for (sem, n) in inc:
                            inst.then_inc(sem, n)
                getattr(blk, eng)(runner)


class S16:

    def __init__(self, nc, name):
        self.sem = nc.alloc_semaphore(name)
        self.v = 0

    def inc(self):
        self.v += 16
        return (self.sem, 16)

    def at(self):
        return (self.sem, self.v)


class Ctr:
    def __init__(self, sem):
        self.sem = sem
        self.v = 0

    def inc(self, n=1):
        self.v += n
        return (self.sem, n)

    def at(self):
        return (self.sem, self.v)


def build_nc(stop_stage=3):
    nc = bass.Bass("TRN2", target_bir_lowering=False,
                   detect_race_conditions=False)

    def din(name, shape, dt=F32):
        return nc.dram_tensor(name, shape, dt, kind="ExternalInput").ap()

    xT = din("xT", [D, T])
    wqr = din("wqr", [NQ, 128, KC_D, 128])
    wkr = din("wkr", [NKV, 128, KC_D, 128])
    wvr = din("wvr", [NKV, 128, KC_D, 128])
    wor = din("wor", [KC_D, 128, KC_D, 128])
    w1r = din("w1r", [MC_H, 128, KC_F, 128])
    w3r = din("w3r", [MC_H, 128, KC_F, 128])
    w2r = din("w2r", [KC_D, 128, KC_H, 128])
    b1T = din("b1T", [2, HP])
    b3T = din("b3T", [2, HP])
    b2T = din("b2T", [2, D])
    a2S = din("a2S", [128, 2 * MC_H])
    SaT = din("SaT", [D, 66])
    qwD = din("qw", [128, 1])
    kwD = din("kw", [128, 1])
    masksD = din("masks", [128, 16 * 128])
    qposD = din("qpos", [2, T])
    sel40D = din("sel40", [40, 40 * 128])
    idenD = din("iden", [128, 128])
    iotacD = din("iotac", [128, 16])

    outT = nc.dram_tensor("outT", [D, T], F32, kind="ExternalOutput").ap()
    r1d = nc.dram_tensor("r1d", [D, T], F32, kind="Internal").ap()
    r2d = nc.dram_tensor("r2d", [D, T], F32, kind="Internal").ap()
    qd = nc.dram_tensor("qd", [40 * 128, T], BF16, kind="Internal").ap()
    kv_in = nc.dram_tensor("kv_in", [16384, 128], BF16, kind="Internal").ap()
    kv_out = nc.dram_tensor("kv_out", [32768, 128], BF16, kind="Internal").ap()

    sT = Ctr(nc.alloc_semaphore("sT"))
    sV = Ctr(nc.alloc_semaphore("sV"))
    sA = Ctr(nc.alloc_semaphore("sA"))
    sW = Ctr(nc.alloc_semaphore("sW"))
    sR = Ctr(nc.alloc_semaphore("sR"))
    sO = Ctr(nc.alloc_semaphore("sO"))
    sCC = Ctr(nc.alloc_semaphore("sCC"))
    sB = Ctr(nc.alloc_semaphore("sB"))
    sSet = S16(nc, "sSet")
    LA = [S16(nc, f"LA{i}") for i in range(2)]
    QA = [S16(nc, f"QA{i}") for i in range(2)]
    W1 = [S16(nc, f"W1{i}") for i in range(2)]
    W3 = [S16(nc, f"W3{i}") for i in range(2)]
    KF = [S16(nc, f"KF{i}") for i in range(2)]
    SA_ = [S16(nc, f"SA{i}") for i in range(2)]
    QS = [S16(nc, f"QS{i}") for i in range(2)]
    KV = [S16(nc, f"KV{i}") for i in range(2)]
    KVV = S16(nc, "KVV")
    ALL_STORE = SA_ + QS + KV + [KVV]

    AL = nc.alloc_sbuf_tensor
    ones2 = AL("ones2", [2, 128], BF16, side="left")
    ones2n = AL("ones2n", [2, 128], BF16, side="left")
    onesc = AL("onesc", [128, 2], BF16, side="left")
    iden_bf = AL("iden_bf", [128, 128], BF16, side="left")
    qw = AL("qw_s", [128, 1], F32, side="left")
    kw = AL("kw_s", [128, 1], F32, side="left")
    iotac = AL("iotac_s", [128, 16], F32, side="left")
    u_bf = AL("u_bf", [66, 1024], BF16, side="left")
    u2_bf = AL("u2_bf", [2, 1024], BF16, side="left")
    bcA = AL("bcA", [128, 1024], F32, side="left")
    bcC = AL("bcC", [128, 1024], F32, side="left")
    mrow = AL("mrow", [2, 1024], F32, side="left")
    mrow_bf = AL("mrow_bf", [2, 1024], BF16, side="left")
    vrow = AL("vrow", [2, 1024], F32, side="left")
    rsr = AL("rsr", [2, 1024], F32, side="left")
    rsr_bf = AL("rsr_bf", [2, 1024], BF16, side="left")
    crow = AL("crow", [2, 1024], F32, side="left")
    crow_bf = AL("crow_bf", [2, 1024], BF16, side="left")
    negsA = AL("negsA", [66, 1], F32, side="left")
    xfb = AL("xfb", [128, KC_F * 1024], BF16, side="left")
    satb = AL("satb", [128, KC_D * 66], BF16, side="left")
    W40 = AL("W40", [128, 79], BF16, side="left")
    epsc = AL("epsc", [128, 2], F32, side="left")
    qposs = AL("qposs", [2, 1024], F32, side="left")

    PS = [nc.alloc_psum_tensor(f"ps{i}", [128, 512], F32) for i in range(8)]

    BARP = [[]]
    OWN = {"tensor": sT, "vector": sV, "scalar": sA}

    def seal(sch):
        pairs = [(sCC.sem, sCC.v), (sSet.sem, sSet.v)]
        pairs += [s.at() for s in ALL_STORE if s.v > 0]
        for eng in ("tensor", "vector", "scalar"):
            if sch.L[eng]:
                last = sch.L[eng][-1]
                if not last[4]:
                    last[4].append(OWN[eng].inc())
                pairs.append(OWN[eng].at())
        BARP[0] = [p for p in pairs if p[1] > 0]

    def phase_wait(sch):
        for eng in Sched.ENGINES:
            for (sem, v) in BARP[0]:
                sch.add(eng, "wait_ge", sem, v)

    def ln_post(sch, sa_ps, sq_ps, eps, cmul, pb0, pb1, stats_done,
                first_w=(), fold=None, tfa=None):
        n = float(D)
        for i in range(2):
            sch.add("scalar", "mul", mrow[:, i * 512:(i + 1) * 512],
                    sa_ps[i][0:2, :], 1.0 / n,
                    w=([stats_done] if i == 0 else []),
                    inc=([sA.inc()] if i == 1 else []))
        m_done = sA.at()
        sch.add("vector", "tensor_copy", mrow_bf[:, :], mrow[:, :], w=[m_done])
        sch.add("vector", "tensor_mul", vrow[:, :], mrow[:, :], mrow[:, :])
        for i in range(2):
            sch.add("vector", "scalar_tensor_tensor",
                    rsr[:, i * 512:(i + 1) * 512], sq_ps[i][0:2, :], 1.0 / n,
                    vrow[:, i * 512:(i + 1) * 512], ALU.mult, ALU.subtract,
                    inc=([sV.inc()] if i == 1 else []))
        v_done = sV.at()
        sch.add("scalar", "activation", rsr[:, :], rsr[:, :], AF.Ln,
                bias=epsc[0:2, 0:1], w=[v_done])
        sch.add("scalar", "activation", rsr[:, :], rsr[:, :], AF.Exp,
                scale=-0.5, inc=[sA.inc()])
        r_done = sA.at()
        sch.add("vector", "tensor_copy", rsr_bf[:, :], rsr[:, :], w=[r_done])
        sch.add("vector", "tensor_mul", crow[:, :], mrow[:, :], rsr[:, :])
        sch.add("vector", "tensor_scalar_mul", crow[:, :], crow[:, :],
                float(cmul))
        sch.add("vector", "tensor_copy", crow_bf[:, :], crow[:, :],
                inc=[sV.inc()])
        cb_done = sV.at()
        for i in range(2):
            wl0 = ([cb_done] + list(first_w)) if i == 0 else [sA.at()]
            sch.add("tensor", "matmul", pb0[:, :], ones2[:, :],
                    rsr_bf[:, i * 512:(i + 1) * 512], start=True, stop=True,
                    w=wl0)
            sch.add("tensor", "matmul", pb1[:, :], ones2[:, :],
                    crow_bf[:, i * 512:(i + 1) * 512], start=True, stop=True,
                    inc=[sT.inc()])
            t_done = sT.at()
            sch.add("scalar", "copy", bcA[:, i * 512:(i + 1) * 512],
                    pb0[:, :], w=[t_done])
            sch.add("scalar", "copy", bcC[:, i * 512:(i + 1) * 512],
                    pb1[:, :], inc=[sA.inc()])
        ab_done = sA.at()
        if fold is not None:
            for c in range(KC_F):
                sch.add("vector", "tensor_mul", tfa[:, :],
                        fold[:, c * 1024:(c + 1) * 1024], bcA[:, :],
                        w=([ab_done] if c == 0 else []))
                sch.add("vector", "tensor_add",
                        xfb[:, c * 1024:(c + 1) * 1024], tfa[:, :], bcC[:, :],
                        inc=([sV.inc()] if c == KC_F - 1 else []))
        return ab_done

    def ln_post_lora(sch, sa_ps, pb0, pb1, scale, bc_done, asb, tmp66):
        for i in range(2):
            sch.add("scalar", "copy", asb[:, i * 512:(i + 1) * 512],
                    sa_ps[i][:, :], inc=([sA.inc()] if i == 1 else []))
        a_done = sA.at()
        mb = []
        for i in range(2):
            pb = pb0 if i == 0 else pb1
            sch.add("tensor", "matmul", pb[0:66, :], ones2[:, 0:66],
                    mrow_bf[:, i * 512:(i + 1) * 512], start=True, stop=True,
                    w=([bc_done] if i == 0 else []), inc=[sT.inc()])
            mb.append(sT.at())
        for i in range(2):
            pb = pb0 if i == 0 else pb1
            sch.add("vector", "scalar_tensor_tensor",
                    tmp66[:, i * 512:(i + 1) * 512], pb[0:66, :], negsA[:, :],
                    asb[:, i * 512:(i + 1) * 512], ALU.mult, ALU.add,
                    w=[a_done, mb[i]], inc=[sV.inc()])
        tdone = sV.at()
        rb = []
        for i in range(2):
            pb = pb0 if i == 0 else pb1
            sch.add("tensor", "matmul", pb[0:66, :], ones2[:, 0:66],
                    rsr_bf[:, i * 512:(i + 1) * 512], start=True, stop=True,
                    w=([tdone] if i == 0 else []), inc=[sT.inc()])
            rb.append(sT.at())
        for i in range(2):
            pb = pb0 if i == 0 else pb1
            sch.add("vector", "scalar_tensor_tensor",
                    u_bf[:, i * 512:(i + 1) * 512],
                    tmp66[:, i * 512:(i + 1) * 512], float(scale),
                    pb[0:66, :], ALU.mult, ALU.mult, w=[rb[i]],
                    inc=([sV.inc()] if i == 1 else []))

    with ExitStack() as es:
        stg = es.enter_context(nc.sbuf_tensor("stg_set", [128, 2240], F32,
                                              side="right"))
        sch = Sched()
        Sched.TAG[0] = "P1"
        sch.add("vector", "memset", ones2[:, :], 0.0)
        sch.add("vector", "memset", ones2[0:1, :], 1.0)
        sch.add("vector", "memset", ones2n[:, :], 0.0)
        sch.add("vector", "memset", ones2n[0:1, :], -1.0)
        sch.add("vector", "memset", onesc[:, :], 1.0)
        sch.add("vector", "memset", W40[:, :], 0.0)
        sch.add("vector", "memset", W40[:, 39:40], 1.0)
        sch.add("vector", "memset", epsc[:, 0:1], 1e-5)
        sch.add("vector", "memset", epsc[:, 1:2], 1e-6)
        sch.add("sync", "dma_start", out=stg[:, 2112:2240], in_=idenD,
                inc=[sSet.inc()])
        sch.add("sync", "dma_start", out=qw[:, :], in_=qwD, inc=[sSet.inc()])
        sch.add("sync", "dma_start", out=kw[:, :], in_=kwD, inc=[sSet.inc()])
        sch.add("sync", "dma_start", out=iotac[:, :], in_=iotacD,
                inc=[sSet.inc()])
        sch.add("sync", "dma_start", out=qposs[:, :], in_=qposD,
                inc=[sSet.inc()])
        sch.add("sync", "dma_start", out=stg[:, 0:2112],
                in_=SaT.rearrange("(c p) j -> p c j", p=128),
                inc=[sSet.inc()])
        ld = sSet.at()
        sch.add("vector", "tensor_copy", iden_bf[:, :], stg[:, 2112:2240],
                w=[ld])
        sch.add("vector", "tensor_scalar_mul", qw[:, :], qw[:, :], ISQ)
        sch.add("vector", "tensor_copy", satb[:, :], stg[:, 0:2112])
        seal(sch)
        sch.emit(nc)

    with ExitStack() as es:
        sb = lambda n, s, d: es.enter_context(
            nc.sbuf_tensor(n, s, d, side="right"))
        fold = sb("fold0", [128, KC_F * 1024], F32)
        asb0 = sb("asb0", [66, 1024], F32)
        tmp0 = sb("tmp0", [66, 1024], F32)
        tfold = sb("tfold0", [128, 1024], F32)
        rstg = [sb(f"rstg0{i}", [128, 1024], F32) for i in range(2)]
        xbf = [sb(f"xbf0{i}", [128, 1024], BF16) for i in range(2)]
        sqb = [sb(f"sqb0{i}", [128, 1024], BF16) for i in range(2)]
        sch = Sched()
        Sched.TAG[0] = "P2"
        phase_wait(sch)
        for kc in range(KC_D):
            sch.add("tensor", "matmul", PS[6][0:66, 0:2],
                    satb[:, kc * 66:(kc + 1) * 66], onesc[:, :],
                    start=(kc == 0), stop=(kc == KC_D - 1),
                    inc=([sT.inc()] if kc == KC_D - 1 else []))
        sch.add("scalar", "mul", negsA[:, :], PS[6][0:66, 0:1], -1.0,
                w=[sT.at()], inc=[sA.inc()])
        sa_ps = [PS[0][0:66, :], PS[1][0:66, :]]
        sq_ps = [PS[2][0:2, :], PS[3][0:2, :]]
        cvd, std, sqd, sqa, fdd = {}, {}, {}, {}, {}
        for c in range(KC_D):
            p = c % 2
            wl = [] if c < 2 else [(sV.sem, fdd[c - 2]), (sV.sem, cvd[c - 2])]
            sch.add("sync", "dma_start", out=rstg[p][:, :],
                    in_=xT[c * 128:(c + 1) * 128, :], w=wl,
                    inc=[LA[p].inc()])
            ld = LA[p].at()
            wl = [ld]
            if c >= 2:
                wl += [(sT.sem, std[c - 2]), (sA.sem, sqa[c - 2])]
            sch.add("vector", "tensor_copy", xbf[p][:, :], rstg[p][:, :],
                    w=wl, inc=[sV.inc()])
            cvd[c] = sV.v
            wl = [(sV.sem, cvd[c])]
            if c >= 2:
                wl.append((sT.sem, sqd[c - 2]))
            sch.add("scalar", "activation", sqb[p][:, :], xbf[p][:, :],
                    AF.Square, w=wl, inc=[sA.inc()])
            sqa[c] = sA.v
            for i in range(2):
                sch.add("tensor", "matmul", sa_ps[i],
                        satb[:, c * 66:(c + 1) * 66],
                        xbf[p][:, i * 512:(i + 1) * 512],
                        start=(c == 0), stop=(c == KC_D - 1),
                        w=([(sV.sem, cvd[c])] if i == 0 else []),
                        inc=([sT.inc()] if i == 1 else []))
            std[c] = sT.v
            for i in range(2):
                sch.add("tensor", "matmul", sq_ps[i], onesc[:, :],
                        sqb[p][:, i * 512:(i + 1) * 512],
                        start=(c == 0), stop=(c == KC_D - 1),
                        w=([(sA.sem, sqa[c])] if i == 0 else []),
                        inc=([sT.inc()] if i == 1 else []))
            sqd[c] = sT.v
            fc = c % KC_F
            if c < KC_F:
                sch.add("vector", "tensor_copy",
                        fold[:, fc * 1024:(fc + 1) * 1024], rstg[p][:, :],
                        w=[ld], inc=[sV.inc()])
            else:
                sch.add("vector", "tensor_add",
                        fold[:, fc * 1024:(fc + 1) * 1024],
                        fold[:, fc * 1024:(fc + 1) * 1024], rstg[p][:, :],
                        w=[ld], inc=[sV.inc()])
            fdd[c] = sV.v
        bc = ln_post(sch, sa_ps, sq_ps, eps=1e-5, cmul=-4.0,
                     pb0=PS[4], pb1=PS[5], stats_done=(sT.sem, sT.v),
                     fold=fold, tfa=tfold)
        ln_post_lora(sch, sa_ps, PS[4], PS[5], scale=0.5, bc_done=bc,
                     asb=asb0, tmp66=tmp0)
        seal(sch)
        sch.emit(nc)

    def ffn_phase(r0_src, rdst, is_final, tag, dump_after_p2=False):
        with ExitStack() as es:
            sb = lambda n, s, d: es.enter_context(
                nc.sbuf_tensor(n + tag, s, d, side="right"))
            gf = sb("gf", [128, KC_H * 1024], BF16)
            bv13 = sb("bv13", [66, HP], BF16)
            b2b = sb("b2b", [2, D], BF16)
            a2b = sb("a2b", [128, 2 * MC_H], BF16)
            with ExitStack() as es2:
                bstg = es2.enter_context(nc.sbuf_tensor(
                    "bstg" + tag, [128, 2 * MC_H], F32, side="right"))
                bstg2 = es2.enter_context(nc.sbuf_tensor(
                    "bstg2" + tag, [66, HP], F32, side="right"))
                sch = Sched()
                Sched.TAG[0] = "P3"
                phase_wait(sch)
                sch.add("sync", "dma_start", out=bstg2[32:34, :], in_=b1T,
                        inc=[sSet.inc()])
                sch.add("sync", "dma_start", out=bstg2[64:66, :], in_=b3T,
                        inc=[sSet.inc()])
                sch.add("sync", "dma_start", out=bstg2[0:2, 0:D], in_=b2T,
                        inc=[sSet.inc()])
                sch.add("sync", "dma_start", out=bstg[:, :], in_=a2S,
                        inc=[sSet.inc()])
                bl = sSet.at()
                sch.add("vector", "memset", bv13[:, :], 0.0, w=[bl])
                sch.add("vector", "tensor_copy", bv13[32:34, :],
                        bstg2[32:34, :])
                sch.add("vector", "tensor_copy", bv13[64:66, :],
                        bstg2[64:66, :])
                sch.add("vector", "tensor_copy", b2b[:, :], bstg2[0:2, 0:D])
                sch.add("vector", "tensor_copy", a2b[:, :], bstg[:, :],
                        inc=[sV.inc()])
                seal(sch)
                sch.emit(nc)

            esp2 = ExitStack()
            sbp = lambda n, s, d: esp2.enter_context(
                nc.sbuf_tensor(n + tag, s, d, side="right"))
            w1f = [sbp(f"w1f{i}", [128, 1024], F32) for i in range(2)]
            w3f = [sbp(f"w3f{i}", [128, 1024], F32) for i in range(2)]
            w1b = [sbp(f"w1b{i}", [128, 1024], BF16) for i in range(2)]
            w3b = [sbp(f"w3b{i}", [128, 1024], BF16) for i in range(2)]
            s1b = sbp("s1b", [128, 1024], F32)
            gbf = [sbp(f"gbf{i}", [128, 1024], BF16) for i in range(2)]
            tf2 = sbp("tf2", [128, 1024], F32)

            sch = Sched()
            Sched.TAG[0] = "P4"
            phase_wait(sch)
            y1p = [[PS[0], PS[1]], [PS[2], PS[3]]]
            y3p = [PS[4], PS[5]]
            psU = [PS[6][0:2, :], PS[7][0:2, :]]
            wcv, sld, y1d, y3d, gpd, a2d, pmc = ({}, {}, {}, {}, {}, {}, {})
            for c in range(KC_H):
                for qt in range(4):
                    mc = qt * KC_H + c
                    it = c * 4 + qt
                    p = it % 2
                    wl = [] if it < 2 else [(sA.sem, wcv[it - 2])]
                    sch.add("gpsimd", "dma_start", out=w1f[p][:, :],
                            in_=w1r[mc], w=wl, inc=[W1[p].inc()])
                    sch.add("gpsimd", "dma_start", out=w3f[p][:, :],
                            in_=w3r[mc], inc=[W3[p].inc()])
                    wl = [W1[p].at(), W3[p].at()]
                    if it >= 2:
                        wl.append((sT.sem, y3d[it - 2]))
                    sch.add("scalar", "copy", w1b[p][:, :], w1f[p][:, :],
                            w=wl)
                    sch.add("scalar", "copy", w3b[p][:, :], w3f[p][:, :],
                            inc=[sA.inc()])
                    wcv[it] = sA.v
                    pa = y1p[p]
                    wl = [(sA.sem, wcv[it])]
                    if it >= 2:
                        wl.append((sA.sem, sld[it - 2]))
                    for h in range(2):
                        for kc in range(KC_F):
                            sch.add("tensor", "matmul", pa[h][:, :],
                                    w1b[p][:, kc * 128:(kc + 1) * 128],
                                    xfb[:, kc * 1024 + h * 512:
                                        kc * 1024 + (h + 1) * 512],
                                    start=(kc == 0), stop=False,
                                    w=(wl if (h == 0 and kc == 0) else []))
                        sch.add("tensor", "matmul", pa[h][:, :],
                                bv13[32:34, mc * 128:(mc + 1) * 128],
                                u_bf[32:34, h * 512:(h + 1) * 512],
                                start=False, stop=True,
                                inc=([sT.inc()] if h == 1 else []))
                    y1d[it] = sT.v
                    wl = [] if it < 1 else [(sV.sem, gpd[it - 1])]
                    for h in range(2):
                        for kc in range(KC_F):
                            sch.add("tensor", "matmul", y3p[h][:, :],
                                    w3b[p][:, kc * 128:(kc + 1) * 128],
                                    xfb[:, kc * 1024 + h * 512:
                                        kc * 1024 + (h + 1) * 512],
                                    start=(kc == 0), stop=False,
                                    w=(wl if (h == 0 and kc == 0) else []))
                        sch.add("tensor", "matmul", y3p[h][:, :],
                                bv13[64:66, mc * 128:(mc + 1) * 128],
                                u_bf[64:66, h * 512:(h + 1) * 512],
                                start=False, stop=True,
                                inc=([sT.inc()] if h == 1 else []))
                    y3d[it] = sT.v
                    for h in range(2):
                        wl5 = []
                        if h == 0:
                            wl5 = [(sT.sem, y1d[it])]
                            if it >= 1:
                                wl5.append((sV.sem, gpd[it - 1]))
                        sch.add("scalar", "activation",
                                s1b[:, h * 512:(h + 1) * 512], pa[h][:, :],
                                AF.Silu, w=wl5,
                                inc=([sA.inc()] if h == 1 else []))
                    sld[it] = sA.v
                    gp = it % 2
                    wl6 = [(sT.sem, y3d[it]), (sA.sem, sld[it])]
                    if it >= 2:
                        wl6.append((sT.sem, a2d[it - 2]))
                    for h in range(2):
                        sch.add("vector", "tensor_mul",
                                gbf[gp][:, h * 512:(h + 1) * 512],
                                s1b[:, h * 512:(h + 1) * 512], y3p[h][:, :],
                                w=(wl6 if h == 0 else []),
                                inc=([sV.inc()] if h == 1 else []))
                    gpd[it] = sV.v
                    if it >= 1:
                        for h in range(2):
                            sch.add("tensor", "matmul", psU[h],
                                    a2b[:, 2 * pmc[it - 1]:
                                        2 * pmc[it - 1] + 2],
                                    gbf[(it - 1) % 2]
                                    [:, h * 512:(h + 1) * 512],
                                    start=(it == 1), stop=False,
                                    w=([(sV.sem, gpd[it - 1])]
                                       if h == 0 else []),
                                    inc=([sT.inc()] if h == 1 else []))
                        a2d[it - 1] = sT.v
                    pmc[it] = mc
                    if qt == 0:
                        sch.add("vector", "tensor_copy", tf2[:, :],
                                gbf[gp][:, :])
                    elif qt < 3:
                        sch.add("vector", "tensor_add", tf2[:, :], tf2[:, :],
                                gbf[gp][:, :])
                    else:
                        sch.add("vector", "tensor_add",
                                gf[:, c * 1024:(c + 1) * 1024], tf2[:, :],
                                gbf[gp][:, :], inc=[sV.inc()])
            last_it = KC_H * 4 - 1
            for h in range(2):
                sch.add("tensor", "matmul", psU[h],
                        a2b[:, 2 * pmc[last_it]:2 * pmc[last_it] + 2],
                        gbf[last_it % 2][:, h * 512:(h + 1) * 512],
                        start=False, stop=True,
                        w=([(sV.sem, gpd[last_it])] if h == 0 else []),
                        inc=([sT.inc()] if h == 1 else []))
            sch.add("scalar", "mul", u2_bf[:, 0:512], psU[0], 0.5,
                    w=[(sT.sem, sT.v)])
            sch.add("scalar", "mul", u2_bf[:, 512:1024], psU[1], 0.5,
                    inc=[sA.inc()])
            seal(sch)
            sch.emit(nc)
            esp2.close()

            if dump_after_p2:
                sch = Sched()
                Sched.TAG[0] = "DBGG"
                phase_wait(sch)
                rfd = tf2
                row = 0
                for c in range(KC_F):
                    sch.add("vector", "tensor_copy", rfd[:, :],
                            xfb[:, c * 1024:(c + 1) * 1024],
                            w=([] if row == 0 else [SA_[0].at()]),
                            inc=[sV.inc()])
                    sch.add("sync", "dma_start",
                            out=outT[row * 128:(row + 1) * 128, :],
                            in_=rfd[:, :], w=[sV.at()], inc=[SA_[0].inc()])
                    row += 1
                for c in range(KC_H):
                    sch.add("vector", "tensor_copy", rfd[:, :],
                            gf[:, c * 1024:(c + 1) * 1024],
                            w=[SA_[0].at()], inc=[sV.inc()])
                    sch.add("sync", "dma_start",
                            out=outT[row * 128:(row + 1) * 128, :],
                            in_=rfd[:, :], w=[sV.at()], inc=[SA_[0].inc()])
                    row += 1
                sch.add("vector", "memset", rfd[:, :], 0.0,
                        w=[SA_[0].at()], inc=[sV.inc()])
                for c in range(row, KC_D):
                    sch.add("sync", "dma_start",
                            out=outT[c * 128:(c + 1) * 128, :],
                            in_=rfd[:, :],
                            w=[sV.at(), SA_[0].at()], inc=[SA_[0].inc()])
                sch.add("sync", "wait_ge", *SA_[0].at())
                seal(sch)
                sch.emit(nc)
                return

            with ExitStack() as es3:
                sb3 = lambda n, s, d: es3.enter_context(
                    nc.sbuf_tensor(n + tag, s, d, side="right"))
                w2f = [sb3(f"w2f{i}", [128, KC_H * 128], F32)
                       for i in range(2)]
                w2b = [sb3(f"w2b{i}", [128, KC_H * 128], BF16)
                       for i in range(2)]
                rst3 = [sb3(f"rst3{i}", [128, 1024], F32) for i in range(2)]
                r1c = [sb3(f"r1c{i}", [128, 1024], F32) for i in range(2)]
                xb3 = [sb3(f"xb3{i}", [128, 1024], BF16) for i in range(2)]
                sq3 = [sb3(f"sq3{i}", [128, 1024], BF16) for i in range(2)]
                sch = Sched()
                Sched.TAG[0] = "P5"
                phase_wait(sch)
                wps = [[PS[0], PS[1]], [PS[2], PS[3]]]
                stx = [PS[4][0:2, :], PS[5][0:2, :]]
                stq = [PS[6][0:2, :], PS[7][0:2, :]]
                wcv, mmd, evd, std2, sqd2, cvd2, sqa2, stod = (
                    {} for _ in range(8))
                for mc in range(KC_D):
                    p = mc % 2
                    wl = [] if mc < 2 else [(sA.sem, wcv[mc - 2])]
                    sch.add("gpsimd", "dma_start", out=w2f[p][:, :],
                            in_=w2r[mc], w=wl, inc=[W1[p].inc()])
                    wl = [W1[p].at()]
                    if mc >= 2:
                        wl.append((sT.sem, mmd[mc - 2]))
                    sch.add("scalar", "copy", w2b[p][:, :], w2f[p][:, :],
                            w=wl, inc=[sA.inc()])
                    wcv[mc] = sA.v
                    wl = [] if mc < 2 else [(sV.sem, evd[mc - 2])]
                    sch.add("sync", "dma_start", out=rst3[p][:, :],
                            in_=r0_src[mc * 128:(mc + 1) * 128, :], w=wl,
                            inc=[LA[p].inc()])
                    r0ld = LA[p].at()
                    pa = wps[p]
                    wl = [(sA.sem, wcv[mc])]
                    if mc >= 2:
                        wl.append((sV.sem, evd[mc - 2]))
                    for h in range(2):
                        for kc in range(KC_H):
                            sch.add("tensor", "matmul", pa[h][:, :],
                                    w2b[p][:, kc * 128:(kc + 1) * 128],
                                    gf[:, kc * 1024 + h * 512:
                                        kc * 1024 + (h + 1) * 512],
                                    start=(kc == 0), stop=False,
                                    w=(wl if (h == 0 and kc == 0) else []))
                        sch.add("tensor", "matmul", pa[h][:, :],
                                b2b[:, mc * 128:(mc + 1) * 128],
                                u2_bf[:, h * 512:(h + 1) * 512],
                                start=False, stop=True,
                                inc=([sT.inc()] if h == 1 else []))
                    mmd[mc] = sT.v
                    for h in range(2):
                        wl4 = []
                        if h == 0:
                            wl4 = [(sT.sem, mmd[mc]), r0ld]
                            if mc >= 2:
                                wl4.append(stod[mc - 2])
                                if not is_final:
                                    wl4.append((sV.sem, cvd2[mc - 2]))
                        sch.add("vector", "scalar_tensor_tensor",
                                r1c[p][:, h * 512:(h + 1) * 512],
                                pa[h][:, :], 0.5,
                                rst3[p][:, h * 512:(h + 1) * 512],
                                ALU.mult, ALU.add, w=wl4,
                                inc=([sV.inc()] if h == 1 else []))
                    evd[mc] = sV.v
                    sch.add("sync", "dma_start",
                            out=rdst[mc * 128:(mc + 1) * 128, :],
                            in_=r1c[p][:, :], w=[(sV.sem, evd[mc])],
                            inc=[SA_[p].inc()])
                    stod[mc] = SA_[p].at()
                    if not is_final:
                        wl = [(sV.sem, evd[mc])]
                        if mc >= 2:
                            wl += [(sT.sem, std2.get(mc - 2, 0)),
                                   (sT.sem, sqd2.get(mc - 2, 0))]
                        sch.add("vector", "tensor_copy", xb3[p][:, :],
                                r1c[p][:, :], w=wl, inc=[sV.inc()])
                        cvd2[mc] = sV.v
                        sch.add("vector", "tensor_mul", sq3[p][:, :],
                                xb3[p][:, :], xb3[p][:, :],
                                inc=[sV.inc()], lp=True)
                        sqa2[mc] = sV.v
                        for mcx in ([mc - 1] if mc >= 1 else []) + (
                                [mc] if mc == KC_D - 1 else []):
                            px = mcx % 2
                            for i in range(2):
                                sch.add("tensor", "matmul", stx[i],
                                        onesc[:, :],
                                        xb3[px][:, i * 512:(i + 1) * 512],
                                        start=(mcx == 0),
                                        stop=(mcx == KC_D - 1),
                                        w=([(sV.sem, cvd2[mcx])] if i == 0
                                           else []),
                                        inc=([sT.inc()] if i == 1 else []))
                            std2[mcx] = sT.v
                            for i in range(2):
                                sch.add("tensor", "matmul", stq[i],
                                        onesc[:, :],
                                        sq3[px][:, i * 512:(i + 1) * 512],
                                        start=(mcx == 0),
                                        stop=(mcx == KC_D - 1),
                                        w=([(sV.sem, sqa2[mcx])] if i == 0
                                           else []),
                                        inc=([sT.inc()] if i == 1 else []))
                            sqd2[mcx] = sT.v
                if not is_final:
                    ln_post(sch, stx, stq, eps=1e-5, cmul=-1.0,
                            pb0=PS[0], pb1=PS[1], stats_done=(sT.sem, sT.v),
                            first_w=[(sV.sem, sV.v)])
                seal(sch)
                sch.emit(nc)

    ffn_phase(xT, r1d, is_final=False, tag="f1")

    def debug_copy_out(srcd):
        with ExitStack() as es:
            rr = es.enter_context(nc.sbuf_tensor("rrcopy", [128, 1024], F32,
                                                 side="right"))
            sch = Sched()
            Sched.TAG[0] = "DBG"
            phase_wait(sch)
            for c in range(KC_D):
                sch.add("sync", "dma_start", out=rr[:, :],
                        in_=srcd[c * 128:(c + 1) * 128, :],
                        w=([] if c == 0 else [SA_[0].at()]),
                        inc=[sSet.inc()])
                sch.add("sync", "dma_start",
                        out=outT[c * 128:(c + 1) * 128, :], in_=rr[:, :],
                        w=[sSet.at()], inc=[SA_[0].inc()])
            sch.add("sync", "wait_ge", *SA_[0].at())
            seal(sch)
            sch.emit(nc)

    def debug_dump_bf16(srcd_view):
        with ExitStack() as es:
            rb = es.enter_context(nc.sbuf_tensor("rdbg_b", [128, 1024], BF16,
                                                 side="right"))
            rf = es.enter_context(nc.sbuf_tensor("rdbg_f", [128, 1024], F32,
                                                 side="right"))
            sch = Sched()
            Sched.TAG[0] = "DBGB"
            phase_wait(sch)
            for c in range(KC_D):
                sch.add("sync", "dma_start", out=rb[:, :],
                        in_=srcd_view[c * 128:(c + 1) * 128, :],
                        w=([] if c == 0 else [SA_[0].at()]),
                        inc=[sSet.inc()])
                sch.add("vector", "tensor_copy", rf[:, :], rb[:, :],
                        w=[sSet.at()], inc=[sV.inc()])
                sch.add("sync", "dma_start",
                        out=outT[c * 128:(c + 1) * 128, :], in_=rf[:, :],
                        w=[sV.at()], inc=[SA_[0].inc()])
            sch.add("sync", "wait_ge", *SA_[0].at())
            seal(sch)
            sch.emit(nc)

    if stop_stage == 1:
        debug_copy_out(r1d)
        return nc

    with ExitStack() as es4:
        sb4 = lambda n, s, d: es4.enter_context(
            nc.sbuf_tensor(n, s, d, side="right"))

        with ExitStack() as es:
            sb = lambda n, s, d: es.enter_context(
                nc.sbuf_tensor(n, s, d, side="right"))
            xln = sb("xln", [128, KC_D * 1024], BF16)
            wfA = [sb(f"wfA{i}", [128, 2048], F32) for i in range(2)]
            wbA = [sb(f"wbA{i}", [128, 2048], BF16) for i in range(2)]
            qet = [sb(f"qet{i}", [128, 1024], BF16) for i in range(2)]
            q2t = [sb(f"q2t{i}", [128, 1024], BF16) for i in range(2)]
            vtb = sb("vtb", [128, 1024], BF16)
            vtm = [sb(f"vtm{i}", [128, 1024], BF16)
                   for i in range(2)]
            xtp = sb("xtp", [128, 1024], F32)
            rst4 = [sb(f"rst4{i}", [128, 1024], F32) for i in range(2)]
            sch = Sched()
            Sched.TAG[0] = "P7"
            phase_wait(sch)
            xld = {}
            for c in range(KC_D):
                p = c % 2
                wl = [] if c < 2 else [(sV.sem, xld[c - 2])]
                sch.add("sync", "dma_start", out=rst4[p][:, :],
                        in_=r1d[c * 128:(c + 1) * 128, :], w=wl,
                        inc=[LA[p].inc()])
                ld = LA[p].at()
                sch.add("vector", "tensor_mul", xtp[:, :], rst4[p][:, :],
                        bcA[:, :], w=[ld])
                sch.add("vector", "tensor_add",
                        xln[:, c * 1024:(c + 1) * 1024], xtp[:, :],
                        bcC[:, :], inc=[sV.inc()])
                xld[c] = sV.v
            xall = sV.at()
            psQ = [[PS[0], PS[1]], [PS[2], PS[3]]]
            psRM = [PS[4][0:40, :], PS[5][0:40, :]]
            psV = PS[6][:, :].bitcast(BF16)
            wcv = {}
            mmh = {}
            evd = {}
            rmd = {}
            vst = {}
            qsd = {}
            for m in range(48):
                p = m % 2
                src_w = wqr if m < 32 else (wkr if m < 40 else wvr)
                hh = m if m < 32 else (m - 32 if m < 40 else m - 40)
                pa = psQ[p]
                for i in range(2):
                    wl = []
                    if i == 1:
                        wl = [(sV.sem, wcv[(m, 0)])]
                    elif m >= 2:
                        wl = [(sV.sem, wcv[(m - 2, 1)])]
                    sch.add("gpsimd", "dma_start", out=wfA[p][:, :],
                            in_=src_w[hh][:, i * 16:(i + 1) * 16, :],
                            w=wl, inc=[W1[p].inc()])
                    w2l = [W1[p].at()]
                    if i == 1:
                        w2l.append((sT.sem, mmh[(m, 0)]))
                    elif m >= 2:
                        w2l.append((sT.sem, mmh[(m - 2, 1)]))
                    sch.add("vector", "tensor_copy", wbA[p][:, :],
                            wfA[p][:, :], w=w2l, inc=[sV.inc()])
                    wcv[(m, i)] = sV.v
                    wl3 = [(sV.sem, wcv[(m, i)])]
                    if i == 0:
                        if m == 0:
                            wl3.append(xall)
                        if m >= 2:
                            wl3.append((sA.sem, evd[m - 2]))
                    for h in range(2):
                        for kk in range(16):
                            kc = i * 16 + kk
                            sch.add("tensor", "matmul", pa[h][:, :],
                                    wbA[p][:, kk * 128:(kk + 1) * 128],
                                    xln[:, kc * 1024 + h * 512:
                                        kc * 1024 + (h + 1) * 512],
                                    start=(kc == 0), stop=(kc == KC_D - 1),
                                    w=(wl3 if (h == 0 and kk == 0) else []),
                                    inc=([sT.inc()]
                                         if (h == 1 and kk == 15) else []))
                    mmh[(m, i)] = sT.v
                if m < 40:
                    for h in range(2):
                        wl = []
                        if h == 0:
                            wl = [(sT.sem, mmh[(m, 1)])]
                            if m >= 2:
                                wl += [qsd[m - 2],
                                       (sT.sem, rmd.get(m - 2, 0))]
                        sch.add("scalar", "copy",
                                qet[p][:, h * 512:(h + 1) * 512],
                                pa[h][:, :], w=wl)
                        sch.add("scalar", "activation",
                                q2t[p][:, h * 512:(h + 1) * 512],
                                pa[h][:, :], AF.Square,
                                inc=([sA.inc()] if h == 1 else []))
                    evd[m] = sA.v
                    sch.add("sync", "dma_start",
                            out=qd[m * 128:(m + 1) * 128, :],
                            in_=qet[p][:, :], w=[(sA.sem, evd[m])],
                            inc=[QS[p].inc()])
                    qsd[m] = QS[p].at()
                    for mx in ([m - 1] if 1 <= m else []) + (
                            [m] if m == 39 else []):
                        if mx >= 40 or mx < 0:
                            continue
                        px = mx % 2
                        for h in range(2):
                            sch.add("tensor", "matmul", psRM[h],
                                    W40[:, 39 - mx:79 - mx],
                                    q2t[px][:, h * 512:(h + 1) * 512],
                                    start=(mx == 0), stop=(mx == 39),
                                    w=([(sA.sem, evd[mx])] if h == 0
                                       else []),
                                    inc=([sT.inc()] if h == 1 else []))
                        rmd[mx] = sT.v
                else:
                    hv = m - 40
                    for h in range(2):
                        wl = []
                        if h == 0:
                            wl = [(sT.sem, mmh[(m, 1)])]
                            if m > 40:
                                wl.append((sT.sem, vst[m - 1]))
                        sch.add("scalar", "copy",
                                vtb[:, h * 512:(h + 1) * 512], pa[h][:, :],
                                w=wl, inc=([sA.inc()] if h == 1 else []))
                    evd[m] = sA.v
                    for l in range(8):
                        wl = []
                        if l == 0:
                            wl = [(sA.sem, evd[m])]
                            if m > 40:
                                wl.append((sA.sem, vcp[m - 1]))
                        sch.add("tensor", "transpose",
                                psV[:, l * 128:(l + 1) * 128],
                                vtb[:, l * 128:(l + 1) * 128],
                                iden_bf[:, :], w=wl,
                                inc=([sT.inc()] if l == 7 else []))
                    vst[m] = sT.v
                    wl = [(sT.sem, vst[m])]
                    if m > 41:
                        wl.append(vkd[m - 2])
                    sch.add("scalar", "copy", vtm[m % 2][:, :], psV[:, :],
                            w=wl, inc=[sA.inc()])
                    vcp = vcp if m > 40 else {}
                    vcp[m] = sA.v
                    sch.add("sync", "dma_start",
                            out=kv_in.rearrange(
                                "(q kvh l t) c -> q kvh t l c",
                                q=2, kvh=8, l=8)[1, hv],
                            in_=vtm[m % 2][:, :], w=[(sA.sem, vcp[m])],
                            inc=[KVV.inc()])
                    vkd = vkd if m > 40 else {}
                    vkd[m] = KVV.at()
            seal(sch)
            sch.emit(nc)

        if stop_stage == 21:
            debug_dump_bf16(qd[0:4096, :])
            return nc

        rstdr_bf = sb4("rstdr_bf", [40, 1024], BF16)
        osb = sb4("osb", [128, NQ * 1024], BF16)
        sel4 = sb4("sel4", [40, 5120], BF16)
        mskb = sb4("mskb", [128, 2048], BF16)
        with ExitStack() as es:
            sb = lambda n, s, d: es.enter_context(
                nc.sbuf_tensor(n, s, d, side="right"))
            rmsm = sb("rmsm", [40, 1024], F32)
            ket = [sb(f"ket{i}", [128, 1024], BF16) for i in range(2)]
            knt = [sb(f"knt{i}", [128, 1024], BF16) for i in range(2)]
            bcs = [sb(f"bcs{i}", [128, 1024], F32) for i in range(2)]
            self4 = sb("self4", [40, 5120], F32)
            sch = Sched()
            Sched.TAG[0] = "P8"
            phase_wait(sch)
            sch.add("sync", "dma_start", out=self4[:, :], in_=sel40D,
                    inc=[sSet.inc()])
            for a in range(2):
                sch.add("sync", "dma_start", out=bcs[a][:, :],
                        in_=masksD[:, a * 1024:(a + 1) * 1024],
                        inc=[sSet.inc()])
            mld1 = sSet.at()
            sch.add("vector", "tensor_copy", sel4[:, :], self4[:, :],
                    w=[sSet.at()], inc=[sV.inc()])
            for a in range(2):
                sch.add("vector", "tensor_copy",
                        mskb[:, a * 1024:(a + 1) * 1024], bcs[a][:, :],
                        w=[mld1], inc=[sV.inc()])
            mcv1 = sV.at()
            for i in range(2):
                sch.add("scalar", "mul", rmsm[:, i * 512:(i + 1) * 512],
                        psRM[i], 1.0 / 128.0,
                        inc=([sA.inc()] if i == 1 else []))
            sch.add("scalar", "activation", rmsm[:, :], rmsm[:, :], AF.Ln,
                    bias=epsc[0:40, 1:2])
            sch.add("scalar", "activation", rmsm[:, :], rmsm[:, :], AF.Exp,
                    scale=-0.5, inc=[sA.inc()])
            rk = sA.at()
            sch.add("vector", "tensor_copy", rstdr_bf[:, :], rmsm[:, :],
                    w=[rk], inc=[sV.inc()])
            rbf = sV.at()
            bcd = {}
            knd = {}
            ksd = {}
            for h in list(range(32, 40)) + list(range(32)):
                p = h % 2
                hseq = (h - 32) if h >= 32 else (h + 8)
                wl = [rbf] if hseq < 2 else [(sV.sem, knd[hseq - 2])]
                sch.add("sync", "dma_start", out=ket[p][:, :],
                        in_=qd[h * 128:(h + 1) * 128, :], w=wl,
                        inc=[QA[p].inc()])
                kld = QA[p].at()
                for i in range(2):
                    wl = []
                    if i == 0:
                        wl = [rbf] if hseq == 0 else [
                            (sA.sem, bcd[hseq - 1])]
                        if hseq >= 2:
                            wl.append((sV.sem, knd[hseq - 2]))
                    sch.add("tensor", "matmul", PS[0 + i][:, :],
                            sel4[:, h * 128:(h + 1) * 128],
                            rstdr_bf[:, i * 512:(i + 1) * 512],
                            start=True, stop=True, w=wl,
                            inc=([sT.inc()] if i == 1 else []))
                bmm = sT.v
                for i in range(2):
                    sch.add("scalar", "copy",
                            bcs[p][:, i * 512:(i + 1) * 512],
                            PS[0 + i][:, :],
                            w=([(sT.sem, bmm), mcv1] if i == 0 else []),
                            inc=([sA.inc()] if i == 1 else []))
                bcd[hseq] = sA.v
                wv = qw if h < 32 else kw
                sch.add("vector", "scalar_tensor_tensor", knt[p][:, :],
                        ket[p][:, :], wv[:, :], bcs[p][:, :],
                        ALU.mult, ALU.mult, w=[kld, (sA.sem, bcd[hseq])],
                        inc=[sV.inc()])
                knd[hseq] = sV.v
                if h < 32:
                    sch.add("sync", "dma_start",
                            out=qd[h * 128:(h + 1) * 128, :],
                            in_=knt[p][:, :], w=[(sV.sem, knd[hseq])],
                            inc=[QS[p].inc()])
                else:
                    sch.add("gpsimd", "dma_start",
                            out=kv_in.rearrange("(q hh l) c -> q hh l c",
                                                q=2, l=8)
                            [0, (h - 32) * 128:(h - 31) * 128],
                            in_=knt[p][:, :], w=[(sV.sem, knd[hseq])],
                            inc=[KV[p].inc()])
                    ksd[h] = KV[p].at()
                if h == 39:
                    sch.add("gpsimd", "collective_compute", "AllGather",
                            ALU.bypass,
                            replica_groups=[[0, 1], [2, 3], [4, 5], [6, 7]],
                            ins=[kv_in.opt()], outs=[kv_out.opt()],
                            w=[KV[0].at(), KV[1].at(), KVV.at()],
                            inc=[sCC.inc()])
            seal(sch)
            sch.emit(nc)

        if stop_stage == 22:
            debug_dump_bf16(
                kv_out.rearrange("(a b) c -> a (b c)", b=8)[0:4096, :])
            return nc

        with ExitStack() as es:
            sb = lambda n, s, d: es.enter_context(
                nc.sbuf_tensor(n, s, d, side="right"))
            kfl = [sb(f"kfl{i}", [128, 2048], BF16) for i in range(2)]
            vfl = [sb(f"vfl{i}", [128, 2048], BF16) for i in range(2)]
            pTA = sb("pTA", [128, 8 * 512], BF16)
            pTB = sb("pTB", [128, 16 * 512], BF16)
            qnb = [sb(f"qnb{i}", [128, 1024], BF16) for i in range(2)]
            acol = [sb(f"acol{i}", [128, 16], F32) for i in range(2)]
            ngr = sb("ngr", [2, 2048], BF16)
            rcb = sb("rcb", [2, 1024], BF16)
            rtmp = sb("rtmp", [2, 1024], F32)
            bco = [sb(f"bco{i}", [128, 512], F32) for i in range(2)]
            q2b = [sb(f"q2b{i}", [128, 1024], BF16) for i in range(2)]
            msqA = sb("msqA", [2, 1024], BF16)
            msqB = sb("msqB", [2, 1024], BF16)
            aqr = sb("aqr", [2, 1024], BF16)
            kwb = sb("kwb", [128, 2], BF16)
            c2m = sb("c2m", [2, 2], BF16)
            e10 = sb("e10", [2, 2], BF16)
            onesC = sb("onesC", [128, 2], BF16)
            ktmp = msqA[:, 0:128]
            c2rs = rcb[:, 0:128]
            sch = Sched()
            Sched.TAG[0] = "P9"
            phase_wait(sch)
            sch.add("vector", "memset", pTA[:, :], 0.0)
            sch.add("vector", "memset", pTB[:, :], 0.0)
            sch.add("vector", "memset", kwb[:, :], 0.0)
            sch.add("vector", "memset", c2m[:, :], 0.0)
            sch.add("vector", "memset", e10[:, :], 0.0)
            sch.add("vector", "memset", e10[0:1, 0:2], 1.0,
                    inc=[sV.inc()])
            sch.add("scalar", "activation", kwb[:, 0:1], kw[:, 0:1],
                    AF.Square, w=[(sV.sem, sV.v)], inc=[sA.inc()], lp=True)
            sch.add("tensor", "transpose",
                    PS[2][:, :].bitcast(BF16)[0:2, 0:128], kwb[:, :],
                    iden_bf[:, :], w=[(sA.sem, sA.v)], inc=[sT.inc()])
            sch.add("scalar", "copy", ktmp,
                    PS[2][:, :].bitcast(BF16)[0:2, 0:128],
                    w=[(sT.sem, sT.v)], inc=[sA.inc()], lp=True)
            ktmp_v = sA.v
            sch.add("vector", "reduce_max", c2m[:, 0:1], ktmp,
                    axis=AX.X, w=[(sA.sem, sA.v)], inc=[sV.inc()], lp=True)
            sch.add("tensor", "matmul", PS[0][0:2, 0:128], c2m[:, :],
                    ones2[:, 0:128], start=True, stop=True,
                    w=[(sV.sem, sV.v)], inc=[sT.inc()])
            sch.add("scalar", "copy", c2rs, PS[0][0:2, 0:128],
                    w=[(sT.sem, sT.v)], inc=[sA.inc()], lp=True)
            sch.add("tensor", "matmul", PS[1][:, 0:2], c2rs,
                    e10[:, :], start=True, stop=True,
                    w=[(sA.sem, sA.v)], inc=[sT.inc()])
            cb0 = sA.v
            sch.add("scalar", "copy", onesC[:, :], PS[1][:, 0:2],
                    w=[(sT.sem, sT.v)], inc=[sA.inc()], lp=True)
            cb1 = sA.v
            kvd = {}
            kvuse = {}
            qld = {}
            acd = {}
            p2use = [0, 0]
            ngruse = {}
            mbank_use = [cb0, cb1]
            q2use = [0, 0]
            sumuse = [0]
            ouse = {6: (sV.sem, 0), 2: (sA.sem, ktmp_v)}
            bc7use = [0]
            bcouse = [0, 0]
            pTuse = {0: 0, 1: 0}
            slopes = _alibi_slopes(NQ)
            NKV_RUN = 1 if stop_stage == 24 else NKV
            NG_RUN = 1 if stop_stage == 24 else 4
            GRPS = 1 if stop_stage == 24 else 2

            AVB = {0: 6, 1: 2}

            def s2_sr(hq, ph, pk, grp, ngr_v):
                pT = pTA if grp == 0 else pTB
                kcm = 8 * grp + 8
                LEAD = 2
                evs = []

                def red(kc):
                    lmin = max(4 * grp, kc // 2)
                    qs = (lmin - 4 * grp) * 128
                    wl = [(sA.sem, evs[kc])]
                    if kc == 0:
                        wl += [(sA.sem, sumuse[0]),
                               ouse[AVB[grp]], kvd[hq // 4]]
                    sch.add("tensor", "matmul", PS[5][0:2, qs:512],
                            onesc[:, :],
                            pT[:, kc * 512 + qs:(kc + 1) * 512],
                            start=(kc == 0), stop=(kc == kcm - 1), w=wl)
                    sch.add("tensor", "matmul", PS[AVB[grp]][:, :],
                            vfl[pk][:, kc * 128:(kc + 1) * 128],
                            pT[:, kc * 512:(kc + 1) * 512],
                            start=(kc == 0), stop=(kc == kcm - 1),
                            inc=([sT.inc()] if kc == kcm - 1 else []))

                for kc in range(kcm):
                    lmin = max(4 * grp, kc // 2)
                    qs = (lmin - 4 * grp) * 128
                    bk = kc % 2
                    p2b = PS[3 + bk]
                    wl = [(sA.sem, p2use[bk])]
                    if kc == 0:
                        wl += [(sV.sem, ngr_v), qld[hq], kvd[hq // 4]]
                    sl = (ph * 2 + grp) * 512
                    sch.add("tensor", "matmul", p2b[:, qs:512],
                            ones2n[:, :], ngr[:, sl + qs:sl + 512],
                            start=True, stop=False, w=wl)
                    sch.add("tensor", "matmul", p2b[:, qs:512],
                            kfl[pk][:, kc * 128:(kc + 1) * 128],
                            qnb[ph][:, grp * 512 + qs:grp * 512 + 512],
                            start=False, stop=True, inc=[sT.inc()])
                    smm = sT.v
                    need_mask = (kc >= 8 * grp)
                    exp_w = [(sT.sem, smm)]
                    if need_mask:
                        sch.add("vector", "tensor_add",
                                p2b[:, qs:qs + 128], p2b[:, qs:qs + 128],
                                mskb[:, kc * 128:(kc + 1) * 128],
                                w=[(sT.sem, smm)], inc=[sV.inc()])
                        exp_w = [(sV.sem, sV.v)]
                    wl = exp_w + [(sV.sem, acd[hq])]
                    if pTuse[grp]:
                        wl.append((sT.sem, pTuse[grp]))
                    sch.add("scalar", "activation",
                            pT[:, kc * 512 + qs:(kc + 1) * 512],
                            p2b[:, qs:512], AF.Exp,
                            bias=acol[ph][:, kc:kc + 1], w=wl,
                            inc=[sA.inc()])
                    p2use[bk] = sA.v
                    evs.append(sA.v)
                    if kc >= LEAD:
                        red(kc - LEAD)
                ngruse[(ph, grp)] = sT.v
                for kc in range(kcm - LEAD, kcm):
                    red(kc)
                avd = sT.v
                pTuse[grp] = avd
                kvuse[hq // 4] = avd
                sch.add("scalar", "activation",
                        rtmp[:, grp * 512:(grp + 1) * 512], PS[5][0:2, :],
                        AF.Ln, w=[(sT.sem, avd)], inc=[sA.inc()])
                sumuse[0] = sA.v
                sch.add("scalar", "activation",
                        rcb[:, grp * 512:(grp + 1) * 512],
                        rtmp[:, grp * 512:(grp + 1) * 512], AF.Exp,
                        scale=-1.0, inc=[sA.inc()], lp=True)
                return avd, sA.v

            def s2_evac(hq, grp, rc_v):
                sch.add("tensor", "matmul", PS[7][:, :], ones2[:, :],
                        rcb[:, grp * 512:(grp + 1) * 512], start=True,
                        stop=True,
                        w=[(sA.sem, rc_v), (sV.sem, bcouse[grp]),
                           (sA.sem, bc7use[0])],
                        inc=[sT.inc()])
                bc_mm = sT.v
                sch.add("scalar", "copy", bco[grp][:, :], PS[7][:, :],
                        w=[(sT.sem, bc_mm)], inc=[sA.inc()])
                bcod = sA.v
                bc7use[0] = sA.v
                sch.add("vector", "tensor_mul",
                        osb[:, hq * 1024 + grp * 512:
                            hq * 1024 + (grp + 1) * 512],
                        PS[AVB[grp]][:, :], bco[grp][:, :],
                        w=[(sA.sem, bcod)], inc=[sV.inc()])
                ouse[AVB[grp]] = (sV.sem, sV.v)
                return sV.v

            NH = NKV_RUN * NG_RUN

            def kv_load(kv):
                pk = kv % 2
                wl = [(sCC.sem, 1)]
                if kv >= 2:
                    wl.append((sT.sem, kvuse[kv - 2]))
                for par in range(2):
                    sch.add("gpsimd", "dma_start",
                        out=kfl[pk][:, :].rearrange(
                            "p (l q c) -> p l q c", l=8, q=2)[:, :, par, :],
                        in_=kv_out.rearrange(
                            "(q hh l) c -> q hh l c", q=2, l=8)
                        [par, kv * 128:(kv + 1) * 128],
                        w=(wl if par == 0 else []), inc=[KF[pk].inc()])
                    sch.add("gpsimd", "dma_start",
                        out=vfl[pk][:, :].rearrange(
                            "p (l q c) -> p l q c", l=8, q=2)[:, :, par, :],
                        in_=kv_out.rearrange(
                            "(q x kvh l t) c -> q x kvh t l c",
                            q=2, x=2, kvh=8, l=8)[par, 1, kv],
                        inc=[KF[pk].inc()])
                kvd[kv] = KF[pk].at()

            def prep_head(hq):
                ph = hq % 2
                pk = (hq // 4) % 2
                if hq % 4 == 0:
                    kv_load(hq // 4)
                wl = []
                if hq >= 2:
                    wl.append((sT.sem, pTuse[0]))
                    wl.append((sT.sem, pTuse[1]))
                sch.add("sync", "dma_start", out=qnb[ph][:, :],
                        in_=qd[hq * 128:(hq + 1) * 128, :], w=wl,
                        inc=[QA[ph].inc()])
                qld[hq] = QA[ph].at()
                wl = []
                if hq >= 2:
                    wl += [(sA.sem, p2use[0]), (sA.sem, p2use[1])]
                sch.add("vector", "tensor_scalar_mul", acol[ph][:, :],
                        iotac[:, :], float(slopes[hq]), w=wl,
                        inc=[sV.inc()])
                acd[hq] = sV.v
                sch.add("vector", "tensor_scalar_mul", aqr[:, :],
                        qposs[:, :], float(slopes[hq]), lp=True)
                sch.add("scalar", "activation", q2b[ph][:, :],
                        qnb[ph][:, :], AF.Square,
                        w=[qld[hq], (sT.sem, q2use[ph])],
                        inc=[sA.inc()], lp=True)
                sq_v = sA.v
                ngv = []
                for grp in range(GRPS):
                    sl = (ph * 2 + grp) * 512
                    sch.add("tensor", "matmul", PS[grp][0:2, 0:512],
                            onesC[:, :],
                            q2b[ph][:, grp * 512:(grp + 1) * 512],
                            start=True, stop=True,
                            w=[(sA.sem, sq_v), (sA.sem, mbank_use[grp])],
                            inc=[sT.inc()])
                    q2use[ph] = sT.v
                    sch.add("scalar", "activation",
                            msqA[:, grp * 512:(grp + 1) * 512],
                            PS[grp][0:2, 0:512], AF.Ln, scale=133.12,
                            w=[(sT.sem, sT.v)], inc=[sA.inc()], lp=True)
                    mbank_use[grp] = sA.v
                    sch.add("scalar", "activation",
                            msqB[:, grp * 512:(grp + 1) * 512],
                            msqA[:, grp * 512:(grp + 1) * 512], AF.Exp,
                            scale=0.5, inc=[sA.inc()], lp=True)
                    sch.add("vector", "tensor_add", ngr[:, sl:sl + 512],
                            msqB[:, grp * 512:(grp + 1) * 512],
                            aqr[:, grp * 512:(grp + 1) * 512],
                            w=[(sA.sem, sA.v),
                               (sT.sem, ngruse.get((ph, grp), 0))],
                            inc=[sV.inc()], lp=True)
                    ngv.append(sV.v)
                return ngv

            ngs_q = {0: prep_head(0)}
            e1_pend = None
            for hq in range(NH):
                ph = hq % 2
                pk = (hq // 4) % 2
                if e1_pend is not None:
                    bcouse[1] = s2_evac(e1_pend[0], 1, e1_pend[1])
                    e1_pend = None
                if hq + 1 < NH:
                    ngs_q[hq + 1] = prep_head(hq + 1)
                ngs = ngs_q.pop(hq)
                if GRPS == 1:
                    avda, rca = s2_sr(hq, ph, pk, 0, ngs[0])
                    bcouse[0] = s2_evac(hq, 0, rca)
                else:
                    avda, rca = s2_sr(hq, ph, pk, 0, ngs[0])
                    avdb, rcb_v = s2_sr(hq, ph, pk, 1, ngs[1])
                    bcouse[0] = s2_evac(hq, 0, rca)
                    e1_pend = (hq, rcb_v)
            if e1_pend is not None:
                bcouse[1] = s2_evac(e1_pend[0], 1, e1_pend[1])
            seal(sch)
            sch.emit(nc)

        if stop_stage == 23:
            with ExitStack() as es:
                rf = es.enter_context(nc.sbuf_tensor(
                    "rdbg_f2", [128, 1024], F32, side="right"))
                sch = Sched()
                Sched.TAG[0] = "DBGO"
                phase_wait(sch)
                for c in range(KC_D):
                    sch.add("vector", "tensor_copy", rf[:, :],
                            osb[:, c * 1024:(c + 1) * 1024],
                            w=([] if c == 0 else [SA_[0].at()]),
                            inc=[sV.inc()])
                    sch.add("sync", "dma_start",
                            out=outT[c * 128:(c + 1) * 128, :], in_=rf[:, :],
                            w=[sV.at()], inc=[SA_[0].inc()])
                sch.add("sync", "wait_ge", *SA_[0].at())
                seal(sch)
                sch.emit(nc)
            return nc

        with ExitStack() as es:
            sb = lambda n, s, d: es.enter_context(
                nc.sbuf_tensor(n, s, d, side="right"))
            asb4 = sb("asb4", [66, 1024], F32)
            tmp4 = sb("tmp4", [66, 1024], F32)
            wfO = [sb(f"wfO{i}", [128, 2048], F32) for i in range(2)]
            wbO = [sb(f"wbO{i}", [128, 2048], BF16) for i in range(2)]
            rst5 = [sb(f"rst5{i}", [128, 1024], F32) for i in range(2)]
            r2c = [sb(f"r2c{i}", [128, 1024], F32) for i in range(2)]
            xb5 = [sb(f"xb5{i}", [128, 1024], BF16) for i in range(2)]
            sq5 = [sb(f"sq5{i}", [128, 1024], BF16) for i in range(2)]
            sch = Sched()
            Sched.TAG[0] = "P10"
            phase_wait(sch)
            wps = [[PS[0], PS[1]], [PS[2], PS[3]]]
            sa_ps = [PS[4][0:66, :], PS[5][0:66, :]]
            sq_ps = [PS[6][0:2, :], PS[7][0:2, :]]
            wcv = {}
            mmh = {}
            mmd = {}
            evd = {}
            std2 = {}
            sqd2 = {}
            cvd2 = {}
            sqa2 = {}
            fdd = {}
            stod = {}
            for mc in range(KC_D):
                p = mc % 2
                pa = wps[p]
                for i in range(2):
                    wl = []
                    if i == 1:
                        wl = [(sA.sem, wcv[(mc, 0)])]
                    elif mc >= 2:
                        wl = [(sA.sem, wcv[(mc - 2, 1)])]
                    sch.add("gpsimd", "dma_start", out=wfO[p][:, :],
                            in_=wor[mc][:, i * 16:(i + 1) * 16, :], w=wl,
                            inc=[W1[p].inc()])
                    w2l = [W1[p].at()]
                    if i == 1:
                        w2l.append((sT.sem, mmh[(mc, 0)]))
                    elif mc >= 2:
                        w2l.append((sT.sem, mmh[(mc - 2, 1)]))
                    sch.add("scalar", "copy", wbO[p][:, :],
                            wfO[p][:, :], w=w2l, inc=[sA.inc()])
                    wcv[(mc, i)] = sA.v
                    wl3 = [(sA.sem, wcv[(mc, i)])]
                    if i == 0 and mc >= 2:
                        wl3.append((sV.sem, evd[mc - 2]))
                    for h in range(2):
                        for kk in range(16):
                            kc = i * 16 + kk
                            sch.add("tensor", "matmul", pa[h][:, :],
                                    wbO[p][:, kk * 128:(kk + 1) * 128],
                                    osb[:, kc * 1024 + h * 512:
                                        kc * 1024 + (h + 1) * 512],
                                    start=(kc == 0), stop=(kc == KC_D - 1),
                                    w=(wl3 if (h == 0 and kk == 0) else []),
                                    inc=([sT.inc()]
                                         if (h == 1 and kk == 15) else []))
                    mmh[(mc, i)] = sT.v
                mmd[mc] = sT.v
                wl = [] if mc < 2 else [(sV.sem, evd[mc - 2])]
                sch.add("sync", "dma_start", out=rst5[p][:, :],
                        in_=r1d[mc * 128:(mc + 1) * 128, :], w=wl,
                        inc=[LA[p].inc()])
                r1ld = LA[p].at()
                for h in range(2):
                    wl = []
                    if h == 0:
                        wl = [(sT.sem, mmd[mc]), r1ld]
                        if mc >= 2:
                            wl += [stod[mc - 2],
                                   (sV.sem, cvd2[mc - 2])]
                    sch.add("vector", "tensor_add",
                            r2c[p][:, h * 512:(h + 1) * 512], pa[h][:, :],
                            rst5[p][:, h * 512:(h + 1) * 512], w=wl,
                            inc=([sV.inc()] if h == 1 else []))
                evd[mc] = sV.v
                sch.add("sync", "dma_start",
                        out=r2d[mc * 128:(mc + 1) * 128, :],
                        in_=r2c[p][:, :], w=[(sV.sem, evd[mc])],
                        inc=[SA_[p].inc()])
                stod[mc] = SA_[p].at()
                wl = [(sV.sem, evd[mc])]
                if mc >= 2:
                    wl += [(sT.sem, std2.get(mc - 2, 0))]
                sch.add("vector", "tensor_copy", xb5[p][:, :], r2c[p][:, :],
                        w=wl, inc=[sV.inc()])
                cvd2[mc] = sV.v
                wl = []
                if mc >= 2:
                    wl.append((sT.sem, sqd2.get(mc - 2, 0)))
                sch.add("vector", "tensor_mul", sq5[p][:, :], xb5[p][:, :],
                        xb5[p][:, :], w=wl, inc=[sV.inc()], lp=True)
                sqa2[mc] = sV.v
                for mcx in ([mc - 1] if mc >= 1 else []) + (
                        [mc] if mc == KC_D - 1 else []):
                    px = mcx % 2
                    for i in range(2):
                        sch.add("tensor", "matmul", sa_ps[i],
                                satb[:, mcx * 66:(mcx + 1) * 66],
                                xb5[px][:, i * 512:(i + 1) * 512],
                                start=(mcx == 0), stop=(mcx == KC_D - 1),
                                w=([(sV.sem, cvd2[mcx])] if i == 0 else []),
                                inc=([sT.inc()] if i == 1 else []))
                    std2[mcx] = sT.v
                    for i in range(2):
                        sch.add("tensor", "matmul", sq_ps[i], onesc[:, :],
                                sq5[px][:, i * 512:(i + 1) * 512],
                                start=(mcx == 0), stop=(mcx == KC_D - 1),
                                w=([(sV.sem, sqa2[mcx])] if i == 0 else []),
                                inc=([sT.inc()] if i == 1 else []))
                    sqd2[mcx] = sT.v
                fdd[mc] = sV.v
            bc = ln_post(sch, sa_ps, sq_ps, eps=1e-5, cmul=-4.0,
                         pb0=PS[0], pb1=PS[1], stats_done=(sT.sem, sT.v),
                         first_w=[(sV.sem, sV.v)])
            ln_post_lora(sch, sa_ps, PS[0], PS[1], scale=0.5, bc_done=bc,
                         asb=asb4, tmp66=tmp4)
            seal(sch)
            sch.emit(nc)

    if stop_stage == 2:
        debug_copy_out(r2d)
        return nc

    with ExitStack() as es:
        sb = lambda n, s, d: es.enter_context(
            nc.sbuf_tensor(n, s, d, side="right"))
        fold5 = sb("fold5", [128, KC_F * 1024], F32)
        tfo5 = sb("tfo5", [128, 1024], F32)
        rst6 = [sb(f"rst6{i}", [128, 1024], F32) for i in range(2)]
        sch = Sched()
        Sched.TAG[0] = "P11"
        phase_wait(sch)
        fdd = {}
        for c in range(KC_D):
            p = c % 2
            wl = [] if c < 2 else [(sV.sem, fdd[c - 2])]
            sch.add("sync", "dma_start", out=rst6[p][:, :],
                    in_=r2d[c * 128:(c + 1) * 128, :], w=wl,
                    inc=[LA[p].inc()])
            ld = LA[p].at()
            fc = c % KC_F
            if c < KC_F:
                sch.add("vector", "tensor_copy",
                        fold5[:, fc * 1024:(fc + 1) * 1024], rst6[p][:, :],
                        w=[ld], inc=[sV.inc()])
            else:
                sch.add("vector", "tensor_add",
                        fold5[:, fc * 1024:(fc + 1) * 1024],
                        fold5[:, fc * 1024:(fc + 1) * 1024], rst6[p][:, :],
                        w=[ld], inc=[sV.inc()])
            fdd[c] = sV.v
        for c in range(KC_F):
            sch.add("vector", "tensor_mul", tfo5[:, :],
                    fold5[:, c * 1024:(c + 1) * 1024], bcA[:, :])
            sch.add("vector", "tensor_add", xfb[:, c * 1024:(c + 1) * 1024],
                    tfo5[:, :], bcC[:, :],
                    inc=([sV.inc()] if c == KC_F - 1 else []))
        seal(sch)
        sch.emit(nc)

    ffn_phase(r2d, outT, is_final=True, tag="f2",
              dump_after_p2=(stop_stage == 28))
    if stop_stage == 28:
        return nc

    sch = Sched()
    Sched.TAG[0] = "P12"
    phase_wait(sch)
    for s in ALL_STORE:
        if s.v > 0:
            sch.add("sync", "wait_ge", *s.at())
    sch.emit(nc)
    return nc


def _prep_weights(inputs):
    f = np.float32
    Wq = np.asarray(inputs["L0_attn_Wq"], f)
    Wk = np.asarray(inputs["L0_attn_Wk"], f)
    Wv = np.asarray(inputs["L0_attn_Wv"], f)
    Wo = np.asarray(inputs["L0_attn_Wo"], f)
    w1q = np.asarray(inputs["L0_ffn_w1q"], f)
    a1 = np.asarray(inputs["L0_ffn_a1"], f)
    b1 = np.asarray(inputs["L0_ffn_b1"], f)
    w3q = np.asarray(inputs["L0_ffn_w3q"], f)
    a3 = np.asarray(inputs["L0_ffn_a3"], f)
    b3 = np.asarray(inputs["L0_ffn_b3"], f)
    w2q = np.asarray(inputs["L0_ffn_w2q"], f)
    a2 = np.asarray(inputs["L0_ffn_a2"], f)
    b2 = np.asarray(inputs["L0_ffn_b2"], f)

    def tile4(wT, nm, nk):
        K, M = wT.shape
        assert K == nk * 128 and M == nm * 128
        return np.ascontiguousarray(
            wT.reshape(nk, 128, nm, 128).transpose(2, 1, 0, 3))

    wqr = tile4(np.ascontiguousarray(Wq.T), NQ, KC_D)
    wkr = tile4(np.ascontiguousarray(Wk.T), NKV, KC_D)
    wvr = tile4(np.ascontiguousarray(Wv.T), NKV, KC_D)
    wor = tile4(np.ascontiguousarray(Wo.T), KC_D, KC_D)

    def padh_cols(m):
        R = m.shape[0]
        z = np.zeros((R, 4, 2816), f)
        z[:, :, :2752] = m.reshape(R, 4, 2752)
        return z.reshape(R, HP)

    w1p = padh_cols(np.ascontiguousarray(w1q.T))
    w3p = padh_cols(np.ascontiguousarray(w3q.T))
    b1p = padh_cols(np.ascontiguousarray(b1.T))
    b3p = padh_cols(np.ascontiguousarray(b3.T))
    a2p = padh_cols(a2)
    w2p = np.zeros((2816, D), f)
    w2p[:2752] = w2q.T
    w1r = tile4(w1p, MC_H, KC_F)
    w3r = tile4(w3p, MC_H, KC_F)
    w2r = tile4(w2p, KC_D, KC_H)
    a2S = np.ascontiguousarray(
        a2p.reshape(2, MC_H, 128).transpose(2, 1, 0).reshape(128, 2 * MC_H))
    b2T = np.ascontiguousarray(b2.T)

    SaTa = np.zeros((D, 66), f)
    SaTa[:, 0] = 1.0
    SaTa[:, 1] = 1.0
    SaTa[:, 32] = a1[0]
    SaTa[:, 33] = a1[1]
    SaTa[:, 64] = a3[0]
    SaTa[:, 65] = a3[1]

    sel40 = np.zeros((40, 40 * 128), f)
    for h in range(40):
        sel40[h, h * 128:(h + 1) * 128] = 1.0
    iotacv = (np.arange(128, dtype=f)[:, None]
              + 128.0 * np.arange(16, dtype=f)[None, :])
    return dict(
        wqr=wqr, wkr=wkr, wvr=wvr, wor=wor, w1r=w1r, w3r=w3r, w2r=w2r,
        b1T=np.ascontiguousarray(b1p), b3T=np.ascontiguousarray(b3p),
        b2T=b2T, a2S=a2S, SaT=SaTa,
        qw=np.ascontiguousarray(
            np.asarray(inputs["L0_attn_q_norm_w"], f).reshape(128, 1)),
        kw=np.ascontiguousarray(
            np.asarray(inputs["L0_attn_k_norm_w"], f).reshape(128, 1)),
        sel40=sel40,
        iden=np.eye(128, dtype=f), iotac=np.ascontiguousarray(iotacv))


def _core_masks(parity):
    f = np.float32
    tri = np.where(np.arange(128)[:, None] <= np.arange(128)[None, :],
                   0.0, -1e30).astype(f)
    m = np.empty((128, 16 * 128), f)
    for kc in range(16):
        cc = kc % 2
        if cc == parity:
            t = tri
        elif parity == 0:
            t = np.full((128, 128), -1e30, f)
        else:
            t = np.zeros((128, 128), f)
        m[:, kc * 128:(kc + 1) * 128] = t
    return m


_BLOCKS = {p: [p + 2 * i for i in range(8)] for p in (0, 1)}


def kernel(_stop=3, **inputs):
    x = np.asarray(inputs["L0_attn_x"], np.float32)
    shared = _prep_weights(inputs)
    masks = {p: _core_masks(p) for p in (0, 1)}
    qpos = {}
    for p in (0, 1):
        qp = np.zeros((2, T), np.float32)
        qp[0] = (128 * np.repeat(np.array(_BLOCKS[p], np.float32), 128)
                 + np.tile(np.arange(128, dtype=np.float32), 8))
        qpos[p] = qp
    in_maps = []
    for core in range(8):
        b, p = core // 2, core % 2
        blocks = _BLOCKS[p]
        xs = x[b].reshape(16, 128, D)[blocks].reshape(T, D)
        m = dict(shared)
        m["xT"] = np.ascontiguousarray(xs.T)
        m["masks"] = masks[p]
        m["qpos"] = qpos[p]
        in_maps.append(m)
    nc = build_nc(stop_stage=_stop)
    res = run_bass_kernel_spmd(nc, in_maps, core_ids=list(range(8)))
    out = np.empty((4, 2048, D), np.float32)
    for core in range(8):
        b, p = core // 2, core % 2
        blocks = _BLOCKS[p]
        oT = res.results[core]["outT"]
        out[b].reshape(16, 128, D)[blocks] = (
            oT.T.reshape(8, 128, D))
    return out


# version 14
# speedup vs baseline: 1.2946x; 1.0388x over previous
import math
from contextlib import ExitStack
import numpy as np
import concourse.bass as bass
import concourse.mybir as mybir
from concourse.bass_utils import run_bass_kernel_spmd

F32 = mybir.dt.float32
BF16 = mybir.dt.bfloat16
AF = mybir.ActivationFunctionType
ALU = mybir.AluOpType
AX = mybir.AxisListType

D = 4096
T = 1024
S = 2048
NB = 8
HP = 11264
NKV = 8
NQ = 32
HD = 128
KC_D = 32
KC_F = 8
MC_H = 88
KC_H = 22
ISQ = 1.0 / math.sqrt(HD)


def _alibi_slopes(n):
    start = 2 ** (-2 ** (-(math.log2(n) - 3)))
    return np.array([start * start ** i for i in range(n)], dtype=np.float32)


class Sched:
    ENGINES = ("gpsimd", "sync", "tensor", "vector", "scalar")

    def __init__(self):
        self.L = {e: [] for e in self.ENGINES}

    TAG = [""]

    def add(self, eng, meth, *a, w=(), inc=(), lp=False, **k):
        self.L[eng].append([meth, a, k, list(w), list(inc), lp,
                            f"{self.TAG[0]}#{len(self.L[eng])}"])

    def emit(self, nc):
        from contextlib import nullcontext
        with nc.Block() as blk:
            for eng in self.ENGINES:
                if not self.L[eng]:
                    continue

                def runner(e, L=self.L[eng]):
                    for (meth, a, k, w, inc, lp, tag) in L:
                        for (sem, v) in w:
                            e.wait_ge(sem, v)
                        ctx = (nc.allow_low_precision("bf16 matmul operand")
                               if lp else nullcontext())
                        with ctx:
                            inst = getattr(e, meth)(*a, **k)
                        try:
                            inst.annotate(tag)
                        except Exception:
                            pass
                        for (sem, n) in inc:
                            inst.then_inc(sem, n)
                getattr(blk, eng)(runner)


class S16:

    def __init__(self, nc, name):
        self.sem = nc.alloc_semaphore(name)
        self.v = 0

    def inc(self):
        self.v += 16
        return (self.sem, 16)

    def at(self):
        return (self.sem, self.v)


class Ctr:
    def __init__(self, sem):
        self.sem = sem
        self.v = 0

    def inc(self, n=1):
        self.v += n
        return (self.sem, n)

    def at(self):
        return (self.sem, self.v)


def build_nc(stop_stage=3):
    nc = bass.Bass("TRN2", target_bir_lowering=False,
                   detect_race_conditions=False)

    def din(name, shape, dt=F32):
        return nc.dram_tensor(name, shape, dt, kind="ExternalInput").ap()

    xT = din("xT", [D, T])
    wqr = din("wqr", [NQ, 128, KC_D, 128])
    wkr = din("wkr", [NKV, 128, KC_D, 128])
    wvr = din("wvr", [NKV, 128, KC_D, 128])
    wor = din("wor", [KC_D, 128, KC_D, 128])
    w1r = din("w1r", [MC_H, 128, KC_F, 128])
    w3r = din("w3r", [MC_H, 128, KC_F, 128])
    w2r = din("w2r", [KC_D, 128, KC_H, 128])
    b1T = din("b1T", [2, HP])
    b3T = din("b3T", [2, HP])
    b2T = din("b2T", [2, D])
    a2S = din("a2S", [128, 2 * MC_H])
    SaT = din("SaT", [D, 66])
    qwD = din("qw", [128, 1])
    kwD = din("kw", [128, 1])
    masksD = din("masks", [128, 16 * 128])
    qposD = din("qpos", [2, T])
    sel40D = din("sel40", [40, 40 * 128])
    idenD = din("iden", [128, 128])
    iotacD = din("iotac", [128, 16])

    outT = nc.dram_tensor("outT", [D, T], F32, kind="ExternalOutput").ap()
    r1d = nc.dram_tensor("r1d", [D, T], F32, kind="Internal").ap()
    r2d = nc.dram_tensor("r2d", [D, T], F32, kind="Internal").ap()
    qd = nc.dram_tensor("qd", [40 * 128, T], BF16, kind="Internal").ap()
    kv_in = nc.dram_tensor("kv_in", [16384, 128], BF16, kind="Internal").ap()
    kv_out = nc.dram_tensor("kv_out", [32768, 128], BF16, kind="Internal").ap()

    sT = Ctr(nc.alloc_semaphore("sT"))
    sV = Ctr(nc.alloc_semaphore("sV"))
    sA = Ctr(nc.alloc_semaphore("sA"))
    sW = Ctr(nc.alloc_semaphore("sW"))
    sR = Ctr(nc.alloc_semaphore("sR"))
    sO = Ctr(nc.alloc_semaphore("sO"))
    sCC = Ctr(nc.alloc_semaphore("sCC"))
    sB = Ctr(nc.alloc_semaphore("sB"))
    sSet = S16(nc, "sSet")
    LA = [S16(nc, f"LA{i}") for i in range(2)]
    QA = [S16(nc, f"QA{i}") for i in range(2)]
    W1 = [S16(nc, f"W1{i}") for i in range(2)]
    W3 = [S16(nc, f"W3{i}") for i in range(2)]
    KF = [S16(nc, f"KF{i}") for i in range(2)]
    SA_ = [S16(nc, f"SA{i}") for i in range(2)]
    QS = [S16(nc, f"QS{i}") for i in range(2)]
    KV = [S16(nc, f"KV{i}") for i in range(2)]
    KVV = S16(nc, "KVV")
    ALL_STORE = SA_ + QS + KV + [KVV]

    AL = nc.alloc_sbuf_tensor
    ones2 = AL("ones2", [2, 128], BF16, side="left")
    ones2n = AL("ones2n", [2, 128], BF16, side="left")
    onesc = AL("onesc", [128, 2], BF16, side="left")
    iden_bf = AL("iden_bf", [128, 128], BF16, side="left")
    qw = AL("qw_s", [128, 1], F32, side="left")
    kw = AL("kw_s", [128, 1], F32, side="left")
    iotac = AL("iotac_s", [128, 16], F32, side="left")
    u_bf = AL("u_bf", [66, 1024], BF16, side="left")
    u2_bf = AL("u2_bf", [2, 1024], BF16, side="left")
    bcA = AL("bcA", [128, 1024], F32, side="left")
    bcC = AL("bcC", [128, 1024], F32, side="left")
    mrow = AL("mrow", [2, 1024], F32, side="left")
    mrow_bf = AL("mrow_bf", [2, 1024], BF16, side="left")
    vrow = AL("vrow", [2, 1024], F32, side="left")
    rsr = AL("rsr", [2, 1024], F32, side="left")
    rsr_bf = AL("rsr_bf", [2, 1024], BF16, side="left")
    crow = AL("crow", [2, 1024], F32, side="left")
    crow_bf = AL("crow_bf", [2, 1024], BF16, side="left")
    negsA = AL("negsA", [66, 1], F32, side="left")
    xfb = AL("xfb", [128, KC_F * 1024], BF16, side="left")
    satb = AL("satb", [128, KC_D * 66], BF16, side="left")
    W40 = AL("W40", [128, 79], BF16, side="left")
    epsc = AL("epsc", [128, 2], F32, side="left")
    qposs = AL("qposs", [2, 1024], F32, side="left")

    PS = [nc.alloc_psum_tensor(f"ps{i}", [128, 512], F32) for i in range(8)]

    BARP = [[]]
    OWN = {"tensor": sT, "vector": sV, "scalar": sA}

    def seal(sch):
        pairs = [(sCC.sem, sCC.v), (sSet.sem, sSet.v)]
        pairs += [s.at() for s in ALL_STORE if s.v > 0]
        for eng in ("tensor", "vector", "scalar"):
            if sch.L[eng]:
                last = sch.L[eng][-1]
                if not last[4]:
                    last[4].append(OWN[eng].inc())
                pairs.append(OWN[eng].at())
        BARP[0] = [p for p in pairs if p[1] > 0]

    def phase_wait(sch):
        for eng in Sched.ENGINES:
            for (sem, v) in BARP[0]:
                sch.add(eng, "wait_ge", sem, v)

    def ln_post(sch, sa_ps, sq_ps, eps, cmul, pb0, pb1, stats_done,
                first_w=(), fold=None, tfa=None):
        n = float(D)
        for i in range(2):
            sch.add("scalar", "mul", mrow[:, i * 512:(i + 1) * 512],
                    sa_ps[i][0:2, :], 1.0 / n,
                    w=([stats_done] if i == 0 else []),
                    inc=([sA.inc()] if i == 1 else []))
        m_done = sA.at()
        sch.add("vector", "tensor_copy", mrow_bf[:, :], mrow[:, :], w=[m_done])
        sch.add("vector", "tensor_mul", vrow[:, :], mrow[:, :], mrow[:, :])
        for i in range(2):
            sch.add("vector", "scalar_tensor_tensor",
                    rsr[:, i * 512:(i + 1) * 512], sq_ps[i][0:2, :], 1.0 / n,
                    vrow[:, i * 512:(i + 1) * 512], ALU.mult, ALU.subtract,
                    inc=([sV.inc()] if i == 1 else []))
        v_done = sV.at()
        sch.add("scalar", "activation", rsr[:, :], rsr[:, :], AF.Ln,
                bias=epsc[0:2, 0:1], w=[v_done])
        sch.add("scalar", "activation", rsr[:, :], rsr[:, :], AF.Exp,
                scale=-0.5, inc=[sA.inc()])
        r_done = sA.at()
        sch.add("vector", "tensor_copy", rsr_bf[:, :], rsr[:, :], w=[r_done])
        sch.add("vector", "tensor_mul", crow[:, :], mrow[:, :], rsr[:, :])
        sch.add("vector", "tensor_scalar_mul", crow[:, :], crow[:, :],
                float(cmul))
        sch.add("vector", "tensor_copy", crow_bf[:, :], crow[:, :],
                inc=[sV.inc()])
        cb_done = sV.at()
        for i in range(2):
            wl0 = ([cb_done] + list(first_w)) if i == 0 else [sA.at()]
            sch.add("tensor", "matmul", pb0[:, :], ones2[:, :],
                    rsr_bf[:, i * 512:(i + 1) * 512], start=True, stop=True,
                    w=wl0)
            sch.add("tensor", "matmul", pb1[:, :], ones2[:, :],
                    crow_bf[:, i * 512:(i + 1) * 512], start=True, stop=True,
                    inc=[sT.inc()])
            t_done = sT.at()
            sch.add("scalar", "copy", bcA[:, i * 512:(i + 1) * 512],
                    pb0[:, :], w=[t_done])
            sch.add("scalar", "copy", bcC[:, i * 512:(i + 1) * 512],
                    pb1[:, :], inc=[sA.inc()])
        ab_done = sA.at()
        if fold is not None:
            for c in range(KC_F):
                sch.add("vector", "tensor_mul", tfa[:, :],
                        fold[:, c * 1024:(c + 1) * 1024], bcA[:, :],
                        w=([ab_done] if c == 0 else []))
                sch.add("vector", "tensor_add",
                        xfb[:, c * 1024:(c + 1) * 1024], tfa[:, :], bcC[:, :],
                        inc=([sV.inc()] if c == KC_F - 1 else []))
        return ab_done

    def ln_post_lora(sch, sa_ps, pb0, pb1, scale, bc_done, asb, tmp66):
        for i in range(2):
            sch.add("scalar", "copy", asb[:, i * 512:(i + 1) * 512],
                    sa_ps[i][:, :], inc=([sA.inc()] if i == 1 else []))
        a_done = sA.at()
        mb = []
        for i in range(2):
            pb = pb0 if i == 0 else pb1
            sch.add("tensor", "matmul", pb[0:66, :], ones2[:, 0:66],
                    mrow_bf[:, i * 512:(i + 1) * 512], start=True, stop=True,
                    w=([bc_done] if i == 0 else []), inc=[sT.inc()])
            mb.append(sT.at())
        for i in range(2):
            pb = pb0 if i == 0 else pb1
            sch.add("vector", "scalar_tensor_tensor",
                    tmp66[:, i * 512:(i + 1) * 512], pb[0:66, :], negsA[:, :],
                    asb[:, i * 512:(i + 1) * 512], ALU.mult, ALU.add,
                    w=[a_done, mb[i]], inc=[sV.inc()])
        tdone = sV.at()
        rb = []
        for i in range(2):
            pb = pb0 if i == 0 else pb1
            sch.add("tensor", "matmul", pb[0:66, :], ones2[:, 0:66],
                    rsr_bf[:, i * 512:(i + 1) * 512], start=True, stop=True,
                    w=([tdone] if i == 0 else []), inc=[sT.inc()])
            rb.append(sT.at())
        for i in range(2):
            pb = pb0 if i == 0 else pb1
            sch.add("vector", "scalar_tensor_tensor",
                    u_bf[:, i * 512:(i + 1) * 512],
                    tmp66[:, i * 512:(i + 1) * 512], float(scale),
                    pb[0:66, :], ALU.mult, ALU.mult, w=[rb[i]],
                    inc=([sV.inc()] if i == 1 else []))

    with ExitStack() as es:
        stg = es.enter_context(nc.sbuf_tensor("stg_set", [128, 2240], F32,
                                              side="right"))
        sch = Sched()
        Sched.TAG[0] = "P1"
        sch.add("vector", "memset", ones2[:, :], 0.0)
        sch.add("vector", "memset", ones2[0:1, :], 1.0)
        sch.add("vector", "memset", ones2n[:, :], 0.0)
        sch.add("vector", "memset", ones2n[0:1, :], -1.0)
        sch.add("vector", "memset", onesc[:, :], 1.0)
        sch.add("vector", "memset", W40[:, :], 0.0)
        sch.add("vector", "memset", W40[:, 39:40], 1.0)
        sch.add("vector", "memset", epsc[:, 0:1], 1e-5)
        sch.add("vector", "memset", epsc[:, 1:2], 1e-6)
        sch.add("sync", "dma_start", out=stg[:, 2112:2240], in_=idenD,
                inc=[sSet.inc()])
        sch.add("sync", "dma_start", out=qw[:, :], in_=qwD, inc=[sSet.inc()])
        sch.add("sync", "dma_start", out=kw[:, :], in_=kwD, inc=[sSet.inc()])
        sch.add("sync", "dma_start", out=iotac[:, :], in_=iotacD,
                inc=[sSet.inc()])
        sch.add("sync", "dma_start", out=qposs[:, :], in_=qposD,
                inc=[sSet.inc()])
        sch.add("sync", "dma_start", out=stg[:, 0:2112],
                in_=SaT.rearrange("(c p) j -> p c j", p=128),
                inc=[sSet.inc()])
        ld = sSet.at()
        sch.add("vector", "tensor_copy", iden_bf[:, :], stg[:, 2112:2240],
                w=[ld])
        sch.add("vector", "tensor_scalar_mul", qw[:, :], qw[:, :], ISQ)
        sch.add("vector", "tensor_copy", satb[:, :], stg[:, 0:2112])
        seal(sch)
        sch.emit(nc)

    with ExitStack() as es:
        sb = lambda n, s, d: es.enter_context(
            nc.sbuf_tensor(n, s, d, side="right"))
        fold = sb("fold0", [128, KC_F * 1024], F32)
        asb0 = sb("asb0", [66, 1024], F32)
        tmp0 = sb("tmp0", [66, 1024], F32)
        tfold = sb("tfold0", [128, 1024], F32)
        rstg = [sb(f"rstg0{i}", [128, 1024], F32) for i in range(2)]
        xbf = [sb(f"xbf0{i}", [128, 1024], BF16) for i in range(2)]
        sqb = [sb(f"sqb0{i}", [128, 1024], BF16) for i in range(2)]
        sch = Sched()
        Sched.TAG[0] = "P2"
        phase_wait(sch)
        for kc in range(KC_D):
            sch.add("tensor", "matmul", PS[6][0:66, 0:2],
                    satb[:, kc * 66:(kc + 1) * 66], onesc[:, :],
                    start=(kc == 0), stop=(kc == KC_D - 1),
                    inc=([sT.inc()] if kc == KC_D - 1 else []))
        sch.add("scalar", "mul", negsA[:, :], PS[6][0:66, 0:1], -1.0,
                w=[sT.at()], inc=[sA.inc()])
        sa_ps = [PS[0][0:66, :], PS[1][0:66, :]]
        sq_ps = [PS[2][0:2, :], PS[3][0:2, :]]
        cvd, std, sqd, sqa, fdd = {}, {}, {}, {}, {}
        for c in range(KC_D):
            p = c % 2
            wl = [] if c < 2 else [(sV.sem, fdd[c - 2]), (sV.sem, cvd[c - 2])]
            sch.add("sync", "dma_start", out=rstg[p][:, :],
                    in_=xT[c * 128:(c + 1) * 128, :], w=wl,
                    inc=[LA[p].inc()])
            ld = LA[p].at()
            wl = [ld]
            if c >= 2:
                wl += [(sT.sem, std[c - 2]), (sA.sem, sqa[c - 2])]
            sch.add("vector", "tensor_copy", xbf[p][:, :], rstg[p][:, :],
                    w=wl, inc=[sV.inc()])
            cvd[c] = sV.v
            wl = [(sV.sem, cvd[c])]
            if c >= 2:
                wl.append((sT.sem, sqd[c - 2]))
            sch.add("scalar", "activation", sqb[p][:, :], xbf[p][:, :],
                    AF.Square, w=wl, inc=[sA.inc()])
            sqa[c] = sA.v
            for i in range(2):
                sch.add("tensor", "matmul", sa_ps[i],
                        satb[:, c * 66:(c + 1) * 66],
                        xbf[p][:, i * 512:(i + 1) * 512],
                        start=(c == 0), stop=(c == KC_D - 1),
                        w=([(sV.sem, cvd[c])] if i == 0 else []),
                        inc=([sT.inc()] if i == 1 else []))
            std[c] = sT.v
            for i in range(2):
                sch.add("tensor", "matmul", sq_ps[i], onesc[:, :],
                        sqb[p][:, i * 512:(i + 1) * 512],
                        start=(c == 0), stop=(c == KC_D - 1),
                        w=([(sA.sem, sqa[c])] if i == 0 else []),
                        inc=([sT.inc()] if i == 1 else []))
            sqd[c] = sT.v
            fc = c % KC_F
            if c < KC_F:
                sch.add("vector", "tensor_copy",
                        fold[:, fc * 1024:(fc + 1) * 1024], rstg[p][:, :],
                        w=[ld], inc=[sV.inc()])
            else:
                sch.add("vector", "tensor_add",
                        fold[:, fc * 1024:(fc + 1) * 1024],
                        fold[:, fc * 1024:(fc + 1) * 1024], rstg[p][:, :],
                        w=[ld], inc=[sV.inc()])
            fdd[c] = sV.v
        bc = ln_post(sch, sa_ps, sq_ps, eps=1e-5, cmul=-4.0,
                     pb0=PS[4], pb1=PS[5], stats_done=(sT.sem, sT.v),
                     fold=fold, tfa=tfold)
        ln_post_lora(sch, sa_ps, PS[4], PS[5], scale=0.5, bc_done=bc,
                     asb=asb0, tmp66=tmp0)
        seal(sch)
        sch.emit(nc)

    def ffn_phase(r0_src, rdst, is_final, tag, dump_after_p2=False):
        with ExitStack() as es:
            sb = lambda n, s, d: es.enter_context(
                nc.sbuf_tensor(n + tag, s, d, side="right"))
            gf = sb("gf", [128, KC_H * 1024], BF16)
            bv13 = sb("bv13", [66, HP], BF16)
            b2b = sb("b2b", [2, D], BF16)
            a2b = sb("a2b", [128, 2 * MC_H], BF16)
            with ExitStack() as es2:
                bstg = es2.enter_context(nc.sbuf_tensor(
                    "bstg" + tag, [128, 2 * MC_H], F32, side="right"))
                bstg2 = es2.enter_context(nc.sbuf_tensor(
                    "bstg2" + tag, [66, HP], F32, side="right"))
                sch = Sched()
                Sched.TAG[0] = "P3"
                phase_wait(sch)
                sch.add("sync", "dma_start", out=bstg2[32:34, :], in_=b1T,
                        inc=[sSet.inc()])
                sch.add("sync", "dma_start", out=bstg2[64:66, :], in_=b3T,
                        inc=[sSet.inc()])
                sch.add("sync", "dma_start", out=bstg2[0:2, 0:D], in_=b2T,
                        inc=[sSet.inc()])
                sch.add("sync", "dma_start", out=bstg[:, :], in_=a2S,
                        inc=[sSet.inc()])
                bl = sSet.at()
                sch.add("vector", "memset", bv13[:, :], 0.0, w=[bl])
                sch.add("vector", "tensor_copy", bv13[32:34, :],
                        bstg2[32:34, :])
                sch.add("vector", "tensor_copy", bv13[64:66, :],
                        bstg2[64:66, :])
                sch.add("vector", "tensor_copy", b2b[:, :], bstg2[0:2, 0:D])
                sch.add("vector", "tensor_copy", a2b[:, :], bstg[:, :],
                        inc=[sV.inc()])
                seal(sch)
                sch.emit(nc)

            esp2 = ExitStack()
            sbp = lambda n, s, d: esp2.enter_context(
                nc.sbuf_tensor(n + tag, s, d, side="right"))
            w1f = [sbp(f"w1f{i}", [128, 1024], F32) for i in range(2)]
            w3f = [sbp(f"w3f{i}", [128, 1024], F32) for i in range(2)]
            w1b = [sbp(f"w1b{i}", [128, 1024], BF16) for i in range(2)]
            w3b = [sbp(f"w3b{i}", [128, 1024], BF16) for i in range(2)]
            s1b = sbp("s1b", [128, 1024], F32)
            gbf = [sbp(f"gbf{i}", [128, 1024], BF16) for i in range(2)]
            tf2 = sbp("tf2", [128, 1024], F32)

            sch = Sched()
            Sched.TAG[0] = "P4"
            phase_wait(sch)
            y1p = [[PS[0], PS[1]], [PS[2], PS[3]]]
            y3p = [PS[4], PS[5]]
            psU = [PS[6][0:2, :], PS[7][0:2, :]]
            wcv, sld, y1d, y3d, gpd, a2d, pmc = ({}, {}, {}, {}, {}, {}, {})
            for c in range(KC_H):
                for qt in range(4):
                    mc = qt * KC_H + c
                    it = c * 4 + qt
                    p = it % 2
                    wl = [] if it < 2 else [(sA.sem, wcv[it - 2])]
                    sch.add("gpsimd", "dma_start", out=w1f[p][:, :],
                            in_=w1r[mc], w=wl, inc=[W1[p].inc()])
                    sch.add("gpsimd", "dma_start", out=w3f[p][:, :],
                            in_=w3r[mc], inc=[W3[p].inc()])
                    wl = [W1[p].at(), W3[p].at()]
                    if it >= 2:
                        wl.append((sT.sem, y3d[it - 2]))
                    sch.add("scalar", "copy", w1b[p][:, :], w1f[p][:, :],
                            w=wl)
                    sch.add("scalar", "copy", w3b[p][:, :], w3f[p][:, :],
                            inc=[sA.inc()])
                    wcv[it] = sA.v
                    pa = y1p[p]
                    wl = [(sA.sem, wcv[it])]
                    if it >= 2:
                        wl.append((sA.sem, sld[it - 2]))
                    for h in range(2):
                        for kc in range(KC_F):
                            sch.add("tensor", "matmul", pa[h][:, :],
                                    w1b[p][:, kc * 128:(kc + 1) * 128],
                                    xfb[:, kc * 1024 + h * 512:
                                        kc * 1024 + (h + 1) * 512],
                                    start=(kc == 0), stop=False,
                                    w=(wl if (h == 0 and kc == 0) else []))
                        sch.add("tensor", "matmul", pa[h][:, :],
                                bv13[32:34, mc * 128:(mc + 1) * 128],
                                u_bf[32:34, h * 512:(h + 1) * 512],
                                start=False, stop=True,
                                inc=([sT.inc()] if h == 1 else []))
                    y1d[it] = sT.v
                    wl = [] if it < 1 else [(sV.sem, gpd[it - 1])]
                    for h in range(2):
                        for kc in range(KC_F):
                            sch.add("tensor", "matmul", y3p[h][:, :],
                                    w3b[p][:, kc * 128:(kc + 1) * 128],
                                    xfb[:, kc * 1024 + h * 512:
                                        kc * 1024 + (h + 1) * 512],
                                    start=(kc == 0), stop=False,
                                    w=(wl if (h == 0 and kc == 0) else []))
                        sch.add("tensor", "matmul", y3p[h][:, :],
                                bv13[64:66, mc * 128:(mc + 1) * 128],
                                u_bf[64:66, h * 512:(h + 1) * 512],
                                start=False, stop=True,
                                inc=([sT.inc()] if h == 1 else []))
                    y3d[it] = sT.v
                    for h in range(2):
                        wl5 = []
                        if h == 0:
                            wl5 = [(sT.sem, y1d[it])]
                            if it >= 1:
                                wl5.append((sV.sem, gpd[it - 1]))
                        sch.add("scalar", "activation",
                                s1b[:, h * 512:(h + 1) * 512], pa[h][:, :],
                                AF.Silu, w=wl5,
                                inc=([sA.inc()] if h == 1 else []))
                    sld[it] = sA.v
                    gp = it % 2
                    wl6 = [(sT.sem, y3d[it]), (sA.sem, sld[it])]
                    if it >= 2:
                        wl6.append((sT.sem, a2d[it - 2]))
                    for h in range(2):
                        sch.add("vector", "tensor_mul",
                                gbf[gp][:, h * 512:(h + 1) * 512],
                                s1b[:, h * 512:(h + 1) * 512], y3p[h][:, :],
                                w=(wl6 if h == 0 else []),
                                inc=([sV.inc()] if h == 1 else []))
                    gpd[it] = sV.v
                    if it >= 1:
                        for h in range(2):
                            sch.add("tensor", "matmul", psU[h],
                                    a2b[:, 2 * pmc[it - 1]:
                                        2 * pmc[it - 1] + 2],
                                    gbf[(it - 1) % 2]
                                    [:, h * 512:(h + 1) * 512],
                                    start=(it == 1), stop=False,
                                    w=([(sV.sem, gpd[it - 1])]
                                       if h == 0 else []),
                                    inc=([sT.inc()] if h == 1 else []))
                        a2d[it - 1] = sT.v
                    pmc[it] = mc
                    if qt == 0:
                        sch.add("vector", "tensor_copy", tf2[:, :],
                                gbf[gp][:, :])
                    elif qt < 3:
                        sch.add("vector", "tensor_add", tf2[:, :], tf2[:, :],
                                gbf[gp][:, :])
                    else:
                        sch.add("vector", "tensor_add",
                                gf[:, c * 1024:(c + 1) * 1024], tf2[:, :],
                                gbf[gp][:, :], inc=[sV.inc()])
            last_it = KC_H * 4 - 1
            for h in range(2):
                sch.add("tensor", "matmul", psU[h],
                        a2b[:, 2 * pmc[last_it]:2 * pmc[last_it] + 2],
                        gbf[last_it % 2][:, h * 512:(h + 1) * 512],
                        start=False, stop=True,
                        w=([(sV.sem, gpd[last_it])] if h == 0 else []),
                        inc=([sT.inc()] if h == 1 else []))
            sch.add("scalar", "mul", u2_bf[:, 0:512], psU[0], 0.5,
                    w=[(sT.sem, sT.v)])
            sch.add("scalar", "mul", u2_bf[:, 512:1024], psU[1], 0.5,
                    inc=[sA.inc()])
            seal(sch)
            sch.emit(nc)
            esp2.close()

            if dump_after_p2:
                sch = Sched()
                Sched.TAG[0] = "DBGG"
                phase_wait(sch)
                rfd = tf2
                row = 0
                for c in range(KC_F):
                    sch.add("vector", "tensor_copy", rfd[:, :],
                            xfb[:, c * 1024:(c + 1) * 1024],
                            w=([] if row == 0 else [SA_[0].at()]),
                            inc=[sV.inc()])
                    sch.add("sync", "dma_start",
                            out=outT[row * 128:(row + 1) * 128, :],
                            in_=rfd[:, :], w=[sV.at()], inc=[SA_[0].inc()])
                    row += 1
                for c in range(KC_H):
                    sch.add("vector", "tensor_copy", rfd[:, :],
                            gf[:, c * 1024:(c + 1) * 1024],
                            w=[SA_[0].at()], inc=[sV.inc()])
                    sch.add("sync", "dma_start",
                            out=outT[row * 128:(row + 1) * 128, :],
                            in_=rfd[:, :], w=[sV.at()], inc=[SA_[0].inc()])
                    row += 1
                sch.add("vector", "memset", rfd[:, :], 0.0,
                        w=[SA_[0].at()], inc=[sV.inc()])
                for c in range(row, KC_D):
                    sch.add("sync", "dma_start",
                            out=outT[c * 128:(c + 1) * 128, :],
                            in_=rfd[:, :],
                            w=[sV.at(), SA_[0].at()], inc=[SA_[0].inc()])
                sch.add("sync", "wait_ge", *SA_[0].at())
                seal(sch)
                sch.emit(nc)
                return

            with ExitStack() as es3:
                sb3 = lambda n, s, d: es3.enter_context(
                    nc.sbuf_tensor(n + tag, s, d, side="right"))
                w2f = [sb3(f"w2f{i}", [128, KC_H * 128], F32)
                       for i in range(2)]
                w2b = [sb3(f"w2b{i}", [128, KC_H * 128], BF16)
                       for i in range(2)]
                rst3 = [sb3(f"rst3{i}", [128, 1024], F32) for i in range(2)]
                r1c = [sb3(f"r1c{i}", [128, 1024], F32) for i in range(2)]
                xb3 = [sb3(f"xb3{i}", [128, 1024], BF16) for i in range(2)]
                sq3 = [sb3(f"sq3{i}", [128, 1024], BF16) for i in range(2)]
                sch = Sched()
                Sched.TAG[0] = "P5"
                phase_wait(sch)
                wps = [[PS[0], PS[1]], [PS[2], PS[3]]]
                stx = [PS[4][0:2, :], PS[5][0:2, :]]
                stq = [PS[6][0:2, :], PS[7][0:2, :]]
                wcv, mmd, evd, std2, sqd2, cvd2, sqa2, stod = (
                    {} for _ in range(8))
                for mc in range(KC_D):
                    p = mc % 2
                    wl = [] if mc < 2 else [(sA.sem, wcv[mc - 2])]
                    sch.add("gpsimd", "dma_start", out=w2f[p][:, :],
                            in_=w2r[mc], w=wl, inc=[W1[p].inc()])
                    wl = [W1[p].at()]
                    if mc >= 2:
                        wl.append((sT.sem, mmd[mc - 2]))
                    sch.add("scalar", "copy", w2b[p][:, :], w2f[p][:, :],
                            w=wl, inc=[sA.inc()])
                    wcv[mc] = sA.v
                    wl = [] if mc < 2 else [(sV.sem, evd[mc - 2])]
                    sch.add("sync", "dma_start", out=rst3[p][:, :],
                            in_=r0_src[mc * 128:(mc + 1) * 128, :], w=wl,
                            inc=[LA[p].inc()])
                    r0ld = LA[p].at()
                    pa = wps[p]
                    wl = [(sA.sem, wcv[mc])]
                    if mc >= 2:
                        wl.append((sV.sem, evd[mc - 2]))
                    for h in range(2):
                        for kc in range(KC_H):
                            sch.add("tensor", "matmul", pa[h][:, :],
                                    w2b[p][:, kc * 128:(kc + 1) * 128],
                                    gf[:, kc * 1024 + h * 512:
                                        kc * 1024 + (h + 1) * 512],
                                    start=(kc == 0), stop=False,
                                    w=(wl if (h == 0 and kc == 0) else []))
                        sch.add("tensor", "matmul", pa[h][:, :],
                                b2b[:, mc * 128:(mc + 1) * 128],
                                u2_bf[:, h * 512:(h + 1) * 512],
                                start=False, stop=True,
                                inc=([sT.inc()] if h == 1 else []))
                    mmd[mc] = sT.v
                    for h in range(2):
                        wl4 = []
                        if h == 0:
                            wl4 = [(sT.sem, mmd[mc]), r0ld]
                            if mc >= 2:
                                wl4.append(stod[mc - 2])
                                if not is_final:
                                    wl4.append((sV.sem, cvd2[mc - 2]))
                        sch.add("vector", "scalar_tensor_tensor",
                                r1c[p][:, h * 512:(h + 1) * 512],
                                pa[h][:, :], 0.5,
                                rst3[p][:, h * 512:(h + 1) * 512],
                                ALU.mult, ALU.add, w=wl4,
                                inc=([sV.inc()] if h == 1 else []))
                    evd[mc] = sV.v
                    sch.add("sync", "dma_start",
                            out=rdst[mc * 128:(mc + 1) * 128, :],
                            in_=r1c[p][:, :], w=[(sV.sem, evd[mc])],
                            inc=[SA_[p].inc()])
                    stod[mc] = SA_[p].at()
                    if not is_final:
                        wl = [(sV.sem, evd[mc])]
                        if mc >= 2:
                            wl += [(sT.sem, std2.get(mc - 2, 0)),
                                   (sT.sem, sqd2.get(mc - 2, 0))]
                        sch.add("vector", "tensor_copy", xb3[p][:, :],
                                r1c[p][:, :], w=wl, inc=[sV.inc()])
                        cvd2[mc] = sV.v
                        sch.add("vector", "tensor_mul", sq3[p][:, :],
                                xb3[p][:, :], xb3[p][:, :],
                                inc=[sV.inc()], lp=True)
                        sqa2[mc] = sV.v
                        for mcx in ([mc - 1] if mc >= 1 else []) + (
                                [mc] if mc == KC_D - 1 else []):
                            px = mcx % 2
                            for i in range(2):
                                sch.add("tensor", "matmul", stx[i],
                                        onesc[:, :],
                                        xb3[px][:, i * 512:(i + 1) * 512],
                                        start=(mcx == 0),
                                        stop=(mcx == KC_D - 1),
                                        w=([(sV.sem, cvd2[mcx])] if i == 0
                                           else []),
                                        inc=([sT.inc()] if i == 1 else []))
                            std2[mcx] = sT.v
                            for i in range(2):
                                sch.add("tensor", "matmul", stq[i],
                                        onesc[:, :],
                                        sq3[px][:, i * 512:(i + 1) * 512],
                                        start=(mcx == 0),
                                        stop=(mcx == KC_D - 1),
                                        w=([(sV.sem, sqa2[mcx])] if i == 0
                                           else []),
                                        inc=([sT.inc()] if i == 1 else []))
                            sqd2[mcx] = sT.v
                if not is_final:
                    ln_post(sch, stx, stq, eps=1e-5, cmul=-1.0,
                            pb0=PS[0], pb1=PS[1], stats_done=(sT.sem, sT.v),
                            first_w=[(sV.sem, sV.v)])
                seal(sch)
                sch.emit(nc)

    ffn_phase(xT, r1d, is_final=False, tag="f1")

    def debug_copy_out(srcd):
        with ExitStack() as es:
            rr = es.enter_context(nc.sbuf_tensor("rrcopy", [128, 1024], F32,
                                                 side="right"))
            sch = Sched()
            Sched.TAG[0] = "DBG"
            phase_wait(sch)
            for c in range(KC_D):
                sch.add("sync", "dma_start", out=rr[:, :],
                        in_=srcd[c * 128:(c + 1) * 128, :],
                        w=([] if c == 0 else [SA_[0].at()]),
                        inc=[sSet.inc()])
                sch.add("sync", "dma_start",
                        out=outT[c * 128:(c + 1) * 128, :], in_=rr[:, :],
                        w=[sSet.at()], inc=[SA_[0].inc()])
            sch.add("sync", "wait_ge", *SA_[0].at())
            seal(sch)
            sch.emit(nc)

    def debug_dump_bf16(srcd_view):
        with ExitStack() as es:
            rb = es.enter_context(nc.sbuf_tensor("rdbg_b", [128, 1024], BF16,
                                                 side="right"))
            rf = es.enter_context(nc.sbuf_tensor("rdbg_f", [128, 1024], F32,
                                                 side="right"))
            sch = Sched()
            Sched.TAG[0] = "DBGB"
            phase_wait(sch)
            for c in range(KC_D):
                sch.add("sync", "dma_start", out=rb[:, :],
                        in_=srcd_view[c * 128:(c + 1) * 128, :],
                        w=([] if c == 0 else [SA_[0].at()]),
                        inc=[sSet.inc()])
                sch.add("vector", "tensor_copy", rf[:, :], rb[:, :],
                        w=[sSet.at()], inc=[sV.inc()])
                sch.add("sync", "dma_start",
                        out=outT[c * 128:(c + 1) * 128, :], in_=rf[:, :],
                        w=[sV.at()], inc=[SA_[0].inc()])
            sch.add("sync", "wait_ge", *SA_[0].at())
            seal(sch)
            sch.emit(nc)

    if stop_stage == 1:
        debug_copy_out(r1d)
        return nc

    with ExitStack() as es4:
        sb4 = lambda n, s, d: es4.enter_context(
            nc.sbuf_tensor(n, s, d, side="right"))

        with ExitStack() as es:
            sb = lambda n, s, d: es.enter_context(
                nc.sbuf_tensor(n, s, d, side="right"))
            xln = sb("xln", [128, KC_D * 1024], BF16)
            wfA = [sb(f"wfA{i}", [128, 2048], F32) for i in range(2)]
            wbA = [sb(f"wbA{i}", [128, 2048], BF16) for i in range(2)]
            qet = [sb(f"qet{i}", [128, 1024], BF16) for i in range(2)]
            q2t = [sb(f"q2t{i}", [128, 1024], BF16) for i in range(2)]
            vtb = sb("vtb", [128, 1024], BF16)
            vtm = [sb(f"vtm{i}", [128, 1024], BF16)
                   for i in range(2)]
            xtp = sb("xtp", [128, 1024], F32)
            rst4 = [sb(f"rst4{i}", [128, 1024], F32) for i in range(2)]
            sch = Sched()
            Sched.TAG[0] = "P7"
            phase_wait(sch)
            xld = {}
            for c in range(KC_D):
                p = c % 2
                wl = [] if c < 2 else [(sV.sem, xld[c - 2])]
                sch.add("sync", "dma_start", out=rst4[p][:, :],
                        in_=r1d[c * 128:(c + 1) * 128, :], w=wl,
                        inc=[LA[p].inc()])
                ld = LA[p].at()
                sch.add("vector", "tensor_mul", xtp[:, :], rst4[p][:, :],
                        bcA[:, :], w=[ld])
                sch.add("vector", "tensor_add",
                        xln[:, c * 1024:(c + 1) * 1024], xtp[:, :],
                        bcC[:, :], inc=[sV.inc()])
                xld[c] = sV.v
            xall = sV.at()
            psQ = [[PS[0], PS[1]], [PS[2], PS[3]]]
            psRM = [PS[4][0:40, :], PS[5][0:40, :]]
            psV = PS[6][:, :].bitcast(BF16)
            wcv = {}
            mmh = {}
            evd = {}
            rmd = {}
            vst = {}
            qsd = {}
            for m in range(48):
                p = m % 2
                src_w = wqr if m < 32 else (wkr if m < 40 else wvr)
                hh = m if m < 32 else (m - 32 if m < 40 else m - 40)
                pa = psQ[p]
                for i in range(2):
                    wl = []
                    if i == 1:
                        wl = [(sV.sem, wcv[(m, 0)])]
                    elif m >= 2:
                        wl = [(sV.sem, wcv[(m - 2, 1)])]
                    sch.add("gpsimd", "dma_start", out=wfA[p][:, :],
                            in_=src_w[hh][:, i * 16:(i + 1) * 16, :],
                            w=wl, inc=[W1[p].inc()])
                    w2l = [W1[p].at()]
                    if i == 1:
                        w2l.append((sT.sem, mmh[(m, 0)]))
                    elif m >= 2:
                        w2l.append((sT.sem, mmh[(m - 2, 1)]))
                    sch.add("vector", "tensor_copy", wbA[p][:, :],
                            wfA[p][:, :], w=w2l, inc=[sV.inc()])
                    wcv[(m, i)] = sV.v
                    wl3 = [(sV.sem, wcv[(m, i)])]
                    if i == 0:
                        if m == 0:
                            wl3.append(xall)
                        if m >= 2:
                            wl3.append((sA.sem, evd[m - 2]))
                    for h in range(2):
                        for kk in range(16):
                            kc = i * 16 + kk
                            sch.add("tensor", "matmul", pa[h][:, :],
                                    wbA[p][:, kk * 128:(kk + 1) * 128],
                                    xln[:, kc * 1024 + h * 512:
                                        kc * 1024 + (h + 1) * 512],
                                    start=(kc == 0), stop=(kc == KC_D - 1),
                                    w=(wl3 if (h == 0 and kk == 0) else []),
                                    inc=([sT.inc()]
                                         if (h == 1 and kk == 15) else []))
                    mmh[(m, i)] = sT.v
                if m < 40:
                    for h in range(2):
                        wl = []
                        if h == 0:
                            wl = [(sT.sem, mmh[(m, 1)])]
                            if m >= 2:
                                wl += [qsd[m - 2],
                                       (sT.sem, rmd.get(m - 2, 0))]
                        sch.add("scalar", "copy",
                                qet[p][:, h * 512:(h + 1) * 512],
                                pa[h][:, :], w=wl)
                        sch.add("scalar", "activation",
                                q2t[p][:, h * 512:(h + 1) * 512],
                                pa[h][:, :], AF.Square,
                                inc=([sA.inc()] if h == 1 else []))
                    evd[m] = sA.v
                    sch.add("sync", "dma_start",
                            out=qd[m * 128:(m + 1) * 128, :],
                            in_=qet[p][:, :], w=[(sA.sem, evd[m])],
                            inc=[QS[p].inc()])
                    qsd[m] = QS[p].at()
                    for mx in ([m - 1] if 1 <= m else []) + (
                            [m] if m == 39 else []):
                        if mx >= 40 or mx < 0:
                            continue
                        px = mx % 2
                        for h in range(2):
                            sch.add("tensor", "matmul", psRM[h],
                                    W40[:, 39 - mx:79 - mx],
                                    q2t[px][:, h * 512:(h + 1) * 512],
                                    start=(mx == 0), stop=(mx == 39),
                                    w=([(sA.sem, evd[mx])] if h == 0
                                       else []),
                                    inc=([sT.inc()] if h == 1 else []))
                        rmd[mx] = sT.v
                else:
                    hv = m - 40
                    for h in range(2):
                        wl = []
                        if h == 0:
                            wl = [(sT.sem, mmh[(m, 1)])]
                            if m > 40:
                                wl.append((sT.sem, vst[m - 1]))
                        sch.add("scalar", "copy",
                                vtb[:, h * 512:(h + 1) * 512], pa[h][:, :],
                                w=wl, inc=([sA.inc()] if h == 1 else []))
                    evd[m] = sA.v
                    for l in range(8):
                        wl = []
                        if l == 0:
                            wl = [(sA.sem, evd[m])]
                            if m > 40:
                                wl.append((sA.sem, vcp[m - 1]))
                        sch.add("tensor", "transpose",
                                psV[:, l * 128:(l + 1) * 128],
                                vtb[:, l * 128:(l + 1) * 128],
                                iden_bf[:, :], w=wl,
                                inc=([sT.inc()] if l == 7 else []))
                    vst[m] = sT.v
                    wl = [(sT.sem, vst[m])]
                    if m > 41:
                        wl.append(vkd[m - 2])
                    sch.add("scalar", "copy", vtm[m % 2][:, :], psV[:, :],
                            w=wl, inc=[sA.inc()])
                    vcp = vcp if m > 40 else {}
                    vcp[m] = sA.v
                    sch.add("sync", "dma_start",
                            out=kv_in.rearrange(
                                "(q kvh l t) c -> q kvh t l c",
                                q=2, kvh=8, l=8)[1, hv],
                            in_=vtm[m % 2][:, :], w=[(sA.sem, vcp[m])],
                            inc=[KVV.inc()])
                    vkd = vkd if m > 40 else {}
                    vkd[m] = KVV.at()
            seal(sch)
            sch.emit(nc)

        if stop_stage == 21:
            debug_dump_bf16(qd[0:4096, :])
            return nc

        osb = sb4("osb", [128, NQ * 1024], BF16)
        es4b = ExitStack()
        sb4b = lambda n, s, d: es4b.enter_context(
            nc.sbuf_tensor(n, s, d, side="right"))
        rstdr_bf = sb4b("rstdr_bf", [40, 1024], BF16)
        sel4 = sb4b("sel4", [40, 5120], BF16)
        mskb = sb4b("mskb", [128, 2048], BF16)
        with ExitStack() as es:
            sb = lambda n, s, d: es.enter_context(
                nc.sbuf_tensor(n, s, d, side="right"))
            rmsm = sb("rmsm", [40, 1024], F32)
            ket = [sb(f"ket{i}", [128, 1024], BF16) for i in range(2)]
            knt = [sb(f"knt{i}", [128, 1024], BF16) for i in range(2)]
            bcs = [sb(f"bcs{i}", [128, 1024], F32) for i in range(2)]
            self4 = sb("self4", [40, 5120], F32)
            sch = Sched()
            Sched.TAG[0] = "P8"
            phase_wait(sch)
            sch.add("sync", "dma_start", out=self4[:, :], in_=sel40D,
                    inc=[sSet.inc()])
            for a in range(2):
                sch.add("sync", "dma_start", out=bcs[a][:, :],
                        in_=masksD[:, a * 1024:(a + 1) * 1024],
                        inc=[sSet.inc()])
            mld1 = sSet.at()
            sch.add("vector", "tensor_copy", sel4[:, :], self4[:, :],
                    w=[sSet.at()], inc=[sV.inc()])
            for a in range(2):
                sch.add("vector", "tensor_copy",
                        mskb[:, a * 1024:(a + 1) * 1024], bcs[a][:, :],
                        w=[mld1], inc=[sV.inc()])
            mcv1 = sV.at()
            for i in range(2):
                sch.add("scalar", "mul", rmsm[:, i * 512:(i + 1) * 512],
                        psRM[i], 1.0 / 128.0,
                        inc=([sA.inc()] if i == 1 else []))
            sch.add("scalar", "activation", rmsm[:, :], rmsm[:, :], AF.Ln,
                    bias=epsc[0:40, 1:2])
            sch.add("scalar", "activation", rmsm[:, :], rmsm[:, :], AF.Exp,
                    scale=-0.5, inc=[sA.inc()])
            rk = sA.at()
            sch.add("vector", "tensor_copy", rstdr_bf[:, :], rmsm[:, :],
                    w=[rk], inc=[sV.inc()])
            rbf = sV.at()
            bcd = {}
            knd = {}
            ksd = {}
            for h in list(range(32, 40)) + list(range(32)):
                p = h % 2
                hseq = (h - 32) if h >= 32 else (h + 8)
                wl = [rbf] if hseq < 2 else [(sV.sem, knd[hseq - 2])]
                sch.add("sync", "dma_start", out=ket[p][:, :],
                        in_=qd[h * 128:(h + 1) * 128, :], w=wl,
                        inc=[QA[p].inc()])
                kld = QA[p].at()
                for i in range(2):
                    wl = []
                    if i == 0:
                        wl = [rbf] if hseq == 0 else [
                            (sA.sem, bcd[hseq - 1])]
                        if hseq >= 2:
                            wl.append((sV.sem, knd[hseq - 2]))
                    sch.add("tensor", "matmul", PS[0 + i][:, :],
                            sel4[:, h * 128:(h + 1) * 128],
                            rstdr_bf[:, i * 512:(i + 1) * 512],
                            start=True, stop=True, w=wl,
                            inc=([sT.inc()] if i == 1 else []))
                bmm = sT.v
                for i in range(2):
                    sch.add("scalar", "copy",
                            bcs[p][:, i * 512:(i + 1) * 512],
                            PS[0 + i][:, :],
                            w=([(sT.sem, bmm), mcv1] if i == 0 else []),
                            inc=([sA.inc()] if i == 1 else []))
                bcd[hseq] = sA.v
                wv = qw if h < 32 else kw
                sch.add("vector", "scalar_tensor_tensor", knt[p][:, :],
                        ket[p][:, :], wv[:, :], bcs[p][:, :],
                        ALU.mult, ALU.mult, w=[kld, (sA.sem, bcd[hseq])],
                        inc=[sV.inc()])
                knd[hseq] = sV.v
                if h < 32:
                    sch.add("sync", "dma_start",
                            out=qd[h * 128:(h + 1) * 128, :],
                            in_=knt[p][:, :], w=[(sV.sem, knd[hseq])],
                            inc=[QS[p].inc()])
                else:
                    sch.add("gpsimd", "dma_start",
                            out=kv_in.rearrange("(q hh l) c -> q hh l c",
                                                q=2, l=8)
                            [0, (h - 32) * 128:(h - 31) * 128],
                            in_=knt[p][:, :], w=[(sV.sem, knd[hseq])],
                            inc=[KV[p].inc()])
                    ksd[h] = KV[p].at()
                if h == 39:
                    sch.add("gpsimd", "collective_compute", "AllGather",
                            ALU.bypass,
                            replica_groups=[[0, 1], [2, 3], [4, 5], [6, 7]],
                            ins=[kv_in.opt()], outs=[kv_out.opt()],
                            w=[KV[0].at(), KV[1].at(), KVV.at()],
                            inc=[sCC.inc()])
            seal(sch)
            sch.emit(nc)

        if stop_stage == 22:
            debug_dump_bf16(
                kv_out.rearrange("(a b) c -> a (b c)", b=8)[0:4096, :])
            return nc

        with ExitStack() as es:
            sb = lambda n, s, d: es.enter_context(
                nc.sbuf_tensor(n, s, d, side="right"))
            kfl = [sb(f"kfl{i}", [128, 2048], BF16) for i in range(2)]
            vfl = [sb(f"vfl{i}", [128, 2048], BF16) for i in range(2)]
            pTA = sb("pTA", [128, 8 * 512], BF16)
            pTB = sb("pTB", [128, 16 * 512], BF16)
            qnb = [sb(f"qnb{i}", [128, 1024], BF16) for i in range(2)]
            acol = [sb(f"acol{i}", [128, 16], F32) for i in range(2)]
            ngr = sb("ngr", [2, 2048], BF16)
            rcb = sb("rcb", [2, 1024], BF16)
            rtmp = sb("rtmp", [2, 1024], F32)
            bco = [sb(f"bco{i}", [128, 512], F32) for i in range(2)]
            q2b = [sb(f"q2b{i}", [128, 1024], BF16) for i in range(2)]
            msqA = sb("msqA", [2, 1024], BF16)
            msqB = sb("msqB", [2, 1024], BF16)
            aqr = sb("aqr", [2, 1024], BF16)
            kwb = sb("kwb", [128, 2], BF16)
            c2m = sb("c2m", [2, 2], BF16)
            e10 = sb("e10", [2, 2], BF16)
            onesC = sb("onesC", [128, 2], BF16)
            ktmp = msqA[:, 0:128]
            c2rs = rcb[:, 0:128]
            sch = Sched()
            Sched.TAG[0] = "P9"
            phase_wait(sch)
            sch.add("vector", "memset", pTA[:, :], 0.0)
            sch.add("vector", "memset", pTB[:, :], 0.0)
            sch.add("vector", "memset", kwb[:, :], 0.0)
            sch.add("vector", "memset", c2m[:, :], 0.0)
            sch.add("vector", "memset", e10[:, :], 0.0)
            sch.add("vector", "memset", e10[0:1, 0:2], 1.0,
                    inc=[sV.inc()])
            sch.add("scalar", "activation", kwb[:, 0:1], kw[:, 0:1],
                    AF.Square, w=[(sV.sem, sV.v)], inc=[sA.inc()], lp=True)
            sch.add("tensor", "transpose",
                    PS[2][:, :].bitcast(BF16)[0:2, 0:128], kwb[:, :],
                    iden_bf[:, :], w=[(sA.sem, sA.v)], inc=[sT.inc()])
            sch.add("scalar", "copy", ktmp,
                    PS[2][:, :].bitcast(BF16)[0:2, 0:128],
                    w=[(sT.sem, sT.v)], inc=[sA.inc()], lp=True)
            ktmp_v = sA.v
            sch.add("vector", "reduce_max", c2m[:, 0:1], ktmp,
                    axis=AX.X, w=[(sA.sem, sA.v)], inc=[sV.inc()], lp=True)
            sch.add("tensor", "matmul", PS[0][0:2, 0:128], c2m[:, :],
                    ones2[:, 0:128], start=True, stop=True,
                    w=[(sV.sem, sV.v)], inc=[sT.inc()])
            sch.add("scalar", "copy", c2rs, PS[0][0:2, 0:128],
                    w=[(sT.sem, sT.v)], inc=[sA.inc()], lp=True)
            sch.add("tensor", "matmul", PS[1][:, 0:2], c2rs,
                    e10[:, :], start=True, stop=True,
                    w=[(sA.sem, sA.v)], inc=[sT.inc()])
            cb0 = sA.v
            sch.add("scalar", "copy", onesC[:, :], PS[1][:, 0:2],
                    w=[(sT.sem, sT.v)], inc=[sA.inc()], lp=True)
            cb1 = sA.v
            kvd = {}
            kvuse = {}
            qld = {}
            acd = {}
            p2use = [0, 0]
            ngruse = {}
            mbank_use = [cb0, cb1]
            q2use = [0, 0]
            sumuse = [0]
            ouse = {6: (sV.sem, 0), 2: (sA.sem, ktmp_v)}
            bc7use = [0]
            bcouse = [0, 0]
            pTuse = {0: 0, 1: 0}
            slopes = _alibi_slopes(NQ)
            NKV_RUN = 1 if stop_stage == 24 else NKV
            NG_RUN = 1 if stop_stage == 24 else 4
            GRPS = 1 if stop_stage == 24 else 2

            AVB = {0: 6, 1: 2}

            def s2_sr(hq, ph, pk, grp, ngr_v):
                pT = pTA if grp == 0 else pTB
                kcm = 8 * grp + 8
                LEAD = 2
                evs = []

                def red(kc):
                    lmin = max(4 * grp, kc // 2)
                    qs = (lmin - 4 * grp) * 128
                    wl = [(sA.sem, evs[kc])]
                    if kc == 0:
                        wl += [(sA.sem, sumuse[0]),
                               ouse[AVB[grp]], kvd[hq // 4]]
                    sch.add("tensor", "matmul", PS[5][0:2, qs:512],
                            onesc[:, :],
                            pT[:, kc * 512 + qs:(kc + 1) * 512],
                            start=(kc == 0), stop=(kc == kcm - 1), w=wl)
                    sch.add("tensor", "matmul", PS[AVB[grp]][:, :],
                            vfl[pk][:, kc * 128:(kc + 1) * 128],
                            pT[:, kc * 512:(kc + 1) * 512],
                            start=(kc == 0), stop=(kc == kcm - 1),
                            inc=([sT.inc()] if kc == kcm - 1 else []))

                for kc in range(kcm):
                    lmin = max(4 * grp, kc // 2)
                    qs = (lmin - 4 * grp) * 128
                    bk = kc % 2
                    p2b = PS[3 + bk]
                    wl = [(sA.sem, p2use[bk])]
                    if kc == 0:
                        wl += [(sV.sem, ngr_v), qld[hq], kvd[hq // 4]]
                    sl = (ph * 2 + grp) * 512
                    sch.add("tensor", "matmul", p2b[:, qs:512],
                            ones2n[:, :], ngr[:, sl + qs:sl + 512],
                            start=True, stop=False, w=wl)
                    sch.add("tensor", "matmul", p2b[:, qs:512],
                            kfl[pk][:, kc * 128:(kc + 1) * 128],
                            qnb[ph][:, grp * 512 + qs:grp * 512 + 512],
                            start=False, stop=True, inc=[sT.inc()])
                    smm = sT.v
                    need_mask = (kc >= 8 * grp)
                    exp_w = [(sT.sem, smm)]
                    if need_mask:
                        sch.add("vector", "tensor_add",
                                p2b[:, qs:qs + 128], p2b[:, qs:qs + 128],
                                mskb[:, kc * 128:(kc + 1) * 128],
                                w=[(sT.sem, smm)], inc=[sV.inc()])
                        exp_w = [(sV.sem, sV.v)]
                    wl = exp_w + [(sV.sem, acd[hq])]
                    if pTuse[grp]:
                        wl.append((sT.sem, pTuse[grp]))
                    sch.add("scalar", "activation",
                            pT[:, kc * 512 + qs:(kc + 1) * 512],
                            p2b[:, qs:512], AF.Exp,
                            bias=acol[ph][:, kc:kc + 1], w=wl,
                            inc=[sA.inc()])
                    p2use[bk] = sA.v
                    evs.append(sA.v)
                    if kc >= LEAD:
                        red(kc - LEAD)
                ngruse[(ph, grp)] = sT.v
                for kc in range(kcm - LEAD, kcm):
                    red(kc)
                avd = sT.v
                pTuse[grp] = avd
                kvuse[hq // 4] = avd
                sch.add("scalar", "activation",
                        rtmp[:, grp * 512:(grp + 1) * 512], PS[5][0:2, :],
                        AF.Ln, w=[(sT.sem, avd)], inc=[sA.inc()])
                sumuse[0] = sA.v
                sch.add("scalar", "activation",
                        rcb[:, grp * 512:(grp + 1) * 512],
                        rtmp[:, grp * 512:(grp + 1) * 512], AF.Exp,
                        scale=-1.0, inc=[sA.inc()], lp=True)
                return avd, sA.v

            def s2_evac(hq, grp, rc_v):
                sch.add("tensor", "matmul", PS[7][:, :], ones2[:, :],
                        rcb[:, grp * 512:(grp + 1) * 512], start=True,
                        stop=True,
                        w=[(sA.sem, rc_v), (sV.sem, bcouse[grp]),
                           (sA.sem, bc7use[0])],
                        inc=[sT.inc()])
                bc_mm = sT.v
                sch.add("scalar", "copy", bco[grp][:, :], PS[7][:, :],
                        w=[(sT.sem, bc_mm)], inc=[sA.inc()])
                bcod = sA.v
                bc7use[0] = sA.v
                sch.add("vector", "tensor_mul",
                        osb[:, hq * 1024 + grp * 512:
                            hq * 1024 + (grp + 1) * 512],
                        PS[AVB[grp]][:, :], bco[grp][:, :],
                        w=[(sA.sem, bcod)], inc=[sV.inc()])
                ouse[AVB[grp]] = (sV.sem, sV.v)
                return sV.v

            NH = NKV_RUN * NG_RUN

            def kv_load(kv):
                pk = kv % 2
                wl = [(sCC.sem, 1)]
                if kv >= 2:
                    wl.append((sT.sem, kvuse[kv - 2]))
                for par in range(2):
                    sch.add("gpsimd", "dma_start",
                        out=kfl[pk][:, :].rearrange(
                            "p (l q c) -> p l q c", l=8, q=2)[:, :, par, :],
                        in_=kv_out.rearrange(
                            "(q hh l) c -> q hh l c", q=2, l=8)
                        [par, kv * 128:(kv + 1) * 128],
                        w=(wl if par == 0 else []), inc=[KF[pk].inc()])
                    sch.add("gpsimd", "dma_start",
                        out=vfl[pk][:, :].rearrange(
                            "p (l q c) -> p l q c", l=8, q=2)[:, :, par, :],
                        in_=kv_out.rearrange(
                            "(q x kvh l t) c -> q x kvh t l c",
                            q=2, x=2, kvh=8, l=8)[par, 1, kv],
                        inc=[KF[pk].inc()])
                kvd[kv] = KF[pk].at()

            def prep_head(hq):
                ph = hq % 2
                pk = (hq // 4) % 2
                if hq % 4 == 0:
                    kv_load(hq // 4)
                wl = []
                if hq >= 2:
                    wl.append((sT.sem, pTuse[0]))
                    wl.append((sT.sem, pTuse[1]))
                sch.add("sync", "dma_start", out=qnb[ph][:, :],
                        in_=qd[hq * 128:(hq + 1) * 128, :], w=wl,
                        inc=[QA[ph].inc()])
                qld[hq] = QA[ph].at()
                wl = []
                if hq >= 2:
                    wl += [(sA.sem, p2use[0]), (sA.sem, p2use[1])]
                sch.add("vector", "tensor_scalar_mul", acol[ph][:, :],
                        iotac[:, :], float(slopes[hq]), w=wl,
                        inc=[sV.inc()])
                acd[hq] = sV.v
                sch.add("vector", "tensor_scalar_mul", aqr[:, :],
                        qposs[:, :], float(slopes[hq]), lp=True)
                sch.add("scalar", "activation", q2b[ph][:, :],
                        qnb[ph][:, :], AF.Square,
                        w=[qld[hq], (sT.sem, q2use[ph])],
                        inc=[sA.inc()], lp=True)
                sq_v = sA.v
                ngv = []
                for grp in range(GRPS):
                    sl = (ph * 2 + grp) * 512
                    sch.add("tensor", "matmul", PS[grp][0:2, 0:512],
                            onesC[:, :],
                            q2b[ph][:, grp * 512:(grp + 1) * 512],
                            start=True, stop=True,
                            w=[(sA.sem, sq_v), (sA.sem, mbank_use[grp])],
                            inc=[sT.inc()])
                    q2use[ph] = sT.v
                    sch.add("scalar", "activation",
                            msqA[:, grp * 512:(grp + 1) * 512],
                            PS[grp][0:2, 0:512], AF.Ln, scale=133.12,
                            w=[(sT.sem, sT.v)], inc=[sA.inc()], lp=True)
                    mbank_use[grp] = sA.v
                    sch.add("scalar", "activation",
                            msqB[:, grp * 512:(grp + 1) * 512],
                            msqA[:, grp * 512:(grp + 1) * 512], AF.Exp,
                            scale=0.5, inc=[sA.inc()], lp=True)
                    sch.add("vector", "tensor_add", ngr[:, sl:sl + 512],
                            msqB[:, grp * 512:(grp + 1) * 512],
                            aqr[:, grp * 512:(grp + 1) * 512],
                            w=[(sA.sem, sA.v),
                               (sT.sem, ngruse.get((ph, grp), 0))],
                            inc=[sV.inc()], lp=True)
                    ngv.append(sV.v)
                return ngv

            ngs_q = {0: prep_head(0)}
            e1_pend = None
            for hq in range(NH):
                ph = hq % 2
                pk = (hq // 4) % 2
                if e1_pend is not None:
                    bcouse[1] = s2_evac(e1_pend[0], 1, e1_pend[1])
                    e1_pend = None
                if hq + 1 < NH:
                    ngs_q[hq + 1] = prep_head(hq + 1)
                ngs = ngs_q.pop(hq)
                if GRPS == 1:
                    avda, rca = s2_sr(hq, ph, pk, 0, ngs[0])
                    bcouse[0] = s2_evac(hq, 0, rca)
                else:
                    avda, rca = s2_sr(hq, ph, pk, 0, ngs[0])
                    avdb, rcb_v = s2_sr(hq, ph, pk, 1, ngs[1])
                    bcouse[0] = s2_evac(hq, 0, rca)
                    e1_pend = (hq, rcb_v)
            if e1_pend is not None:
                bcouse[1] = s2_evac(e1_pend[0], 1, e1_pend[1])
            seal(sch)
            sch.emit(nc)

        if stop_stage == 23:
            with ExitStack() as es:
                rf = es.enter_context(nc.sbuf_tensor(
                    "rdbg_f2", [128, 1024], F32, side="right"))
                sch = Sched()
                Sched.TAG[0] = "DBGO"
                phase_wait(sch)
                for c in range(KC_D):
                    sch.add("vector", "tensor_copy", rf[:, :],
                            osb[:, c * 1024:(c + 1) * 1024],
                            w=([] if c == 0 else [SA_[0].at()]),
                            inc=[sV.inc()])
                    sch.add("sync", "dma_start",
                            out=outT[c * 128:(c + 1) * 128, :], in_=rf[:, :],
                            w=[sV.at()], inc=[SA_[0].inc()])
                sch.add("sync", "wait_ge", *SA_[0].at())
                seal(sch)
                sch.emit(nc)
            return nc

        es4b.close()
        with ExitStack() as es:
            sb = lambda n, s, d: es.enter_context(
                nc.sbuf_tensor(n, s, d, side="right"))
            asb4 = sb("asb4", [66, 1024], F32)
            wfO = [sb(f"wfO{i}", [128, 2048], F32) for i in range(2)]
            wbO = [sb(f"wbO{i}", [128, 2048], BF16) for i in range(2)]
            rst5 = [sb(f"rst5{i}", [128, 1024], F32) for i in range(2)]
            fold5d = sb("fold5d", [128, KC_F * 1024], F32)
            r2c = [sb(f"r2c{i}", [128, 1024], F32) for i in range(2)]
            xb5 = [sb(f"xb5{i}", [128, 1024], BF16) for i in range(2)]
            sq5 = [sb(f"sq5{i}", [128, 1024], BF16) for i in range(2)]
            sch = Sched()
            Sched.TAG[0] = "P10"
            phase_wait(sch)
            wps = [[PS[0], PS[1]], [PS[2], PS[3]]]
            sa_ps = [PS[4][0:66, :], PS[5][0:66, :]]
            sq_ps = [PS[6][0:2, :], PS[7][0:2, :]]
            wcv = {}
            mmh = {}
            mmd = {}
            evd = {}
            std2 = {}
            sqd2 = {}
            cvd2 = {}
            sqa2 = {}
            fdd = {}
            stod = {}
            for mc in range(KC_D):
                p = mc % 2
                pa = wps[p]
                for i in range(2):
                    wl = []
                    if i == 1:
                        wl = [(sA.sem, wcv[(mc, 0)])]
                    elif mc >= 2:
                        wl = [(sA.sem, wcv[(mc - 2, 1)])]
                    sch.add("gpsimd", "dma_start", out=wfO[p][:, :],
                            in_=wor[mc][:, i * 16:(i + 1) * 16, :], w=wl,
                            inc=[W1[p].inc()])
                    w2l = [W1[p].at()]
                    if i == 1:
                        w2l.append((sT.sem, mmh[(mc, 0)]))
                    elif mc >= 2:
                        w2l.append((sT.sem, mmh[(mc - 2, 1)]))
                    sch.add("scalar", "copy", wbO[p][:, :],
                            wfO[p][:, :], w=w2l, inc=[sA.inc()])
                    wcv[(mc, i)] = sA.v
                    wl3 = [(sA.sem, wcv[(mc, i)])]
                    if i == 0 and mc >= 2:
                        wl3.append((sV.sem, evd[mc - 2]))
                    for h in range(2):
                        for kk in range(16):
                            kc = i * 16 + kk
                            sch.add("tensor", "matmul", pa[h][:, :],
                                    wbO[p][:, kk * 128:(kk + 1) * 128],
                                    osb[:, kc * 1024 + h * 512:
                                        kc * 1024 + (h + 1) * 512],
                                    start=(kc == 0), stop=(kc == KC_D - 1),
                                    w=(wl3 if (h == 0 and kk == 0) else []),
                                    inc=([sT.inc()]
                                         if (h == 1 and kk == 15) else []))
                    mmh[(mc, i)] = sT.v
                mmd[mc] = sT.v
                wl = [] if mc < 2 else [(sV.sem, evd[mc - 2])]
                sch.add("sync", "dma_start", out=rst5[p][:, :],
                        in_=r1d[mc * 128:(mc + 1) * 128, :], w=wl,
                        inc=[LA[p].inc()])
                r1ld = LA[p].at()
                for h in range(2):
                    wl = []
                    if h == 0:
                        wl = [(sT.sem, mmd[mc]), r1ld]
                        if mc >= 2:
                            wl += [stod[mc - 2],
                                   (sV.sem, cvd2[mc - 2])]
                    sch.add("vector", "tensor_add",
                            r2c[p][:, h * 512:(h + 1) * 512], pa[h][:, :],
                            rst5[p][:, h * 512:(h + 1) * 512], w=wl,
                            inc=([sV.inc()] if h == 1 else []))
                evd[mc] = sV.v
                sch.add("sync", "dma_start",
                        out=r2d[mc * 128:(mc + 1) * 128, :],
                        in_=r2c[p][:, :], w=[(sV.sem, evd[mc])],
                        inc=[SA_[p].inc()])
                stod[mc] = SA_[p].at()
                fc5 = mc % KC_F
                if mc < KC_F:
                    sch.add("vector", "tensor_copy",
                            fold5d[:, fc5 * 1024:(fc5 + 1) * 1024],
                            r2c[p][:, :])
                else:
                    sch.add("vector", "tensor_add",
                            fold5d[:, fc5 * 1024:(fc5 + 1) * 1024],
                            fold5d[:, fc5 * 1024:(fc5 + 1) * 1024],
                            r2c[p][:, :])
                wl = [(sV.sem, evd[mc])]
                if mc >= 2:
                    wl += [(sT.sem, std2.get(mc - 2, 0))]
                sch.add("vector", "tensor_copy", xb5[p][:, :], r2c[p][:, :],
                        w=wl, inc=[sV.inc()])
                cvd2[mc] = sV.v
                wl = []
                if mc >= 2:
                    wl.append((sT.sem, sqd2.get(mc - 2, 0)))
                sch.add("vector", "tensor_mul", sq5[p][:, :], xb5[p][:, :],
                        xb5[p][:, :], w=wl, inc=[sV.inc()], lp=True)
                sqa2[mc] = sV.v
                for mcx in ([mc - 1] if mc >= 1 else []) + (
                        [mc] if mc == KC_D - 1 else []):
                    px = mcx % 2
                    for i in range(2):
                        sch.add("tensor", "matmul", sa_ps[i],
                                satb[:, mcx * 66:(mcx + 1) * 66],
                                xb5[px][:, i * 512:(i + 1) * 512],
                                start=(mcx == 0), stop=(mcx == KC_D - 1),
                                w=([(sV.sem, cvd2[mcx])] if i == 0 else []),
                                inc=([sT.inc()] if i == 1 else []))
                    std2[mcx] = sT.v
                    for i in range(2):
                        sch.add("tensor", "matmul", sq_ps[i], onesc[:, :],
                                sq5[px][:, i * 512:(i + 1) * 512],
                                start=(mcx == 0), stop=(mcx == KC_D - 1),
                                w=([(sV.sem, sqa2[mcx])] if i == 0 else []),
                                inc=([sT.inc()] if i == 1 else []))
                    sqd2[mcx] = sT.v
                fdd[mc] = sV.v
            bc = ln_post(sch, sa_ps, sq_ps, eps=1e-5, cmul=-4.0,
                         pb0=PS[0], pb1=PS[1], stats_done=(sT.sem, sT.v),
                         first_w=[(sV.sem, sV.v)], fold=fold5d,
                         tfa=rst5[0])
            ln_post_lora(sch, sa_ps, PS[0], PS[1], scale=0.5, bc_done=bc,
                         asb=asb4, tmp66=rst5[1][0:66, :])
            seal(sch)
            sch.emit(nc)

    if stop_stage == 2:
        debug_copy_out(r2d)
        return nc

    ffn_phase(r2d, outT, is_final=True, tag="f2",
              dump_after_p2=(stop_stage == 28))
    if stop_stage == 28:
        return nc

    sch = Sched()
    Sched.TAG[0] = "P12"
    phase_wait(sch)
    for s in ALL_STORE:
        if s.v > 0:
            sch.add("sync", "wait_ge", *s.at())
    sch.emit(nc)
    return nc


def _prep_weights(inputs):
    f = np.float32
    Wq = np.asarray(inputs["L0_attn_Wq"], f)
    Wk = np.asarray(inputs["L0_attn_Wk"], f)
    Wv = np.asarray(inputs["L0_attn_Wv"], f)
    Wo = np.asarray(inputs["L0_attn_Wo"], f)
    w1q = np.asarray(inputs["L0_ffn_w1q"], f)
    a1 = np.asarray(inputs["L0_ffn_a1"], f)
    b1 = np.asarray(inputs["L0_ffn_b1"], f)
    w3q = np.asarray(inputs["L0_ffn_w3q"], f)
    a3 = np.asarray(inputs["L0_ffn_a3"], f)
    b3 = np.asarray(inputs["L0_ffn_b3"], f)
    w2q = np.asarray(inputs["L0_ffn_w2q"], f)
    a2 = np.asarray(inputs["L0_ffn_a2"], f)
    b2 = np.asarray(inputs["L0_ffn_b2"], f)

    def tile4(wT, nm, nk):
        K, M = wT.shape
        assert K == nk * 128 and M == nm * 128
        return np.ascontiguousarray(
            wT.reshape(nk, 128, nm, 128).transpose(2, 1, 0, 3))

    wqr = tile4(np.ascontiguousarray(Wq.T), NQ, KC_D)
    wkr = tile4(np.ascontiguousarray(Wk.T), NKV, KC_D)
    wvr = tile4(np.ascontiguousarray(Wv.T), NKV, KC_D)
    wor = tile4(np.ascontiguousarray(Wo.T), KC_D, KC_D)

    def padh_cols(m):
        R = m.shape[0]
        z = np.zeros((R, 4, 2816), f)
        z[:, :, :2752] = m.reshape(R, 4, 2752)
        return z.reshape(R, HP)

    w1p = padh_cols(np.ascontiguousarray(w1q.T))
    w3p = padh_cols(np.ascontiguousarray(w3q.T))
    b1p = padh_cols(np.ascontiguousarray(b1.T))
    b3p = padh_cols(np.ascontiguousarray(b3.T))
    a2p = padh_cols(a2)
    w2p = np.zeros((2816, D), f)
    w2p[:2752] = w2q.T
    w1r = tile4(w1p, MC_H, KC_F)
    w3r = tile4(w3p, MC_H, KC_F)
    w2r = tile4(w2p, KC_D, KC_H)
    a2S = np.ascontiguousarray(
        a2p.reshape(2, MC_H, 128).transpose(2, 1, 0).reshape(128, 2 * MC_H))
    b2T = np.ascontiguousarray(b2.T)

    SaTa = np.zeros((D, 66), f)
    SaTa[:, 0] = 1.0
    SaTa[:, 1] = 1.0
    SaTa[:, 32] = a1[0]
    SaTa[:, 33] = a1[1]
    SaTa[:, 64] = a3[0]
    SaTa[:, 65] = a3[1]

    sel40 = np.zeros((40, 40 * 128), f)
    for h in range(40):
        sel40[h, h * 128:(h + 1) * 128] = 1.0
    iotacv = (np.arange(128, dtype=f)[:, None]
              + 128.0 * np.arange(16, dtype=f)[None, :])
    return dict(
        wqr=wqr, wkr=wkr, wvr=wvr, wor=wor, w1r=w1r, w3r=w3r, w2r=w2r,
        b1T=np.ascontiguousarray(b1p), b3T=np.ascontiguousarray(b3p),
        b2T=b2T, a2S=a2S, SaT=SaTa,
        qw=np.ascontiguousarray(
            np.asarray(inputs["L0_attn_q_norm_w"], f).reshape(128, 1)),
        kw=np.ascontiguousarray(
            np.asarray(inputs["L0_attn_k_norm_w"], f).reshape(128, 1)),
        sel40=sel40,
        iden=np.eye(128, dtype=f), iotac=np.ascontiguousarray(iotacv))


def _core_masks(parity):
    f = np.float32
    tri = np.where(np.arange(128)[:, None] <= np.arange(128)[None, :],
                   0.0, -1e30).astype(f)
    m = np.empty((128, 16 * 128), f)
    for kc in range(16):
        cc = kc % 2
        if cc == parity:
            t = tri
        elif parity == 0:
            t = np.full((128, 128), -1e30, f)
        else:
            t = np.zeros((128, 128), f)
        m[:, kc * 128:(kc + 1) * 128] = t
    return m


_BLOCKS = {p: [p + 2 * i for i in range(8)] for p in (0, 1)}


def kernel(_stop=3, **inputs):
    x = np.asarray(inputs["L0_attn_x"], np.float32)
    shared = _prep_weights(inputs)
    masks = {p: _core_masks(p) for p in (0, 1)}
    qpos = {}
    for p in (0, 1):
        qp = np.zeros((2, T), np.float32)
        qp[0] = (128 * np.repeat(np.array(_BLOCKS[p], np.float32), 128)
                 + np.tile(np.arange(128, dtype=np.float32), 8))
        qpos[p] = qp
    in_maps = []
    for core in range(8):
        b, p = core // 2, core % 2
        blocks = _BLOCKS[p]
        xs = x[b].reshape(16, 128, D)[blocks].reshape(T, D)
        m = dict(shared)
        m["xT"] = np.ascontiguousarray(xs.T)
        m["masks"] = masks[p]
        m["qpos"] = qpos[p]
        in_maps.append(m)
    nc = build_nc(stop_stage=_stop)
    res = run_bass_kernel_spmd(nc, in_maps, core_ids=list(range(8)))
    out = np.empty((4, 2048, D), np.float32)
    for core in range(8):
        b, p = core // 2, core % 2
        blocks = _BLOCKS[p]
        oT = res.results[core]["outT"]
        out[b].reshape(16, 128, D)[blocks] = (
            oT.T.reshape(8, 128, D))
    return out


# version 15
# speedup vs baseline: 1.3019x; 1.0057x over previous
import math
from contextlib import ExitStack
import numpy as np
import concourse.bass as bass
import concourse.mybir as mybir
from concourse.bass_utils import run_bass_kernel_spmd

F32 = mybir.dt.float32
BF16 = mybir.dt.bfloat16
AF = mybir.ActivationFunctionType
ALU = mybir.AluOpType
AX = mybir.AxisListType

D = 4096
T = 1024
S = 2048
NB = 8
HP = 11264
NKV = 8
NQ = 32
HD = 128
KC_D = 32
KC_F = 8
MC_H = 88
KC_H = 22
ISQ = 1.0 / math.sqrt(HD)


def _alibi_slopes(n):
    start = 2 ** (-2 ** (-(math.log2(n) - 3)))
    return np.array([start * start ** i for i in range(n)], dtype=np.float32)


class Sched:
    ENGINES = ("gpsimd", "sync", "tensor", "vector", "scalar")

    def __init__(self):
        self.L = {e: [] for e in self.ENGINES}

    TAG = [""]

    def add(self, eng, meth, *a, w=(), inc=(), lp=False, **k):
        self.L[eng].append([meth, a, k, list(w), list(inc), lp,
                            f"{self.TAG[0]}#{len(self.L[eng])}"])

    def emit(self, nc):
        from contextlib import nullcontext
        with nc.Block() as blk:
            for eng in self.ENGINES:
                if not self.L[eng]:
                    continue

                def runner(e, L=self.L[eng]):
                    for (meth, a, k, w, inc, lp, tag) in L:
                        for (sem, v) in w:
                            e.wait_ge(sem, v)
                        ctx = (nc.allow_low_precision("bf16 matmul operand")
                               if lp else nullcontext())
                        with ctx:
                            inst = getattr(e, meth)(*a, **k)
                        try:
                            inst.annotate(tag)
                        except Exception:
                            pass
                        for (sem, n) in inc:
                            inst.then_inc(sem, n)
                getattr(blk, eng)(runner)


class S16:

    def __init__(self, nc, name):
        self.sem = nc.alloc_semaphore(name)
        self.v = 0

    def inc(self):
        self.v += 16
        return (self.sem, 16)

    def at(self):
        return (self.sem, self.v)


class Ctr:
    def __init__(self, sem):
        self.sem = sem
        self.v = 0

    def inc(self, n=1):
        self.v += n
        return (self.sem, n)

    def at(self):
        return (self.sem, self.v)


def build_nc(stop_stage=3):
    nc = bass.Bass("TRN2", target_bir_lowering=False,
                   detect_race_conditions=False)

    def din(name, shape, dt=F32):
        return nc.dram_tensor(name, shape, dt, kind="ExternalInput").ap()

    xT = din("xT", [D, T])
    wqr = din("wqr", [NQ, 128, KC_D, 128])
    wkr = din("wkr", [NKV, 128, KC_D, 128])
    wvr = din("wvr", [NKV, 128, KC_D, 128])
    wor = din("wor", [KC_D, 128, KC_D, 128])
    w1r = din("w1r", [MC_H, 128, KC_F, 128])
    w3r = din("w3r", [MC_H, 128, KC_F, 128])
    w2r = din("w2r", [KC_D, 128, KC_H, 128])
    b1T = din("b1T", [2, HP])
    b3T = din("b3T", [2, HP])
    b2T = din("b2T", [2, D])
    a2S = din("a2S", [128, 2 * MC_H])
    SaT = din("SaT", [D, 66])
    qwD = din("qw", [128, 1])
    kwD = din("kw", [128, 1])
    masksD = din("masks", [128, 16 * 128])
    qposD = din("qpos", [2, T])
    sel40D = din("sel40", [40, 40 * 128])
    idenD = din("iden", [128, 128])
    iotacD = din("iotac", [128, 16])

    outT = nc.dram_tensor("outT", [D, T], F32, kind="ExternalOutput").ap()
    r1d = nc.dram_tensor("r1d", [D, T], F32, kind="Internal").ap()
    r2d = nc.dram_tensor("r2d", [D, T], F32, kind="Internal").ap()
    qd = nc.dram_tensor("qd", [40 * 128, T], BF16, kind="Internal").ap()
    kv_inK = nc.dram_tensor("kv_inK", [8192, 128], BF16,
                            kind="Internal").ap()
    kv_inV = nc.dram_tensor("kv_inV", [8192, 128], BF16,
                            kind="Internal").ap()
    kv_outK = nc.dram_tensor("kv_outK", [16384, 128], BF16,
                             kind="Internal").ap()
    kv_outV = nc.dram_tensor("kv_outV", [16384, 128], BF16,
                             kind="Internal").ap()

    sT = Ctr(nc.alloc_semaphore("sT"))
    sV = Ctr(nc.alloc_semaphore("sV"))
    sA = Ctr(nc.alloc_semaphore("sA"))
    sW = Ctr(nc.alloc_semaphore("sW"))
    sR = Ctr(nc.alloc_semaphore("sR"))
    sO = Ctr(nc.alloc_semaphore("sO"))
    sCC = Ctr(nc.alloc_semaphore("sCC"))
    sB = Ctr(nc.alloc_semaphore("sB"))
    sSet = S16(nc, "sSet")
    LA = [S16(nc, f"LA{i}") for i in range(2)]
    QA = [S16(nc, f"QA{i}") for i in range(2)]
    W1 = [S16(nc, f"W1{i}") for i in range(2)]
    W3 = [S16(nc, f"W3{i}") for i in range(2)]
    KF = [S16(nc, f"KF{i}") for i in range(2)]
    SA_ = [S16(nc, f"SA{i}") for i in range(2)]
    QS = [S16(nc, f"QS{i}") for i in range(2)]
    KV = [S16(nc, f"KV{i}") for i in range(2)]
    KVV = S16(nc, "KVV")
    ALL_STORE = SA_ + QS + KV + [KVV]

    AL = nc.alloc_sbuf_tensor
    ones2 = AL("ones2", [2, 128], BF16, side="left")
    ones2n = AL("ones2n", [2, 128], BF16, side="left")
    onesc = AL("onesc", [128, 2], BF16, side="left")
    iden_bf = AL("iden_bf", [128, 128], BF16, side="left")
    qw = AL("qw_s", [128, 1], F32, side="left")
    kw = AL("kw_s", [128, 1], F32, side="left")
    iotac = AL("iotac_s", [128, 16], F32, side="left")
    u_bf = AL("u_bf", [66, 1024], BF16, side="left")
    u2_bf = AL("u2_bf", [2, 1024], BF16, side="left")
    bcA = AL("bcA", [128, 1024], F32, side="left")
    bcC = AL("bcC", [128, 1024], F32, side="left")
    mrow = AL("mrow", [2, 1024], F32, side="left")
    mrow_bf = AL("mrow_bf", [2, 1024], BF16, side="left")
    vrow = AL("vrow", [2, 1024], F32, side="left")
    rsr = AL("rsr", [2, 1024], F32, side="left")
    rsr_bf = AL("rsr_bf", [2, 1024], BF16, side="left")
    crow = AL("crow", [2, 1024], F32, side="left")
    crow_bf = AL("crow_bf", [2, 1024], BF16, side="left")
    negsA = AL("negsA", [66, 1], F32, side="left")
    xfb = AL("xfb", [128, KC_F * 1024], BF16, side="left")
    satb = AL("satb", [128, KC_D * 66], BF16, side="left")
    W40 = AL("W40", [128, 79], BF16, side="left")
    epsc = AL("epsc", [128, 2], F32, side="left")
    qposs = AL("qposs", [2, 1024], F32, side="left")

    PS = [nc.alloc_psum_tensor(f"ps{i}", [128, 512], F32) for i in range(8)]

    BARP = [[]]
    OWN = {"tensor": sT, "vector": sV, "scalar": sA}

    def seal(sch):
        pairs = [(sSet.sem, sSet.v)]
        pairs += [s.at() for s in ALL_STORE if s.v > 0]
        for eng in ("tensor", "vector", "scalar"):
            if sch.L[eng]:
                last = sch.L[eng][-1]
                if not last[4]:
                    last[4].append(OWN[eng].inc())
                pairs.append(OWN[eng].at())
        BARP[0] = [p for p in pairs if p[1] > 0]

    def phase_wait(sch):
        for eng in Sched.ENGINES:
            for (sem, v) in BARP[0]:
                sch.add(eng, "wait_ge", sem, v)

    def ln_post(sch, sa_ps, sq_ps, eps, cmul, pb0, pb1, stats_done,
                first_w=(), fold=None, tfa=None):
        n = float(D)
        for i in range(2):
            sch.add("scalar", "mul", mrow[:, i * 512:(i + 1) * 512],
                    sa_ps[i][0:2, :], 1.0 / n,
                    w=([stats_done] if i == 0 else []),
                    inc=([sA.inc()] if i == 1 else []))
        m_done = sA.at()
        sch.add("vector", "tensor_copy", mrow_bf[:, :], mrow[:, :], w=[m_done])
        sch.add("vector", "tensor_mul", vrow[:, :], mrow[:, :], mrow[:, :])
        for i in range(2):
            sch.add("vector", "scalar_tensor_tensor",
                    rsr[:, i * 512:(i + 1) * 512], sq_ps[i][0:2, :], 1.0 / n,
                    vrow[:, i * 512:(i + 1) * 512], ALU.mult, ALU.subtract,
                    inc=([sV.inc()] if i == 1 else []))
        v_done = sV.at()
        sch.add("scalar", "activation", rsr[:, :], rsr[:, :], AF.Ln,
                bias=epsc[0:2, 0:1], w=[v_done])
        sch.add("scalar", "activation", rsr[:, :], rsr[:, :], AF.Exp,
                scale=-0.5, inc=[sA.inc()])
        r_done = sA.at()
        sch.add("vector", "tensor_copy", rsr_bf[:, :], rsr[:, :], w=[r_done])
        sch.add("vector", "tensor_mul", crow[:, :], mrow[:, :], rsr[:, :])
        sch.add("vector", "tensor_scalar_mul", crow[:, :], crow[:, :],
                float(cmul))
        sch.add("vector", "tensor_copy", crow_bf[:, :], crow[:, :],
                inc=[sV.inc()])
        cb_done = sV.at()
        for i in range(2):
            wl0 = ([cb_done] + list(first_w)) if i == 0 else [sA.at()]
            sch.add("tensor", "matmul", pb0[:, :], ones2[:, :],
                    rsr_bf[:, i * 512:(i + 1) * 512], start=True, stop=True,
                    w=wl0)
            sch.add("tensor", "matmul", pb1[:, :], ones2[:, :],
                    crow_bf[:, i * 512:(i + 1) * 512], start=True, stop=True,
                    inc=[sT.inc()])
            t_done = sT.at()
            sch.add("scalar", "copy", bcA[:, i * 512:(i + 1) * 512],
                    pb0[:, :], w=[t_done])
            sch.add("scalar", "copy", bcC[:, i * 512:(i + 1) * 512],
                    pb1[:, :], inc=[sA.inc()])
        ab_done = sA.at()
        if fold is not None:
            for c in range(KC_F):
                sch.add("vector", "tensor_mul", tfa[:, :],
                        fold[:, c * 1024:(c + 1) * 1024], bcA[:, :],
                        w=([ab_done] if c == 0 else []))
                sch.add("vector", "tensor_add",
                        xfb[:, c * 1024:(c + 1) * 1024], tfa[:, :], bcC[:, :],
                        inc=([sV.inc()] if c == KC_F - 1 else []))
        return ab_done

    def ln_post_lora(sch, sa_ps, pb0, pb1, scale, bc_done, asb, tmp66):
        for i in range(2):
            sch.add("scalar", "copy", asb[:, i * 512:(i + 1) * 512],
                    sa_ps[i][:, :], inc=([sA.inc()] if i == 1 else []))
        a_done = sA.at()
        mb = []
        for i in range(2):
            pb = pb0 if i == 0 else pb1
            sch.add("tensor", "matmul", pb[0:66, :], ones2[:, 0:66],
                    mrow_bf[:, i * 512:(i + 1) * 512], start=True, stop=True,
                    w=([bc_done] if i == 0 else []), inc=[sT.inc()])
            mb.append(sT.at())
        for i in range(2):
            pb = pb0 if i == 0 else pb1
            sch.add("vector", "scalar_tensor_tensor",
                    tmp66[:, i * 512:(i + 1) * 512], pb[0:66, :], negsA[:, :],
                    asb[:, i * 512:(i + 1) * 512], ALU.mult, ALU.add,
                    w=[a_done, mb[i]], inc=[sV.inc()])
        tdone = sV.at()
        rb = []
        for i in range(2):
            pb = pb0 if i == 0 else pb1
            sch.add("tensor", "matmul", pb[0:66, :], ones2[:, 0:66],
                    rsr_bf[:, i * 512:(i + 1) * 512], start=True, stop=True,
                    w=([tdone] if i == 0 else []), inc=[sT.inc()])
            rb.append(sT.at())
        for i in range(2):
            pb = pb0 if i == 0 else pb1
            sch.add("vector", "scalar_tensor_tensor",
                    u_bf[:, i * 512:(i + 1) * 512],
                    tmp66[:, i * 512:(i + 1) * 512], float(scale),
                    pb[0:66, :], ALU.mult, ALU.mult, w=[rb[i]],
                    inc=([sV.inc()] if i == 1 else []))

    with ExitStack() as es:
        stg = es.enter_context(nc.sbuf_tensor("stg_set", [128, 2240], F32,
                                              side="right"))
        sch = Sched()
        Sched.TAG[0] = "P1"
        sch.add("vector", "memset", ones2[:, :], 0.0)
        sch.add("vector", "memset", ones2[0:1, :], 1.0)
        sch.add("vector", "memset", ones2n[:, :], 0.0)
        sch.add("vector", "memset", ones2n[0:1, :], -1.0)
        sch.add("vector", "memset", onesc[:, :], 1.0)
        sch.add("vector", "memset", W40[:, :], 0.0)
        sch.add("vector", "memset", W40[:, 39:40], 1.0)
        sch.add("vector", "memset", epsc[:, 0:1], 1e-5)
        sch.add("vector", "memset", epsc[:, 1:2], 1e-6)
        sch.add("sync", "dma_start", out=stg[:, 2112:2240], in_=idenD,
                inc=[sSet.inc()])
        sch.add("sync", "dma_start", out=qw[:, :], in_=qwD, inc=[sSet.inc()])
        sch.add("sync", "dma_start", out=kw[:, :], in_=kwD, inc=[sSet.inc()])
        sch.add("sync", "dma_start", out=iotac[:, :], in_=iotacD,
                inc=[sSet.inc()])
        sch.add("sync", "dma_start", out=qposs[:, :], in_=qposD,
                inc=[sSet.inc()])
        sch.add("sync", "dma_start", out=stg[:, 0:2112],
                in_=SaT.rearrange("(c p) j -> p c j", p=128),
                inc=[sSet.inc()])
        ld = sSet.at()
        sch.add("vector", "tensor_copy", iden_bf[:, :], stg[:, 2112:2240],
                w=[ld])
        sch.add("vector", "tensor_scalar_mul", qw[:, :], qw[:, :], ISQ)
        sch.add("vector", "tensor_copy", satb[:, :], stg[:, 0:2112])
        seal(sch)
        sch.emit(nc)

    with ExitStack() as es:
        sb = lambda n, s, d: es.enter_context(
            nc.sbuf_tensor(n, s, d, side="right"))
        fold = sb("fold0", [128, KC_F * 1024], F32)
        asb0 = sb("asb0", [66, 1024], F32)
        tmp0 = sb("tmp0", [66, 1024], F32)
        tfold = sb("tfold0", [128, 1024], F32)
        rstg = [sb(f"rstg0{i}", [128, 1024], F32) for i in range(2)]
        xbf = [sb(f"xbf0{i}", [128, 1024], BF16) for i in range(2)]
        sqb = [sb(f"sqb0{i}", [128, 1024], BF16) for i in range(2)]
        sch = Sched()
        Sched.TAG[0] = "P2"
        phase_wait(sch)
        for kc in range(KC_D):
            sch.add("tensor", "matmul", PS[6][0:66, 0:2],
                    satb[:, kc * 66:(kc + 1) * 66], onesc[:, :],
                    start=(kc == 0), stop=(kc == KC_D - 1),
                    inc=([sT.inc()] if kc == KC_D - 1 else []))
        sch.add("scalar", "mul", negsA[:, :], PS[6][0:66, 0:1], -1.0,
                w=[sT.at()], inc=[sA.inc()])
        sa_ps = [PS[0][0:66, :], PS[1][0:66, :]]
        sq_ps = [PS[2][0:2, :], PS[3][0:2, :]]
        cvd, std, sqd, sqa, fdd = {}, {}, {}, {}, {}
        for c in range(KC_D):
            p = c % 2
            wl = [] if c < 2 else [(sV.sem, fdd[c - 2]), (sV.sem, cvd[c - 2])]
            sch.add("sync", "dma_start", out=rstg[p][:, :],
                    in_=xT[c * 128:(c + 1) * 128, :], w=wl,
                    inc=[LA[p].inc()])
            ld = LA[p].at()
            wl = [ld]
            if c >= 2:
                wl += [(sT.sem, std[c - 2]), (sA.sem, sqa[c - 2])]
            sch.add("vector", "tensor_copy", xbf[p][:, :], rstg[p][:, :],
                    w=wl, inc=[sV.inc()])
            cvd[c] = sV.v
            wl = [(sV.sem, cvd[c])]
            if c >= 2:
                wl.append((sT.sem, sqd[c - 2]))
            sch.add("scalar", "activation", sqb[p][:, :], xbf[p][:, :],
                    AF.Square, w=wl, inc=[sA.inc()])
            sqa[c] = sA.v
            for i in range(2):
                sch.add("tensor", "matmul", sa_ps[i],
                        satb[:, c * 66:(c + 1) * 66],
                        xbf[p][:, i * 512:(i + 1) * 512],
                        start=(c == 0), stop=(c == KC_D - 1),
                        w=([(sV.sem, cvd[c])] if i == 0 else []),
                        inc=([sT.inc()] if i == 1 else []))
            std[c] = sT.v
            for i in range(2):
                sch.add("tensor", "matmul", sq_ps[i], onesc[:, :],
                        sqb[p][:, i * 512:(i + 1) * 512],
                        start=(c == 0), stop=(c == KC_D - 1),
                        w=([(sA.sem, sqa[c])] if i == 0 else []),
                        inc=([sT.inc()] if i == 1 else []))
            sqd[c] = sT.v
            fc = c % KC_F
            if c < KC_F:
                sch.add("vector", "tensor_copy",
                        fold[:, fc * 1024:(fc + 1) * 1024], rstg[p][:, :],
                        w=[ld], inc=[sV.inc()])
            else:
                sch.add("vector", "tensor_add",
                        fold[:, fc * 1024:(fc + 1) * 1024],
                        fold[:, fc * 1024:(fc + 1) * 1024], rstg[p][:, :],
                        w=[ld], inc=[sV.inc()])
            fdd[c] = sV.v
        bc = ln_post(sch, sa_ps, sq_ps, eps=1e-5, cmul=-4.0,
                     pb0=PS[4], pb1=PS[5], stats_done=(sT.sem, sT.v),
                     fold=fold, tfa=tfold)
        ln_post_lora(sch, sa_ps, PS[4], PS[5], scale=0.5, bc_done=bc,
                     asb=asb0, tmp66=tmp0)
        seal(sch)
        sch.emit(nc)

    def ffn_phase(r0_src, rdst, is_final, tag, dump_after_p2=False):
        with ExitStack() as es:
            sb = lambda n, s, d: es.enter_context(
                nc.sbuf_tensor(n + tag, s, d, side="right"))
            gf = sb("gf", [128, KC_H * 1024], BF16)
            bv13 = sb("bv13", [66, HP], BF16)
            b2b = sb("b2b", [2, D], BF16)
            a2b = sb("a2b", [128, 2 * MC_H], BF16)
            with ExitStack() as es2:
                bstg = es2.enter_context(nc.sbuf_tensor(
                    "bstg" + tag, [128, 2 * MC_H], F32, side="right"))
                bstg2 = es2.enter_context(nc.sbuf_tensor(
                    "bstg2" + tag, [66, HP], F32, side="right"))
                sch = Sched()
                Sched.TAG[0] = "P3"
                phase_wait(sch)
                sch.add("sync", "dma_start", out=bstg2[32:34, :], in_=b1T,
                        inc=[sSet.inc()])
                sch.add("sync", "dma_start", out=bstg2[64:66, :], in_=b3T,
                        inc=[sSet.inc()])
                sch.add("sync", "dma_start", out=bstg2[0:2, 0:D], in_=b2T,
                        inc=[sSet.inc()])
                sch.add("sync", "dma_start", out=bstg[:, :], in_=a2S,
                        inc=[sSet.inc()])
                bl = sSet.at()
                sch.add("vector", "memset", bv13[:, :], 0.0, w=[bl])
                sch.add("vector", "tensor_copy", bv13[32:34, :],
                        bstg2[32:34, :])
                sch.add("vector", "tensor_copy", bv13[64:66, :],
                        bstg2[64:66, :])
                sch.add("vector", "tensor_copy", b2b[:, :], bstg2[0:2, 0:D])
                sch.add("vector", "tensor_copy", a2b[:, :], bstg[:, :],
                        inc=[sV.inc()])
                seal(sch)
                sch.emit(nc)

            esp2 = ExitStack()
            sbp = lambda n, s, d: esp2.enter_context(
                nc.sbuf_tensor(n + tag, s, d, side="right"))
            w1f = [sbp(f"w1f{i}", [128, 1024], F32) for i in range(2)]
            w3f = [sbp(f"w3f{i}", [128, 1024], F32) for i in range(2)]
            w1b = [sbp(f"w1b{i}", [128, 1024], BF16) for i in range(2)]
            w3b = [sbp(f"w3b{i}", [128, 1024], BF16) for i in range(2)]
            s1b = sbp("s1b", [128, 1024], F32)
            gbf = [sbp(f"gbf{i}", [128, 1024], BF16) for i in range(2)]
            tf2 = sbp("tf2", [128, 1024], F32)

            sch = Sched()
            Sched.TAG[0] = "P4"
            phase_wait(sch)
            y1p = [[PS[0], PS[1]], [PS[2], PS[3]]]
            y3p = [PS[4], PS[5]]
            psU = [PS[6][0:2, :], PS[7][0:2, :]]
            wcv, sld, y1d, y3d, gpd, a2d, pmc = ({}, {}, {}, {}, {}, {}, {})
            for c in range(KC_H):
                for qt in range(4):
                    mc = qt * KC_H + c
                    it = c * 4 + qt
                    p = it % 2
                    wl = [] if it < 2 else [(sA.sem, wcv[it - 2])]
                    sch.add("gpsimd", "dma_start", out=w1f[p][:, :],
                            in_=w1r[mc], w=wl, inc=[W1[p].inc()])
                    sch.add("gpsimd", "dma_start", out=w3f[p][:, :],
                            in_=w3r[mc], inc=[W3[p].inc()])
                    wl = [W1[p].at(), W3[p].at()]
                    if it >= 2:
                        wl.append((sT.sem, y3d[it - 2]))
                    sch.add("scalar", "copy", w1b[p][:, :], w1f[p][:, :],
                            w=wl)
                    sch.add("scalar", "copy", w3b[p][:, :], w3f[p][:, :],
                            inc=[sA.inc()])
                    wcv[it] = sA.v
                    pa = y1p[p]
                    wl = [(sA.sem, wcv[it])]
                    if it >= 2:
                        wl.append((sA.sem, sld[it - 2]))
                    for h in range(2):
                        for kc in range(KC_F):
                            sch.add("tensor", "matmul", pa[h][:, :],
                                    w1b[p][:, kc * 128:(kc + 1) * 128],
                                    xfb[:, kc * 1024 + h * 512:
                                        kc * 1024 + (h + 1) * 512],
                                    start=(kc == 0), stop=False,
                                    w=(wl if (h == 0 and kc == 0) else []))
                        sch.add("tensor", "matmul", pa[h][:, :],
                                bv13[32:34, mc * 128:(mc + 1) * 128],
                                u_bf[32:34, h * 512:(h + 1) * 512],
                                start=False, stop=True,
                                inc=([sT.inc()] if h == 1 else []))
                    y1d[it] = sT.v
                    wl = [] if it < 1 else [(sV.sem, gpd[it - 1])]
                    for h in range(2):
                        for kc in range(KC_F):
                            sch.add("tensor", "matmul", y3p[h][:, :],
                                    w3b[p][:, kc * 128:(kc + 1) * 128],
                                    xfb[:, kc * 1024 + h * 512:
                                        kc * 1024 + (h + 1) * 512],
                                    start=(kc == 0), stop=False,
                                    w=(wl if (h == 0 and kc == 0) else []))
                        sch.add("tensor", "matmul", y3p[h][:, :],
                                bv13[64:66, mc * 128:(mc + 1) * 128],
                                u_bf[64:66, h * 512:(h + 1) * 512],
                                start=False, stop=True,
                                inc=([sT.inc()] if h == 1 else []))
                    y3d[it] = sT.v
                    for h in range(2):
                        wl5 = []
                        if h == 0:
                            wl5 = [(sT.sem, y1d[it])]
                            if it >= 1:
                                wl5.append((sV.sem, gpd[it - 1]))
                        sch.add("scalar", "activation",
                                s1b[:, h * 512:(h + 1) * 512], pa[h][:, :],
                                AF.Silu, w=wl5,
                                inc=([sA.inc()] if h == 1 else []))
                    sld[it] = sA.v
                    gp = it % 2
                    wl6 = [(sT.sem, y3d[it]), (sA.sem, sld[it])]
                    if it >= 2:
                        wl6.append((sT.sem, a2d[it - 2]))
                    for h in range(2):
                        sch.add("vector", "tensor_mul",
                                gbf[gp][:, h * 512:(h + 1) * 512],
                                s1b[:, h * 512:(h + 1) * 512], y3p[h][:, :],
                                w=(wl6 if h == 0 else []),
                                inc=([sV.inc()] if h == 1 else []))
                    gpd[it] = sV.v
                    if it >= 1:
                        for h in range(2):
                            sch.add("tensor", "matmul", psU[h],
                                    a2b[:, 2 * pmc[it - 1]:
                                        2 * pmc[it - 1] + 2],
                                    gbf[(it - 1) % 2]
                                    [:, h * 512:(h + 1) * 512],
                                    start=(it == 1), stop=False,
                                    w=([(sV.sem, gpd[it - 1])]
                                       if h == 0 else []),
                                    inc=([sT.inc()] if h == 1 else []))
                        a2d[it - 1] = sT.v
                    pmc[it] = mc
                    if qt == 0:
                        sch.add("vector", "tensor_copy", tf2[:, :],
                                gbf[gp][:, :])
                    elif qt < 3:
                        sch.add("vector", "tensor_add", tf2[:, :], tf2[:, :],
                                gbf[gp][:, :])
                    else:
                        sch.add("vector", "tensor_add",
                                gf[:, c * 1024:(c + 1) * 1024], tf2[:, :],
                                gbf[gp][:, :], inc=[sV.inc()])
            last_it = KC_H * 4 - 1
            for h in range(2):
                sch.add("tensor", "matmul", psU[h],
                        a2b[:, 2 * pmc[last_it]:2 * pmc[last_it] + 2],
                        gbf[last_it % 2][:, h * 512:(h + 1) * 512],
                        start=False, stop=True,
                        w=([(sV.sem, gpd[last_it])] if h == 0 else []),
                        inc=([sT.inc()] if h == 1 else []))
            sch.add("scalar", "mul", u2_bf[:, 0:512], psU[0], 0.5,
                    w=[(sT.sem, sT.v)])
            sch.add("scalar", "mul", u2_bf[:, 512:1024], psU[1], 0.5,
                    inc=[sA.inc()])
            seal(sch)
            sch.emit(nc)
            esp2.close()

            if dump_after_p2:
                sch = Sched()
                Sched.TAG[0] = "DBGG"
                phase_wait(sch)
                rfd = tf2
                row = 0
                for c in range(KC_F):
                    sch.add("vector", "tensor_copy", rfd[:, :],
                            xfb[:, c * 1024:(c + 1) * 1024],
                            w=([] if row == 0 else [SA_[0].at()]),
                            inc=[sV.inc()])
                    sch.add("sync", "dma_start",
                            out=outT[row * 128:(row + 1) * 128, :],
                            in_=rfd[:, :], w=[sV.at()], inc=[SA_[0].inc()])
                    row += 1
                for c in range(KC_H):
                    sch.add("vector", "tensor_copy", rfd[:, :],
                            gf[:, c * 1024:(c + 1) * 1024],
                            w=[SA_[0].at()], inc=[sV.inc()])
                    sch.add("sync", "dma_start",
                            out=outT[row * 128:(row + 1) * 128, :],
                            in_=rfd[:, :], w=[sV.at()], inc=[SA_[0].inc()])
                    row += 1
                sch.add("vector", "memset", rfd[:, :], 0.0,
                        w=[SA_[0].at()], inc=[sV.inc()])
                for c in range(row, KC_D):
                    sch.add("sync", "dma_start",
                            out=outT[c * 128:(c + 1) * 128, :],
                            in_=rfd[:, :],
                            w=[sV.at(), SA_[0].at()], inc=[SA_[0].inc()])
                sch.add("sync", "wait_ge", *SA_[0].at())
                seal(sch)
                sch.emit(nc)
                return

            with ExitStack() as es3:
                sb3 = lambda n, s, d: es3.enter_context(
                    nc.sbuf_tensor(n + tag, s, d, side="right"))
                w2f = [sb3(f"w2f{i}", [128, KC_H * 128], F32)
                       for i in range(2)]
                w2b = [sb3(f"w2b{i}", [128, KC_H * 128], BF16)
                       for i in range(2)]
                rst3 = [sb3(f"rst3{i}", [128, 1024], F32) for i in range(2)]
                r1c = [sb3(f"r1c{i}", [128, 1024], F32) for i in range(2)]
                xb3 = [sb3(f"xb3{i}", [128, 1024], BF16) for i in range(2)]
                sq3 = [sb3(f"sq3{i}", [128, 1024], BF16) for i in range(2)]
                sch = Sched()
                Sched.TAG[0] = "P5"
                phase_wait(sch)
                wps = [[PS[0], PS[1]], [PS[2], PS[3]]]
                stx = [PS[4][0:2, :], PS[5][0:2, :]]
                stq = [PS[6][0:2, :], PS[7][0:2, :]]
                wcv, mmd, evd, std2, sqd2, cvd2, sqa2, stod = (
                    {} for _ in range(8))
                for mc in range(KC_D):
                    p = mc % 2
                    wl = [] if mc < 2 else [(sA.sem, wcv[mc - 2])]
                    sch.add("gpsimd", "dma_start", out=w2f[p][:, :],
                            in_=w2r[mc], w=wl, inc=[W1[p].inc()])
                    wl = [W1[p].at()]
                    if mc >= 2:
                        wl.append((sT.sem, mmd[mc - 2]))
                    sch.add("scalar", "copy", w2b[p][:, :], w2f[p][:, :],
                            w=wl, inc=[sA.inc()])
                    wcv[mc] = sA.v
                    wl = [] if mc < 2 else [(sV.sem, evd[mc - 2])]
                    sch.add("sync", "dma_start", out=rst3[p][:, :],
                            in_=r0_src[mc * 128:(mc + 1) * 128, :], w=wl,
                            inc=[LA[p].inc()])
                    r0ld = LA[p].at()
                    pa = wps[p]
                    wl = [(sA.sem, wcv[mc])]
                    if mc >= 2:
                        wl.append((sV.sem, evd[mc - 2]))
                    for h in range(2):
                        for kc in range(KC_H):
                            sch.add("tensor", "matmul", pa[h][:, :],
                                    w2b[p][:, kc * 128:(kc + 1) * 128],
                                    gf[:, kc * 1024 + h * 512:
                                        kc * 1024 + (h + 1) * 512],
                                    start=(kc == 0), stop=False,
                                    w=(wl if (h == 0 and kc == 0) else []))
                        sch.add("tensor", "matmul", pa[h][:, :],
                                b2b[:, mc * 128:(mc + 1) * 128],
                                u2_bf[:, h * 512:(h + 1) * 512],
                                start=False, stop=True,
                                inc=([sT.inc()] if h == 1 else []))
                    mmd[mc] = sT.v
                    for h in range(2):
                        wl4 = []
                        if h == 0:
                            wl4 = [(sT.sem, mmd[mc]), r0ld]
                            if mc >= 2:
                                wl4.append(stod[mc - 2])
                                if not is_final:
                                    wl4.append((sV.sem, cvd2[mc - 2]))
                        sch.add("vector", "scalar_tensor_tensor",
                                r1c[p][:, h * 512:(h + 1) * 512],
                                pa[h][:, :], 0.5,
                                rst3[p][:, h * 512:(h + 1) * 512],
                                ALU.mult, ALU.add, w=wl4,
                                inc=([sV.inc()] if h == 1 else []))
                    evd[mc] = sV.v
                    sch.add("sync", "dma_start",
                            out=rdst[mc * 128:(mc + 1) * 128, :],
                            in_=r1c[p][:, :], w=[(sV.sem, evd[mc])],
                            inc=[SA_[p].inc()])
                    stod[mc] = SA_[p].at()
                    if not is_final:
                        wl = [(sV.sem, evd[mc])]
                        if mc >= 2:
                            wl += [(sT.sem, std2.get(mc - 2, 0)),
                                   (sT.sem, sqd2.get(mc - 2, 0))]
                        sch.add("vector", "tensor_copy", xb3[p][:, :],
                                r1c[p][:, :], w=wl, inc=[sV.inc()])
                        cvd2[mc] = sV.v
                        sch.add("vector", "tensor_mul", sq3[p][:, :],
                                xb3[p][:, :], xb3[p][:, :],
                                inc=[sV.inc()], lp=True)
                        sqa2[mc] = sV.v
                        for mcx in ([mc - 1] if mc >= 1 else []) + (
                                [mc] if mc == KC_D - 1 else []):
                            px = mcx % 2
                            for i in range(2):
                                sch.add("tensor", "matmul", stx[i],
                                        onesc[:, :],
                                        xb3[px][:, i * 512:(i + 1) * 512],
                                        start=(mcx == 0),
                                        stop=(mcx == KC_D - 1),
                                        w=([(sV.sem, cvd2[mcx])] if i == 0
                                           else []),
                                        inc=([sT.inc()] if i == 1 else []))
                            std2[mcx] = sT.v
                            for i in range(2):
                                sch.add("tensor", "matmul", stq[i],
                                        onesc[:, :],
                                        sq3[px][:, i * 512:(i + 1) * 512],
                                        start=(mcx == 0),
                                        stop=(mcx == KC_D - 1),
                                        w=([(sV.sem, sqa2[mcx])] if i == 0
                                           else []),
                                        inc=([sT.inc()] if i == 1 else []))
                            sqd2[mcx] = sT.v
                if not is_final:
                    ln_post(sch, stx, stq, eps=1e-5, cmul=-1.0,
                            pb0=PS[0], pb1=PS[1], stats_done=(sT.sem, sT.v),
                            first_w=[(sV.sem, sV.v)])
                seal(sch)
                sch.emit(nc)

    ffn_phase(xT, r1d, is_final=False, tag="f1")

    def debug_copy_out(srcd):
        with ExitStack() as es:
            rr = es.enter_context(nc.sbuf_tensor("rrcopy", [128, 1024], F32,
                                                 side="right"))
            sch = Sched()
            Sched.TAG[0] = "DBG"
            phase_wait(sch)
            for c in range(KC_D):
                sch.add("sync", "dma_start", out=rr[:, :],
                        in_=srcd[c * 128:(c + 1) * 128, :],
                        w=([] if c == 0 else [SA_[0].at()]),
                        inc=[sSet.inc()])
                sch.add("sync", "dma_start",
                        out=outT[c * 128:(c + 1) * 128, :], in_=rr[:, :],
                        w=[sSet.at()], inc=[SA_[0].inc()])
            sch.add("sync", "wait_ge", *SA_[0].at())
            seal(sch)
            sch.emit(nc)

    def debug_dump_bf16(srcd_view):
        with ExitStack() as es:
            rb = es.enter_context(nc.sbuf_tensor("rdbg_b", [128, 1024], BF16,
                                                 side="right"))
            rf = es.enter_context(nc.sbuf_tensor("rdbg_f", [128, 1024], F32,
                                                 side="right"))
            sch = Sched()
            Sched.TAG[0] = "DBGB"
            phase_wait(sch)
            for c in range(KC_D):
                sch.add("sync", "dma_start", out=rb[:, :],
                        in_=srcd_view[c * 128:(c + 1) * 128, :],
                        w=([] if c == 0 else [SA_[0].at()]),
                        inc=[sSet.inc()])
                sch.add("vector", "tensor_copy", rf[:, :], rb[:, :],
                        w=[sSet.at()], inc=[sV.inc()])
                sch.add("sync", "dma_start",
                        out=outT[c * 128:(c + 1) * 128, :], in_=rf[:, :],
                        w=[sV.at()], inc=[SA_[0].inc()])
            sch.add("sync", "wait_ge", *SA_[0].at())
            seal(sch)
            sch.emit(nc)

    if stop_stage == 1:
        debug_copy_out(r1d)
        return nc

    with ExitStack() as es4:
        sb4 = lambda n, s, d: es4.enter_context(
            nc.sbuf_tensor(n, s, d, side="right"))

        with ExitStack() as es:
            sb = lambda n, s, d: es.enter_context(
                nc.sbuf_tensor(n, s, d, side="right"))
            xln = sb("xln", [128, KC_D * 1024], BF16)
            wfA = [sb(f"wfA{i}", [128, 2048], F32) for i in range(2)]
            wbA = [sb(f"wbA{i}", [128, 2048], BF16) for i in range(2)]
            qet = [sb(f"qet{i}", [128, 1024], BF16) for i in range(2)]
            q2t = [sb(f"q2t{i}", [128, 1024], BF16) for i in range(2)]
            vtb = sb("vtb", [128, 1024], BF16)
            vtm = [sb(f"vtm{i}", [128, 1024], BF16)
                   for i in range(2)]
            xtp = sb("xtp", [128, 1024], F32)
            rst4 = [sb(f"rst4{i}", [128, 1024], F32) for i in range(2)]
            sch = Sched()
            Sched.TAG[0] = "P7"
            phase_wait(sch)
            xld = {}
            for c in range(KC_D):
                p = c % 2
                wl = [] if c < 2 else [(sV.sem, xld[c - 2])]
                sch.add("sync", "dma_start", out=rst4[p][:, :],
                        in_=r1d[c * 128:(c + 1) * 128, :], w=wl,
                        inc=[LA[p].inc()])
                ld = LA[p].at()
                sch.add("vector", "tensor_mul", xtp[:, :], rst4[p][:, :],
                        bcA[:, :], w=[ld])
                sch.add("vector", "tensor_add",
                        xln[:, c * 1024:(c + 1) * 1024], xtp[:, :],
                        bcC[:, :], inc=[sV.inc()])
                xld[c] = sV.v
            xall = sV.at()
            psQ = [[PS[0], PS[1]], [PS[2], PS[3]]]
            psRM = [PS[4][0:40, :], PS[5][0:40, :]]
            psV = PS[6][:, :].bitcast(BF16)
            wcv = {}
            mmh = {}
            evd = {}
            rmd = {}
            vst = {}
            qsd = {}
            for m in range(48):
                p = m % 2
                src_w = wqr if m < 32 else (wkr if m < 40 else wvr)
                hh = m if m < 32 else (m - 32 if m < 40 else m - 40)
                pa = psQ[p]
                for i in range(2):
                    wl = []
                    if i == 1:
                        wl = [(sV.sem, wcv[(m, 0)])]
                    elif m >= 2:
                        wl = [(sV.sem, wcv[(m - 2, 1)])]
                    sch.add("gpsimd", "dma_start", out=wfA[p][:, :],
                            in_=src_w[hh][:, i * 16:(i + 1) * 16, :],
                            w=wl, inc=[W1[p].inc()])
                    w2l = [W1[p].at()]
                    if i == 1:
                        w2l.append((sT.sem, mmh[(m, 0)]))
                    elif m >= 2:
                        w2l.append((sT.sem, mmh[(m - 2, 1)]))
                    sch.add("vector", "tensor_copy", wbA[p][:, :],
                            wfA[p][:, :], w=w2l, inc=[sV.inc()])
                    wcv[(m, i)] = sV.v
                    wl3 = [(sV.sem, wcv[(m, i)])]
                    if i == 0:
                        if m == 0:
                            wl3.append(xall)
                        if m >= 2:
                            wl3.append((sA.sem, evd[m - 2]))
                    for h in range(2):
                        for kk in range(16):
                            kc = i * 16 + kk
                            sch.add("tensor", "matmul", pa[h][:, :],
                                    wbA[p][:, kk * 128:(kk + 1) * 128],
                                    xln[:, kc * 1024 + h * 512:
                                        kc * 1024 + (h + 1) * 512],
                                    start=(kc == 0), stop=(kc == KC_D - 1),
                                    w=(wl3 if (h == 0 and kk == 0) else []),
                                    inc=([sT.inc()]
                                         if (h == 1 and kk == 15) else []))
                    mmh[(m, i)] = sT.v
                if m < 40:
                    for h in range(2):
                        wl = []
                        if h == 0:
                            wl = [(sT.sem, mmh[(m, 1)])]
                            if m >= 2:
                                wl += [qsd[m - 2],
                                       (sT.sem, rmd.get(m - 2, 0))]
                        sch.add("scalar", "copy",
                                qet[p][:, h * 512:(h + 1) * 512],
                                pa[h][:, :], w=wl)
                        sch.add("scalar", "activation",
                                q2t[p][:, h * 512:(h + 1) * 512],
                                pa[h][:, :], AF.Square,
                                inc=([sA.inc()] if h == 1 else []))
                    evd[m] = sA.v
                    sch.add("sync", "dma_start",
                            out=qd[m * 128:(m + 1) * 128, :],
                            in_=qet[p][:, :], w=[(sA.sem, evd[m])],
                            inc=[QS[p].inc()])
                    qsd[m] = QS[p].at()
                    for mx in ([m - 1] if 1 <= m else []) + (
                            [m] if m == 39 else []):
                        if mx >= 40 or mx < 0:
                            continue
                        px = mx % 2
                        for h in range(2):
                            sch.add("tensor", "matmul", psRM[h],
                                    W40[:, 39 - mx:79 - mx],
                                    q2t[px][:, h * 512:(h + 1) * 512],
                                    start=(mx == 0), stop=(mx == 39),
                                    w=([(sA.sem, evd[mx])] if h == 0
                                       else []),
                                    inc=([sT.inc()] if h == 1 else []))
                        rmd[mx] = sT.v
                else:
                    hv = m - 40
                    for h in range(2):
                        wl = []
                        if h == 0:
                            wl = [(sT.sem, mmh[(m, 1)])]
                            if m > 40:
                                wl.append((sT.sem, vst[m - 1]))
                        sch.add("scalar", "copy",
                                vtb[:, h * 512:(h + 1) * 512], pa[h][:, :],
                                w=wl, inc=([sA.inc()] if h == 1 else []))
                    evd[m] = sA.v
                    for l in range(8):
                        wl = []
                        if l == 0:
                            wl = [(sA.sem, evd[m])]
                            if m > 40:
                                wl.append((sA.sem, vcp[m - 1]))
                        sch.add("tensor", "transpose",
                                psV[:, l * 128:(l + 1) * 128],
                                vtb[:, l * 128:(l + 1) * 128],
                                iden_bf[:, :], w=wl,
                                inc=([sT.inc()] if l == 7 else []))
                    vst[m] = sT.v
                    wl = [(sT.sem, vst[m])]
                    if m > 41:
                        wl.append(vkd[m - 2])
                    sch.add("scalar", "copy", vtm[m % 2][:, :], psV[:, :],
                            w=wl, inc=[sA.inc()])
                    vcp = vcp if m > 40 else {}
                    vcp[m] = sA.v
                    sch.add("sync", "dma_start",
                            out=kv_inV.rearrange(
                                "(kvh l t) c -> kvh t l c",
                                kvh=8, l=8)[hv],
                            in_=vtm[m % 2][:, :], w=[(sA.sem, vcp[m])],
                            inc=[KVV.inc()])
                    vkd = vkd if m > 40 else {}
                    vkd[m] = KVV.at()
            sch.add("gpsimd", "collective_compute", "AllGather",
                    ALU.bypass,
                    replica_groups=[[0, 1], [2, 3], [4, 5], [6, 7]],
                    ins=[kv_inV.opt()], outs=[kv_outV.opt()],
                    w=[KVV.at()], inc=[sCC.inc()])
            seal(sch)
            sch.emit(nc)

        if stop_stage == 21:
            debug_dump_bf16(qd[0:4096, :])
            return nc

        osb = sb4("osb", [128, NQ * 1024], BF16)
        es4b = ExitStack()
        sb4b = lambda n, s, d: es4b.enter_context(
            nc.sbuf_tensor(n, s, d, side="right"))
        rstdr_bf = sb4b("rstdr_bf", [40, 1024], BF16)
        sel4 = sb4b("sel4", [40, 5120], BF16)
        mskb = sb4b("mskb", [128, 2048], BF16)
        with ExitStack() as es:
            sb = lambda n, s, d: es.enter_context(
                nc.sbuf_tensor(n, s, d, side="right"))
            rmsm = sb("rmsm", [40, 1024], F32)
            ket = [sb(f"ket{i}", [128, 1024], BF16) for i in range(2)]
            knt = [sb(f"knt{i}", [128, 1024], BF16) for i in range(2)]
            bcs = [sb(f"bcs{i}", [128, 1024], F32) for i in range(2)]
            self4 = sb("self4", [40, 5120], F32)
            sch = Sched()
            Sched.TAG[0] = "P8"
            phase_wait(sch)
            sch.add("sync", "dma_start", out=self4[:, :], in_=sel40D,
                    inc=[sSet.inc()])
            for a in range(2):
                sch.add("sync", "dma_start", out=bcs[a][:, :],
                        in_=masksD[:, a * 1024:(a + 1) * 1024],
                        inc=[sSet.inc()])
            mld1 = sSet.at()
            sch.add("vector", "tensor_copy", sel4[:, :], self4[:, :],
                    w=[sSet.at()], inc=[sV.inc()])
            for a in range(2):
                sch.add("vector", "tensor_copy",
                        mskb[:, a * 1024:(a + 1) * 1024], bcs[a][:, :],
                        w=[mld1], inc=[sV.inc()])
            mcv1 = sV.at()
            for i in range(2):
                sch.add("scalar", "mul", rmsm[:, i * 512:(i + 1) * 512],
                        psRM[i], 1.0 / 128.0,
                        inc=([sA.inc()] if i == 1 else []))
            sch.add("scalar", "activation", rmsm[:, :], rmsm[:, :], AF.Ln,
                    bias=epsc[0:40, 1:2])
            sch.add("scalar", "activation", rmsm[:, :], rmsm[:, :], AF.Exp,
                    scale=-0.5, inc=[sA.inc()])
            rk = sA.at()
            sch.add("vector", "tensor_copy", rstdr_bf[:, :], rmsm[:, :],
                    w=[rk], inc=[sV.inc()])
            rbf = sV.at()
            bcd = {}
            knd = {}
            ksd = {}
            for h in list(range(32, 40)) + list(range(32)):
                p = h % 2
                hseq = (h - 32) if h >= 32 else (h + 8)
                wl = [rbf] if hseq < 2 else [(sV.sem, knd[hseq - 2])]
                sch.add("sync", "dma_start", out=ket[p][:, :],
                        in_=qd[h * 128:(h + 1) * 128, :], w=wl,
                        inc=[QA[p].inc()])
                kld = QA[p].at()
                for i in range(2):
                    wl = []
                    if i == 0:
                        wl = [rbf] if hseq == 0 else [
                            (sA.sem, bcd[hseq - 1])]
                        if hseq >= 2:
                            wl.append((sV.sem, knd[hseq - 2]))
                    sch.add("tensor", "matmul", PS[0 + i][:, :],
                            sel4[:, h * 128:(h + 1) * 128],
                            rstdr_bf[:, i * 512:(i + 1) * 512],
                            start=True, stop=True, w=wl,
                            inc=([sT.inc()] if i == 1 else []))
                bmm = sT.v
                for i in range(2):
                    sch.add("scalar", "copy",
                            bcs[p][:, i * 512:(i + 1) * 512],
                            PS[0 + i][:, :],
                            w=([(sT.sem, bmm), mcv1] if i == 0 else []),
                            inc=([sA.inc()] if i == 1 else []))
                bcd[hseq] = sA.v
                wv = qw if h < 32 else kw
                sch.add("vector", "scalar_tensor_tensor", knt[p][:, :],
                        ket[p][:, :], wv[:, :], bcs[p][:, :],
                        ALU.mult, ALU.mult, w=[kld, (sA.sem, bcd[hseq])],
                        inc=[sV.inc()])
                knd[hseq] = sV.v
                if h < 32:
                    sch.add("sync", "dma_start",
                            out=qd[h * 128:(h + 1) * 128, :],
                            in_=knt[p][:, :], w=[(sV.sem, knd[hseq])],
                            inc=[QS[p].inc()])
                else:
                    sch.add("gpsimd", "dma_start",
                            out=kv_inK.rearrange("(hh l) c -> hh l c", l=8)
                            [(h - 32) * 128:(h - 31) * 128],
                            in_=knt[p][:, :], w=[(sV.sem, knd[hseq])],
                            inc=[KV[p].inc()])
                    ksd[h] = KV[p].at()
                if h == 39:
                    sch.add("gpsimd", "collective_compute", "AllGather",
                            ALU.bypass,
                            replica_groups=[[0, 1], [2, 3], [4, 5], [6, 7]],
                            ins=[kv_inK.opt()], outs=[kv_outK.opt()],
                            w=[KV[0].at(), KV[1].at()],
                            inc=[sCC.inc()])
            seal(sch)
            sch.emit(nc)

        if stop_stage == 22:
            debug_dump_bf16(
                kv_outK.rearrange("(a b) c -> a (b c)", b=8)[0:4096, :])
            return nc

        with ExitStack() as es:
            sb = lambda n, s, d: es.enter_context(
                nc.sbuf_tensor(n, s, d, side="right"))
            kfl = [sb(f"kfl{i}", [128, 2048], BF16) for i in range(2)]
            vfl = [sb(f"vfl{i}", [128, 2048], BF16) for i in range(2)]
            pTA = sb("pTA", [128, 8 * 512], BF16)
            pTB = sb("pTB", [128, 16 * 512], BF16)
            qnb = [sb(f"qnb{i}", [128, 1024], BF16) for i in range(2)]
            acol = [sb(f"acol{i}", [128, 16], F32) for i in range(2)]
            ngr = sb("ngr", [2, 2048], BF16)
            rcb = sb("rcb", [2, 1024], BF16)
            rtmp = sb("rtmp", [2, 1024], F32)
            bco = [sb(f"bco{i}", [128, 512], F32) for i in range(2)]
            q2b = [sb(f"q2b{i}", [128, 1024], BF16) for i in range(2)]
            msqA = sb("msqA", [2, 1024], BF16)
            msqB = sb("msqB", [2, 1024], BF16)
            aqr = sb("aqr", [2, 1024], BF16)
            kwb = sb("kwb", [128, 2], BF16)
            c2m = sb("c2m", [2, 2], BF16)
            e10 = sb("e10", [2, 2], BF16)
            onesC = sb("onesC", [128, 2], BF16)
            ktmp = msqA[:, 0:128]
            c2rs = rcb[:, 0:128]
            sch = Sched()
            Sched.TAG[0] = "P9"
            phase_wait(sch)
            sch.add("vector", "memset", pTA[:, :], 0.0)
            sch.add("vector", "memset", pTB[:, :], 0.0)
            sch.add("vector", "memset", kwb[:, :], 0.0)
            sch.add("vector", "memset", c2m[:, :], 0.0)
            sch.add("vector", "memset", e10[:, :], 0.0)
            sch.add("vector", "memset", e10[0:1, 0:2], 1.0,
                    inc=[sV.inc()])
            sch.add("scalar", "activation", kwb[:, 0:1], kw[:, 0:1],
                    AF.Square, w=[(sV.sem, sV.v)], inc=[sA.inc()], lp=True)
            sch.add("tensor", "transpose",
                    PS[2][:, :].bitcast(BF16)[0:2, 0:128], kwb[:, :],
                    iden_bf[:, :], w=[(sA.sem, sA.v)], inc=[sT.inc()])
            sch.add("scalar", "copy", ktmp,
                    PS[2][:, :].bitcast(BF16)[0:2, 0:128],
                    w=[(sT.sem, sT.v)], inc=[sA.inc()], lp=True)
            ktmp_v = sA.v
            sch.add("vector", "reduce_max", c2m[:, 0:1], ktmp,
                    axis=AX.X, w=[(sA.sem, sA.v)], inc=[sV.inc()], lp=True)
            sch.add("tensor", "matmul", PS[0][0:2, 0:128], c2m[:, :],
                    ones2[:, 0:128], start=True, stop=True,
                    w=[(sV.sem, sV.v)], inc=[sT.inc()])
            sch.add("scalar", "copy", c2rs, PS[0][0:2, 0:128],
                    w=[(sT.sem, sT.v)], inc=[sA.inc()], lp=True)
            sch.add("tensor", "matmul", PS[1][:, 0:2], c2rs,
                    e10[:, :], start=True, stop=True,
                    w=[(sA.sem, sA.v)], inc=[sT.inc()])
            cb0 = sA.v
            sch.add("scalar", "copy", onesC[:, :], PS[1][:, 0:2],
                    w=[(sT.sem, sT.v)], inc=[sA.inc()], lp=True)
            cb1 = sA.v
            kvd = {}
            kvuse = {}
            qld = {}
            acd = {}
            p2use = [0, 0]
            ngruse = {}
            mbank_use = [cb0, cb1]
            q2use = [0, 0]
            sumuse = [0]
            ouse = {6: (sV.sem, 0), 2: (sA.sem, ktmp_v)}
            bc7use = [0]
            bcouse = [0, 0]
            pTuse = {0: 0, 1: 0}
            slopes = _alibi_slopes(NQ)
            NKV_RUN = 1 if stop_stage == 24 else NKV
            NG_RUN = 1 if stop_stage == 24 else 4
            GRPS = 1 if stop_stage == 24 else 2

            AVB = {0: 6, 1: 2}

            def s2_sr(hq, ph, pk, grp, ngr_v):
                pT = pTA if grp == 0 else pTB
                kcm = 8 * grp + 8
                LEAD = 2
                evs = []

                def red(kc):
                    lmin = max(4 * grp, kc // 2)
                    qs = (lmin - 4 * grp) * 128
                    wl = [(sA.sem, evs[kc])]
                    if kc == 0:
                        wl += [(sA.sem, sumuse[0]),
                               ouse[AVB[grp]], kvd[hq // 4]]
                    sch.add("tensor", "matmul", PS[5][0:2, qs:512],
                            onesc[:, :],
                            pT[:, kc * 512 + qs:(kc + 1) * 512],
                            start=(kc == 0), stop=(kc == kcm - 1), w=wl)
                    sch.add("tensor", "matmul", PS[AVB[grp]][:, :],
                            vfl[pk][:, kc * 128:(kc + 1) * 128],
                            pT[:, kc * 512:(kc + 1) * 512],
                            start=(kc == 0), stop=(kc == kcm - 1),
                            inc=([sT.inc()] if kc == kcm - 1 else []))

                for kc in range(kcm):
                    lmin = max(4 * grp, kc // 2)
                    qs = (lmin - 4 * grp) * 128
                    bk = kc % 2
                    p2b = PS[3 + bk]
                    wl = [(sA.sem, p2use[bk])]
                    if kc == 0:
                        wl += [(sV.sem, ngr_v), qld[hq], kvd[hq // 4]]
                    sl = (ph * 2 + grp) * 512
                    sch.add("tensor", "matmul", p2b[:, qs:512],
                            ones2n[:, :], ngr[:, sl + qs:sl + 512],
                            start=True, stop=False, w=wl)
                    sch.add("tensor", "matmul", p2b[:, qs:512],
                            kfl[pk][:, kc * 128:(kc + 1) * 128],
                            qnb[ph][:, grp * 512 + qs:grp * 512 + 512],
                            start=False, stop=True, inc=[sT.inc()])
                    smm = sT.v
                    need_mask = (kc >= 8 * grp)
                    exp_w = [(sT.sem, smm)]
                    if need_mask:
                        sch.add("vector", "tensor_add",
                                p2b[:, qs:qs + 128], p2b[:, qs:qs + 128],
                                mskb[:, kc * 128:(kc + 1) * 128],
                                w=[(sT.sem, smm)], inc=[sV.inc()])
                        exp_w = [(sV.sem, sV.v)]
                    wl = exp_w + [(sV.sem, acd[hq])]
                    if pTuse[grp]:
                        wl.append((sT.sem, pTuse[grp]))
                    sch.add("scalar", "activation",
                            pT[:, kc * 512 + qs:(kc + 1) * 512],
                            p2b[:, qs:512], AF.Exp,
                            bias=acol[ph][:, kc:kc + 1], w=wl,
                            inc=[sA.inc()])
                    p2use[bk] = sA.v
                    evs.append(sA.v)
                    if kc >= LEAD:
                        red(kc - LEAD)
                ngruse[(ph, grp)] = sT.v
                for kc in range(kcm - LEAD, kcm):
                    red(kc)
                avd = sT.v
                pTuse[grp] = avd
                kvuse[hq // 4] = avd
                sch.add("scalar", "activation",
                        rtmp[:, grp * 512:(grp + 1) * 512], PS[5][0:2, :],
                        AF.Ln, w=[(sT.sem, avd)], inc=[sA.inc()])
                sumuse[0] = sA.v
                sch.add("scalar", "activation",
                        rcb[:, grp * 512:(grp + 1) * 512],
                        rtmp[:, grp * 512:(grp + 1) * 512], AF.Exp,
                        scale=-1.0, inc=[sA.inc()], lp=True)
                return avd, sA.v

            def s2_evac(hq, grp, rc_v):
                sch.add("tensor", "matmul", PS[7][:, :], ones2[:, :],
                        rcb[:, grp * 512:(grp + 1) * 512], start=True,
                        stop=True,
                        w=[(sA.sem, rc_v), (sV.sem, bcouse[grp]),
                           (sA.sem, bc7use[0])],
                        inc=[sT.inc()])
                bc_mm = sT.v
                sch.add("scalar", "copy", bco[grp][:, :], PS[7][:, :],
                        w=[(sT.sem, bc_mm)], inc=[sA.inc()])
                bcod = sA.v
                bc7use[0] = sA.v
                sch.add("vector", "tensor_mul",
                        osb[:, hq * 1024 + grp * 512:
                            hq * 1024 + (grp + 1) * 512],
                        PS[AVB[grp]][:, :], bco[grp][:, :],
                        w=[(sA.sem, bcod)], inc=[sV.inc()])
                ouse[AVB[grp]] = (sV.sem, sV.v)
                return sV.v

            NH = NKV_RUN * NG_RUN

            def kv_load(kv):
                pk = kv % 2
                wl = [(sCC.sem, 1)]
                if kv >= 2:
                    wl.append((sT.sem, kvuse[kv - 2]))
                for par in range(2):
                    sch.add("gpsimd", "dma_start",
                        out=vfl[pk][:, :].rearrange(
                            "p (l q c) -> p l q c", l=8, q=2)[:, :, par, :],
                        in_=kv_outV.rearrange(
                            "(q kvh l t) c -> q kvh t l c",
                            q=2, kvh=8, l=8)[par, kv],
                        w=(wl if par == 0 else []), inc=[KF[pk].inc()])
                for par in range(2):
                    sch.add("gpsimd", "dma_start",
                        out=kfl[pk][:, :].rearrange(
                            "p (l q c) -> p l q c", l=8, q=2)[:, :, par, :],
                        in_=kv_outK.rearrange(
                            "(q hh l) c -> q hh l c", q=2, l=8)
                        [par, kv * 128:(kv + 1) * 128],
                        w=([(sCC.sem, 2)] if par == 0 else []),
                        inc=[KF[pk].inc()])
                kvd[kv] = KF[pk].at()

            def prep_head(hq):
                ph = hq % 2
                pk = (hq // 4) % 2
                if hq % 4 == 0:
                    kv_load(hq // 4)
                wl = []
                if hq >= 2:
                    wl.append((sT.sem, pTuse[0]))
                    wl.append((sT.sem, pTuse[1]))
                sch.add("sync", "dma_start", out=qnb[ph][:, :],
                        in_=qd[hq * 128:(hq + 1) * 128, :], w=wl,
                        inc=[QA[ph].inc()])
                qld[hq] = QA[ph].at()
                wl = []
                if hq >= 2:
                    wl += [(sA.sem, p2use[0]), (sA.sem, p2use[1])]
                sch.add("vector", "tensor_scalar_mul", acol[ph][:, :],
                        iotac[:, :], float(slopes[hq]), w=wl,
                        inc=[sV.inc()])
                acd[hq] = sV.v
                sch.add("vector", "tensor_scalar_mul", aqr[:, :],
                        qposs[:, :], float(slopes[hq]), lp=True)
                sch.add("scalar", "activation", q2b[ph][:, :],
                        qnb[ph][:, :], AF.Square,
                        w=[qld[hq], (sT.sem, q2use[ph])],
                        inc=[sA.inc()], lp=True)
                sq_v = sA.v
                ngv = []
                for grp in range(GRPS):
                    sl = (ph * 2 + grp) * 512
                    sch.add("tensor", "matmul", PS[grp][0:2, 0:512],
                            onesC[:, :],
                            q2b[ph][:, grp * 512:(grp + 1) * 512],
                            start=True, stop=True,
                            w=[(sA.sem, sq_v), (sA.sem, mbank_use[grp])],
                            inc=[sT.inc()])
                    q2use[ph] = sT.v
                    sch.add("scalar", "activation",
                            msqA[:, grp * 512:(grp + 1) * 512],
                            PS[grp][0:2, 0:512], AF.Ln, scale=133.12,
                            w=[(sT.sem, sT.v)], inc=[sA.inc()], lp=True)
                    mbank_use[grp] = sA.v
                    sch.add("scalar", "activation",
                            msqB[:, grp * 512:(grp + 1) * 512],
                            msqA[:, grp * 512:(grp + 1) * 512], AF.Exp,
                            scale=0.5, inc=[sA.inc()], lp=True)
                    sch.add("vector", "tensor_add", ngr[:, sl:sl + 512],
                            msqB[:, grp * 512:(grp + 1) * 512],
                            aqr[:, grp * 512:(grp + 1) * 512],
                            w=[(sA.sem, sA.v),
                               (sT.sem, ngruse.get((ph, grp), 0))],
                            inc=[sV.inc()], lp=True)
                    ngv.append(sV.v)
                return ngv

            ngs_q = {0: prep_head(0)}
            e1_pend = None
            for hq in range(NH):
                ph = hq % 2
                pk = (hq // 4) % 2
                if e1_pend is not None:
                    bcouse[1] = s2_evac(e1_pend[0], 1, e1_pend[1])
                    e1_pend = None
                if hq + 1 < NH:
                    ngs_q[hq + 1] = prep_head(hq + 1)
                ngs = ngs_q.pop(hq)
                if GRPS == 1:
                    avda, rca = s2_sr(hq, ph, pk, 0, ngs[0])
                    bcouse[0] = s2_evac(hq, 0, rca)
                else:
                    avda, rca = s2_sr(hq, ph, pk, 0, ngs[0])
                    avdb, rcb_v = s2_sr(hq, ph, pk, 1, ngs[1])
                    bcouse[0] = s2_evac(hq, 0, rca)
                    e1_pend = (hq, rcb_v)
            if e1_pend is not None:
                bcouse[1] = s2_evac(e1_pend[0], 1, e1_pend[1])
            seal(sch)
            sch.emit(nc)

        if stop_stage == 23:
            with ExitStack() as es:
                rf = es.enter_context(nc.sbuf_tensor(
                    "rdbg_f2", [128, 1024], F32, side="right"))
                sch = Sched()
                Sched.TAG[0] = "DBGO"
                phase_wait(sch)
                for c in range(KC_D):
                    sch.add("vector", "tensor_copy", rf[:, :],
                            osb[:, c * 1024:(c + 1) * 1024],
                            w=([] if c == 0 else [SA_[0].at()]),
                            inc=[sV.inc()])
                    sch.add("sync", "dma_start",
                            out=outT[c * 128:(c + 1) * 128, :], in_=rf[:, :],
                            w=[sV.at()], inc=[SA_[0].inc()])
                sch.add("sync", "wait_ge", *SA_[0].at())
                seal(sch)
                sch.emit(nc)
            return nc

        es4b.close()
        with ExitStack() as es:
            sb = lambda n, s, d: es.enter_context(
                nc.sbuf_tensor(n, s, d, side="right"))
            asb4 = sb("asb4", [66, 1024], F32)
            wfO = [sb(f"wfO{i}", [128, 2048], F32) for i in range(2)]
            wbO = [sb(f"wbO{i}", [128, 2048], BF16) for i in range(2)]
            rst5 = [sb(f"rst5{i}", [128, 1024], F32) for i in range(2)]
            fold5d = sb("fold5d", [128, KC_F * 1024], F32)
            r2c = [sb(f"r2c{i}", [128, 1024], F32) for i in range(2)]
            xb5 = [sb(f"xb5{i}", [128, 1024], BF16) for i in range(2)]
            sq5 = [sb(f"sq5{i}", [128, 1024], BF16) for i in range(2)]
            sch = Sched()
            Sched.TAG[0] = "P10"
            phase_wait(sch)
            wps = [[PS[0], PS[1]], [PS[2], PS[3]]]
            sa_ps = [PS[4][0:66, :], PS[5][0:66, :]]
            sq_ps = [PS[6][0:2, :], PS[7][0:2, :]]
            wcv = {}
            mmh = {}
            mmd = {}
            evd = {}
            std2 = {}
            sqd2 = {}
            cvd2 = {}
            sqa2 = {}
            fdd = {}
            stod = {}
            for mc in range(KC_D):
                p = mc % 2
                pa = wps[p]
                for i in range(2):
                    wl = []
                    if i == 1:
                        wl = [(sA.sem, wcv[(mc, 0)])]
                    elif mc >= 2:
                        wl = [(sA.sem, wcv[(mc - 2, 1)])]
                    sch.add("gpsimd", "dma_start", out=wfO[p][:, :],
                            in_=wor[mc][:, i * 16:(i + 1) * 16, :], w=wl,
                            inc=[W1[p].inc()])
                    w2l = [W1[p].at()]
                    if i == 1:
                        w2l.append((sT.sem, mmh[(mc, 0)]))
                    elif mc >= 2:
                        w2l.append((sT.sem, mmh[(mc - 2, 1)]))
                    sch.add("scalar", "copy", wbO[p][:, :],
                            wfO[p][:, :], w=w2l, inc=[sA.inc()])
                    wcv[(mc, i)] = sA.v
                    wl3 = [(sA.sem, wcv[(mc, i)])]
                    if i == 0 and mc >= 2:
                        wl3.append((sV.sem, evd[mc - 2]))
                    for h in range(2):
                        for kk in range(16):
                            kc = i * 16 + kk
                            sch.add("tensor", "matmul", pa[h][:, :],
                                    wbO[p][:, kk * 128:(kk + 1) * 128],
                                    osb[:, kc * 1024 + h * 512:
                                        kc * 1024 + (h + 1) * 512],
                                    start=(kc == 0), stop=(kc == KC_D - 1),
                                    w=(wl3 if (h == 0 and kk == 0) else []),
                                    inc=([sT.inc()]
                                         if (h == 1 and kk == 15) else []))
                    mmh[(mc, i)] = sT.v
                mmd[mc] = sT.v
                wl = [] if mc < 2 else [(sV.sem, evd[mc - 2])]
                sch.add("sync", "dma_start", out=rst5[p][:, :],
                        in_=r1d[mc * 128:(mc + 1) * 128, :], w=wl,
                        inc=[LA[p].inc()])
                r1ld = LA[p].at()
                for h in range(2):
                    wl = []
                    if h == 0:
                        wl = [(sT.sem, mmd[mc]), r1ld]
                        if mc >= 2:
                            wl += [stod[mc - 2],
                                   (sV.sem, cvd2[mc - 2])]
                    sch.add("vector", "tensor_add",
                            r2c[p][:, h * 512:(h + 1) * 512], pa[h][:, :],
                            rst5[p][:, h * 512:(h + 1) * 512], w=wl,
                            inc=([sV.inc()] if h == 1 else []))
                evd[mc] = sV.v
                sch.add("sync", "dma_start",
                        out=r2d[mc * 128:(mc + 1) * 128, :],
                        in_=r2c[p][:, :], w=[(sV.sem, evd[mc])],
                        inc=[SA_[p].inc()])
                stod[mc] = SA_[p].at()
                fc5 = mc % KC_F
                if mc < KC_F:
                    sch.add("vector", "tensor_copy",
                            fold5d[:, fc5 * 1024:(fc5 + 1) * 1024],
                            r2c[p][:, :])
                else:
                    sch.add("vector", "tensor_add",
                            fold5d[:, fc5 * 1024:(fc5 + 1) * 1024],
                            fold5d[:, fc5 * 1024:(fc5 + 1) * 1024],
                            r2c[p][:, :])
                wl = [(sV.sem, evd[mc])]
                if mc >= 2:
                    wl += [(sT.sem, std2.get(mc - 2, 0))]
                sch.add("vector", "tensor_copy", xb5[p][:, :], r2c[p][:, :],
                        w=wl, inc=[sV.inc()])
                cvd2[mc] = sV.v
                wl = []
                if mc >= 2:
                    wl.append((sT.sem, sqd2.get(mc - 2, 0)))
                sch.add("vector", "tensor_mul", sq5[p][:, :], xb5[p][:, :],
                        xb5[p][:, :], w=wl, inc=[sV.inc()], lp=True)
                sqa2[mc] = sV.v
                for mcx in ([mc - 1] if mc >= 1 else []) + (
                        [mc] if mc == KC_D - 1 else []):
                    px = mcx % 2
                    for i in range(2):
                        sch.add("tensor", "matmul", sa_ps[i],
                                satb[:, mcx * 66:(mcx + 1) * 66],
                                xb5[px][:, i * 512:(i + 1) * 512],
                                start=(mcx == 0), stop=(mcx == KC_D - 1),
                                w=([(sV.sem, cvd2[mcx])] if i == 0 else []),
                                inc=([sT.inc()] if i == 1 else []))
                    std2[mcx] = sT.v
                    for i in range(2):
                        sch.add("tensor", "matmul", sq_ps[i], onesc[:, :],
                                sq5[px][:, i * 512:(i + 1) * 512],
                                start=(mcx == 0), stop=(mcx == KC_D - 1),
                                w=([(sV.sem, sqa2[mcx])] if i == 0 else []),
                                inc=([sT.inc()] if i == 1 else []))
                    sqd2[mcx] = sT.v
                fdd[mc] = sV.v
            bc = ln_post(sch, sa_ps, sq_ps, eps=1e-5, cmul=-4.0,
                         pb0=PS[0], pb1=PS[1], stats_done=(sT.sem, sT.v),
                         first_w=[(sV.sem, sV.v)], fold=fold5d,
                         tfa=rst5[0])
            ln_post_lora(sch, sa_ps, PS[0], PS[1], scale=0.5, bc_done=bc,
                         asb=asb4, tmp66=rst5[1][0:66, :])
            seal(sch)
            sch.emit(nc)

    if stop_stage == 2:
        debug_copy_out(r2d)
        return nc

    ffn_phase(r2d, outT, is_final=True, tag="f2",
              dump_after_p2=(stop_stage == 28))
    if stop_stage == 28:
        return nc

    sch = Sched()
    Sched.TAG[0] = "P12"
    phase_wait(sch)
    for s in ALL_STORE:
        if s.v > 0:
            sch.add("sync", "wait_ge", *s.at())
    sch.emit(nc)
    return nc


def _prep_weights(inputs):
    f = np.float32
    Wq = np.asarray(inputs["L0_attn_Wq"], f)
    Wk = np.asarray(inputs["L0_attn_Wk"], f)
    Wv = np.asarray(inputs["L0_attn_Wv"], f)
    Wo = np.asarray(inputs["L0_attn_Wo"], f)
    w1q = np.asarray(inputs["L0_ffn_w1q"], f)
    a1 = np.asarray(inputs["L0_ffn_a1"], f)
    b1 = np.asarray(inputs["L0_ffn_b1"], f)
    w3q = np.asarray(inputs["L0_ffn_w3q"], f)
    a3 = np.asarray(inputs["L0_ffn_a3"], f)
    b3 = np.asarray(inputs["L0_ffn_b3"], f)
    w2q = np.asarray(inputs["L0_ffn_w2q"], f)
    a2 = np.asarray(inputs["L0_ffn_a2"], f)
    b2 = np.asarray(inputs["L0_ffn_b2"], f)

    def tile4(wT, nm, nk):
        K, M = wT.shape
        assert K == nk * 128 and M == nm * 128
        return np.ascontiguousarray(
            wT.reshape(nk, 128, nm, 128).transpose(2, 1, 0, 3))

    wqr = tile4(np.ascontiguousarray(Wq.T), NQ, KC_D)
    wkr = tile4(np.ascontiguousarray(Wk.T), NKV, KC_D)
    wvr = tile4(np.ascontiguousarray(Wv.T), NKV, KC_D)
    wor = tile4(np.ascontiguousarray(Wo.T), KC_D, KC_D)

    def padh_cols(m):
        R = m.shape[0]
        z = np.zeros((R, 4, 2816), f)
        z[:, :, :2752] = m.reshape(R, 4, 2752)
        return z.reshape(R, HP)

    w1p = padh_cols(np.ascontiguousarray(w1q.T))
    w3p = padh_cols(np.ascontiguousarray(w3q.T))
    b1p = padh_cols(np.ascontiguousarray(b1.T))
    b3p = padh_cols(np.ascontiguousarray(b3.T))
    a2p = padh_cols(a2)
    w2p = np.zeros((2816, D), f)
    w2p[:2752] = w2q.T
    w1r = tile4(w1p, MC_H, KC_F)
    w3r = tile4(w3p, MC_H, KC_F)
    w2r = tile4(w2p, KC_D, KC_H)
    a2S = np.ascontiguousarray(
        a2p.reshape(2, MC_H, 128).transpose(2, 1, 0).reshape(128, 2 * MC_H))
    b2T = np.ascontiguousarray(b2.T)

    SaTa = np.zeros((D, 66), f)
    SaTa[:, 0] = 1.0
    SaTa[:, 1] = 1.0
    SaTa[:, 32] = a1[0]
    SaTa[:, 33] = a1[1]
    SaTa[:, 64] = a3[0]
    SaTa[:, 65] = a3[1]

    sel40 = np.zeros((40, 40 * 128), f)
    for h in range(40):
        sel40[h, h * 128:(h + 1) * 128] = 1.0
    iotacv = (np.arange(128, dtype=f)[:, None]
              + 128.0 * np.arange(16, dtype=f)[None, :])
    return dict(
        wqr=wqr, wkr=wkr, wvr=wvr, wor=wor, w1r=w1r, w3r=w3r, w2r=w2r,
        b1T=np.ascontiguousarray(b1p), b3T=np.ascontiguousarray(b3p),
        b2T=b2T, a2S=a2S, SaT=SaTa,
        qw=np.ascontiguousarray(
            np.asarray(inputs["L0_attn_q_norm_w"], f).reshape(128, 1)),
        kw=np.ascontiguousarray(
            np.asarray(inputs["L0_attn_k_norm_w"], f).reshape(128, 1)),
        sel40=sel40,
        iden=np.eye(128, dtype=f), iotac=np.ascontiguousarray(iotacv))


def _core_masks(parity):
    f = np.float32
    tri = np.where(np.arange(128)[:, None] <= np.arange(128)[None, :],
                   0.0, -1e30).astype(f)
    m = np.empty((128, 16 * 128), f)
    for kc in range(16):
        cc = kc % 2
        if cc == parity:
            t = tri
        elif parity == 0:
            t = np.full((128, 128), -1e30, f)
        else:
            t = np.zeros((128, 128), f)
        m[:, kc * 128:(kc + 1) * 128] = t
    return m


_BLOCKS = {p: [p + 2 * i for i in range(8)] for p in (0, 1)}


def kernel(_stop=3, **inputs):
    x = np.asarray(inputs["L0_attn_x"], np.float32)
    shared = _prep_weights(inputs)
    masks = {p: _core_masks(p) for p in (0, 1)}
    qpos = {}
    for p in (0, 1):
        qp = np.zeros((2, T), np.float32)
        qp[0] = (128 * np.repeat(np.array(_BLOCKS[p], np.float32), 128)
                 + np.tile(np.arange(128, dtype=np.float32), 8))
        qpos[p] = qp
    in_maps = []
    for core in range(8):
        b, p = core // 2, core % 2
        blocks = _BLOCKS[p]
        xs = x[b].reshape(16, 128, D)[blocks].reshape(T, D)
        m = dict(shared)
        m["xT"] = np.ascontiguousarray(xs.T)
        m["masks"] = masks[p]
        m["qpos"] = qpos[p]
        in_maps.append(m)
    nc = build_nc(stop_stage=_stop)
    res = run_bass_kernel_spmd(nc, in_maps, core_ids=list(range(8)))
    out = np.empty((4, 2048, D), np.float32)
    for core in range(8):
        b, p = core // 2, core % 2
        blocks = _BLOCKS[p]
        oT = res.results[core]["outT"]
        out[b].reshape(16, 128, D)[blocks] = (
            oT.T.reshape(8, 128, D))
    return out
